# version 11 (final)
# speedup vs baseline: 1.3022x; 1.0157x over previous
import numpy as np
import ml_dtypes
from contextlib import ExitStack

import concourse.bass as bass
from concourse.bass import IndirectOffsetOnAxis
import concourse.mybir as mybir
from concourse.bass_utils import run_bass_kernel_spmd

F32 = mybir.dt.float32
BF16 = mybir.dt.bfloat16
I32 = mybir.dt.int32
AF = mybir.ActivationFunctionType
ALU = mybir.AluOpType
AX = mybir.AxisListType

B, S, D, H, DH = 2, 8192, 1024, 16, 64
NT, TSZ = 16, 512
NCORES = 8
TILES = [[0, 7, 8, 15], [1, 6, 9, 14], [2, 5, 10, 13], [3, 4, 11, 12]]
TCORE = 4 * TSZ
NSLOT = 512
DBG_IDX = False
DBG_LOOPBACK = False
DBG_IDENT = False
NKC = S // 128
bf = ml_dtypes.bfloat16

ENGINES = ("sync", "scalar", "vector", "tensor", "gpsimd")


def core_tiles(core):
    return TILES[core % 4]


class Ref:
    __slots__ = ("eng", "cnt")

    def __init__(self, eng, cnt):
        self.eng = eng
        self.cnt = cnt


class Prog:

    SERIAL = False
    STEP_LIMIT = None

    def __init__(self, nc):
        self.nc = nc
        self.steps = {e: [] for e in ENGINES}
        self.cnt = {e: 0 for e in ENGINES}
        self.nsteps = 0
        self.prev = None

    def collective(self, fn, deps=(), sem=None):
        def wrapped(e):
            fn(e).then_inc(sem, 1)
            e.wait_ge(sem, 1)
            return e.nop()
        return self.step("gpsimd", wrapped, deps=deps)

    def step(self, eng, fn, deps=(), dma=False):
        self.nsteps += 1
        if Prog.STEP_LIMIT is not None and self.nsteps > Prog.STEP_LIMIT:
            return None
        inc = 16 if dma else 1
        waits = []
        for d in deps:
            if d is None:
                continue
            if isinstance(d, (list, tuple)):
                for dd in d:
                    if dd is not None and not (eng == "tensor" and
                                               dd.eng == "tensor"):
                        waits.append((dd.eng, dd.cnt))
            elif not (eng == "tensor" and d.eng == "tensor"):
                waits.append((d.eng, d.cnt))
        if Prog.SERIAL and self.prev is not None and self.prev.eng != eng:
            waits.append((self.prev.eng, self.prev.cnt))
        self.steps[eng].append((waits, fn, inc))
        self.cnt[eng] += inc
        self.prev = Ref(eng, self.cnt[eng])
        return self.prev

    def finalize(self, block, sems):
        Prog.LAST_NSTEPS = self.nsteps

        def emit_for(eng):
            def f(e):
                hw = {o: 0 for o in ENGINES}
                for waits, fn, inc in self.steps[eng]:
                    for (oeng, cnt) in waits:
                        if hw[oeng] < cnt:
                            e.wait_ge(sems[oeng], cnt)
                            hw[oeng] = cnt
                    fn(e).then_inc(sems[eng], inc)
                if eng == "gpsimd":
                    for o in ENGINES:
                        if o != eng and self.cnt[o] > 0:
                            e.wait_ge(sems[o], self.cnt[o])
            return f
        block.sync(emit_for("sync"))
        block.scalar(emit_for("scalar"))
        block.vector(emit_for("vector"))
        block.tensor(emit_for("tensor"))
        block.gpsimd(emit_for("gpsimd"))


def kernel_all_vec(P):
    return Ref("vector", P.cnt["vector"])


def now_all(P):
    return [Ref(e, P.cnt[e]) for e in ENGINES if P.cnt[e] > 0]


_INTERMEDIATE = {"x1my", "ymy", "x2my", "x3my"}


def _begin(fused):
    if fused is None:
        nc = bass.Bass("TRN2", target_bir_lowering=False)
        return nc, Prog(nc), {}, True
    nc, P, cache = fused
    return nc, P, cache, False


def dram(ctx, name, shape, dtype, kind):
    nc, cache, isfused = ctx
    if name in cache:
        return cache[name]
    if isfused and name in _INTERMEDIATE:
        kind = "Internal"
    ap = nc.dram_tensor(name, shape, dtype, kind=kind).ap()
    cache[name] = ap
    return ap


_UN = [0]


def uname(n):
    _UN[0] += 1
    return f"{n}_u{_UN[0]}"


def fm3(ap):
    return ap.rearrange("(c p) t -> p c t", p=128)


def emit_attention(nc, P, st, *, causal, x_src, xmy_src, wq_sb, wk_sb, wv_sb,
                   w_deps, mask_src, attn_fm, scale_wT_sb=None, r_scw=None,
                   kv=None):
    e_ = st.enter_context
    kh = e_(nc.sbuf_tensor(uname("kh"), [128, 2, S], BF16))
    vh = e_(nc.sbuf_tensor(uname("vh"), [128, NKC, 256], BF16))
    qp = e_(nc.sbuf_tensor(uname("qp"), [128, TCORE], BF16))
    xt = [e_(nc.sbuf_tensor(uname(f"xt{i}"), [128, 8, TSZ], BF16)) for i in range(2)]
    msk = e_(nc.sbuf_tensor(uname("msk"), [128, 16, TSZ], BF16)) if causal else None
    pp = [e_(nc.sbuf_tensor(uname(f"pp{i}"), [128, 2 * TSZ], BF16))
          for i in range(2)]
    qhat = e_(nc.sbuf_tensor(uname("qhat"), [128, TSZ], BF16))
    sqb = e_(nc.sbuf_tensor(uname("sqb"), [64, TSZ], BF16))
    kss = e_(nc.sbuf_tensor(uname("kss"), [1, 1], F32))
    kssc = e_(nc.sbuf_tensor(uname("kssc"), [1, 1], F32))
    k2m = e_(nc.sbuf_tensor(uname("k2m"), [1, NT], F32))
    q2r = e_(nc.sbuf_tensor(uname("q2r"), [1, TCORE], F32))
    shf = e_(nc.sbuf_tensor(uname("shf"), [1, TCORE], F32))
    crow = e_(nc.sbuf_tensor(uname("crow"), [1, TCORE], F32)) \
        if not causal else None
    rrow = e_(nc.sbuf_tensor(uname("rrow"), [1, TSZ], F32))
    rb_sb = e_(nc.sbuf_tensor(uname("rb_sb"), [64, TSZ], F32))
    ones64b = e_(nc.sbuf_tensor(uname("ones64b"), [64, 1], BF16))
    onesf = e_(nc.sbuf_tensor(uname("onesf"), [1, 128], F32))
    cb_sb = e_(nc.sbuf_tensor(uname("cb_sb"), [128, TSZ], BF16)) if not causal else None

    ps_s = [e_(nc.psum_tensor(uname(f"ps_s{i}"), [128, 2 * TSZ], F32))
            for i in range(3)]
    ps_p0 = ps_s[2][:, 0:TSZ]
    ps_p = [ps_p0, ps_p0]
    ps_v0 = ps_s[2][:, TSZ:2 * TSZ]
    ps_v = [ps_v0, ps_v0]
    ps_av = e_(nc.psum_tensor(uname("ps_av"), [128, TSZ], F32))
    ps_rb = e_(nc.psum_tensor(uname("ps_rb"), [128, TSZ], F32))

    r_ones = P.step("vector", lambda e: e.memset(ones64b[:, :], 1.0))
    r_qh0 = P.step("vector", lambda e: e.memset(qhat[64:128, :], 0.0))
    r_onesf = P.step("vector", lambda e: e.memset(onesf[:, :], 1.0))
    r_aug0 = P.step("vector", lambda e: e.memset(kh[64:128, :, :], 0.0))
    r_aug = P.step("vector", lambda e: e.memset(kh[64:65, :, :], -1.0),
                   deps=[r_aug0])
    r_v0 = P.step("vector", lambda e: e.memset(vh[:, :, :], 0.0))
    r_vone = P.step("vector", lambda e: e.memset(vh[:, :, 64:65], 1.0),
                    deps=[r_v0])
    r_vone2 = P.step("vector", lambda e: e.memset(vh[:, :, 192:193], 1.0),
                     deps=[r_v0])

    x3 = fm3(x_src) if kv is None else None
    xmy3 = fm3(xmy_src)
    vg3 = kv["vg"].rearrange("(c p) f -> p c f", p=128) if kv else None

    r_mask = None
    r_avout = None
    r_ppuse = [None, None]
    r_exuse = [None, None, None]
    r_qhat_use = None
    r_rb_write = None
    r_rb_read = None
    r_sq_use = None
    r_row_evac = None
    r_q2r_read = None
    r_shf_read = []
    r_crow_read = []
    r_xuse = [None, None]
    r_p_evac = [None, None]
    r_v_evac = [None, None]

    for pair in range(8):
        r_kh = {}
        r_vh_done = {}
        if kv is not None:
            kvdeps = [kv["dep"], r_ppuse[0], r_ppuse[1], r_avout]
            r_kl = []
            for hh in range(2):
                for rb in range(4):
                    r0 = 1024 * rb + 128 * pair + 64 * hh
                    r_kl.append(P.step("sync",
                        lambda e, hh=hh, rb=rb, r0=r0: e.dma_start(
                            out=kh[0:64, hh, rb * TCORE:(rb + 1) * TCORE],
                            in_=kv["kg"][r0:r0 + 64, :]),
                        deps=kvdeps, dma=True))
                r_vl = P.step("gpsimd", lambda e, hh=hh, pair=pair:
                    e.dma_start(
                        out=vh[:, :, 128 * hh:128 * hh + 64],
                        in_=vg3[:, :, 128 * pair + 64 * hh:
                                128 * pair + 64 * hh + 64]),
                    deps=kvdeps + [r_vone, r_vone2], dma=True)
                r_kl.append(r_vl)
            for kt in range(NT):
                for hh in range(2):
                    r_kh[(kt, hh)] = r_kl
            for ch in range(NKC):
                r_vh_done[ch] = r_kl
            r_xuse = [r_kl, r_kl]
        else:
         for kt in range(NT):
            buf = xt[kt % 2]
            kb = 0
            r_x = P.step("gpsimd", lambda e, kt=kt, buf=buf: e.dma_start(
                out=buf[:, :, :], in_=x3[:, :, kt * TSZ:(kt + 1) * TSZ]),
                deps=[r_xuse[kt % 2]] + w_deps, dma=True)
            r_mm = None
            for dc in range(8):
                r_mm = P.step("tensor",
                    lambda e, dc=dc, buf=buf, pair=pair, kb=kb: e.matmul(
                        ps_p[kb][:, :],
                        wk_sb[:, dc, pair * 128:(pair + 1) * 128],
                        buf[:, dc, :], start=(dc == 0), stop=(dc == 7)),
                    deps=[r_x, r_p_evac[kb], r_exuse[2]]
                    if dc == 0 else [r_x])
            for hh in range(2):
                r_kh[(kt, hh)] = P.step(
                    "vector", lambda e, hh=hh, kt=kt, kb=kb: e.tensor_copy(
                        kh[0:64, hh, kt * TSZ:(kt + 1) * TSZ],
                        ps_p[kb][64 * hh:64 * hh + 64, :]),
                    deps=[r_mm, r_aug])
            r_p_evac[kb] = r_kh[(kt, 1)]
            for th in range(4):
                ch = kt * 4 + th
                vb = 0
                r_vm = None
                for dc in range(8):
                    r_vm = P.step(
                        "tensor",
                        lambda e, dc=dc, th=th, buf=buf, pair=pair, vb=vb:
                        e.matmul(
                            ps_v[vb][:, 0:128],
                            buf[:, dc, th * 128:(th + 1) * 128],
                            wv_sb[:, dc, pair * 128:(pair + 1) * 128],
                            start=(dc == 0), stop=(dc == 7)),
                        deps=[r_x, r_v_evac[vb], r_exuse[2]]
                        if dc == 0 else [r_x])
                for hh in range(2):
                    r_v_evac[vb] = P.step(
                        "vector", lambda e, hh=hh, ch=ch, vb=vb:
                        e.tensor_copy(
                            vh[:, ch, 128 * hh:128 * hh + 64],
                            ps_v[vb][:, 64 * hh:64 * hh + 64]),
                        deps=[r_vm, r_vone, r_vone2])
                r_vh_done[ch] = r_v_evac[vb]
            r_xuse[kt % 2] = [r_v_evac[0], r_v_evac[1], r_kh[(kt, 1)]]
        r_qp = {}
        for qt in range(4):
            buf = xt[qt % 2]
            kb = 0
            r_x = P.step("gpsimd", lambda e, qt=qt, buf=buf: e.dma_start(
                out=buf[:, :, :], in_=xmy3[:, :, qt * TSZ:(qt + 1) * TSZ]),
                deps=[r_xuse[qt % 2]], dma=True)
            r_mm = None
            for dc in range(8):
                r_mm = P.step("tensor",
                    lambda e, dc=dc, buf=buf, pair=pair, kb=kb: e.matmul(
                        ps_p[kb][:, :],
                        wq_sb[:, dc, pair * 128:(pair + 1) * 128],
                        buf[:, dc, :], start=(dc == 0), stop=(dc == 7)),
                    deps=[r_x, r_p_evac[kb], r_exuse[2]]
                    if dc == 0 else [r_x])
            r_p_evac[kb] = P.step("vector", lambda e, qt=qt, kb=kb:
                e.tensor_copy(
                    qp[:, qt * TSZ:(qt + 1) * TSZ], ps_p[kb][:, :]),
                deps=[r_mm])
            r_qp[qt] = r_p_evac[kb]
            r_xuse[qt % 2] = r_p_evac[kb]

        r_exuse[2] = [r_exuse[2], r_p_evac[0]]

        for hh in range(2):
            r_t = []
            for kt in range(NT):
                r_s = P.step("vector", lambda e, hh=hh, kt=kt: e.tensor_mul(
                    sqb[:, :], kh[0:64, hh, kt * TSZ:(kt + 1) * TSZ],
                    kh[0:64, hh, kt * TSZ:(kt + 1) * TSZ]),
                    deps=[r_kh[(kt, hh)], r_sq_use, r_ones])
                r_m = P.step("tensor", lambda e: e.matmul(
                    ps_rb[0:1, :], ones64b[:, :], sqb[:, :],
                    start=True, stop=True),
                    deps=[r_s, r_row_evac, r_rb_read])
                r_sq_use = r_m
                r_row_evac = P.step(
                    "vector", lambda e, kt=kt: e.reduce_max(
                        k2m[0:1, kt:kt + 1], ps_rb[0:1, :], axis=AX.X),
                    deps=[r_m])
                r_t.append(r_row_evac)
            r_kss = P.step("vector", lambda e: e.reduce_max(
                kss[0:1, 0:1], k2m[0:1, :], axis=AX.X), deps=r_t)
            cmul = (1.02 / 64.0) if causal else 1.02
            r_kssc = P.step("vector", lambda e, cmul=cmul: e.tensor_scalar(
                kssc[0:1, 0:1], kss[0:1, 0:1], cmul, None,
                ALU.mult), deps=[r_kss])
            for qt in range(4):
                r_s = P.step("vector", lambda e, hh=hh, qt=qt: e.tensor_mul(
                    sqb[:, :], qp[64 * hh:64 * hh + 64, qt * TSZ:(qt + 1) * TSZ],
                    qp[64 * hh:64 * hh + 64, qt * TSZ:(qt + 1) * TSZ]),
                    deps=[r_qp[qt], r_sq_use])
                r_m = P.step("tensor", lambda e: e.matmul(
                    ps_rb[0:1, :], ones64b[:, :], sqb[:, :],
                    start=True, stop=True),
                    deps=[r_s, r_row_evac, r_rb_read])
                r_sq_use = r_m
                r_row_evac = P.step(
                    "vector", lambda e, qt=qt: e.tensor_copy(
                        q2r[0:1, qt * TSZ:(qt + 1) * TSZ], ps_rb[0:1, :]),
                    deps=[r_m, r_q2r_read])
            r_l = P.step("scalar", lambda e: e.activation(
                shf[0:1, :], q2r[0:1, :], AF.Ln,
                scale=kssc[0:1, 0:1]), deps=[r_row_evac, r_kssc] + r_shf_read)
            r_q2r_read = r_l
            r_sh = P.step("scalar", lambda e: e.activation(
                shf[0:1, :], shf[0:1, :], AF.Exp, scale=0.5), deps=[r_l])
            if not causal:
                r_cs = []
                for qt in range(4):
                    r_m = P.step("tensor", lambda e, hh=hh, qt=qt, pair=pair: e.matmul(
                        ps_rb[0:1, :],
                        scale_wT_sb[64 * hh:64 * hh + 64,
                                    2 * pair + hh:2 * pair + hh + 1],
                        qp[64 * hh:64 * hh + 64, qt * TSZ:(qt + 1) * TSZ],
                        start=True, stop=True),
                        deps=[r_qp[qt], r_scw, r_row_evac, r_sh,
                              r_rb_read])
                    r_c = P.step("scalar", lambda e, qt=qt: e.activation(
                        crow[0:1, qt * TSZ:(qt + 1) * TSZ], ps_rb[0:1, :],
                        AF.Exp, scale=-1.0, bias=0.0),
                        deps=[r_m] + r_crow_read)
                    r_crow_read = []
                    r_row_evac = r_c
                    r_cs.append(r_c)
                r_a1 = P.step("vector", lambda e: e.tensor_scalar(
                    crow[0:1, :], crow[0:1, :], 1.0, None, ALU.add),
                    deps=r_cs)
                r_rc = P.step("vector", lambda e: e.reciprocal(
                    crow[0:1, :], crow[0:1, :]), deps=[r_a1])
                r_c2 = P.step("vector", lambda e: e.tensor_scalar(
                    crow[0:1, :], crow[0:1, :], 0.25, None, ALU.mult),
                    deps=[r_rc])
                r_shift = P.step("vector", lambda e: e.tensor_mul(
                    shf[0:1, :], shf[0:1, :], crow[0:1, :]),
                    deps=[r_sh, r_c2])
            else:
                r_shift = r_sh
            r_shf_read = []

            for slot in range(4):
                if causal:
                    r_mask = P.step("sync", lambda e, slot=slot: e.dma_start(
                        out=msk[:, :, :],
                        in_=mask_src[slot, :, :, :].rearrange("k m q -> m k q")),
                        deps=[r_ppuse[0], r_ppuse[1]], dma=True)
                nch = 16 * slot + 16 if causal else NKC
                if causal:
                    r_q1 = P.step("vector", lambda e, hh=hh, slot=slot:
                        e.tensor_scalar(
                            qhat[0:64, :],
                            qp[64 * hh:64 * hh + 64,
                               slot * TSZ:(slot + 1) * TSZ],
                            0.125, None, ALU.mult),
                        deps=[r_qhat_use, r_qp[slot]])
                else:
                    r_rb_write = P.step("tensor", lambda e, slot=slot:
                        e.matmul(ps_rb[:, :], onesf[:, :],
                                 crow[0:1, slot * TSZ:(slot + 1) * TSZ],
                                 start=True, stop=True),
                        deps=[r_shift, r_onesf, r_rb_read])
                    r_crow_read.append(r_rb_write)
                    r_rb_read = P.step("vector", lambda e: e.tensor_copy(
                        cb_sb[:, :], ps_rb[:, :]), deps=[r_rb_write])
                    r_q1 = P.step("vector", lambda e, hh=hh, slot=slot:
                        e.tensor_mul(
                            qhat[0:64, :],
                            qp[64 * hh:64 * hh + 64,
                               slot * TSZ:(slot + 1) * TSZ],
                            cb_sb[64 * hh:64 * hh + 64, :]),
                        deps=[r_qhat_use, r_qp[slot], r_rb_read])
                r_q2 = P.step("vector", lambda e, slot=slot:
                    e.tensor_copy(
                        qhat[64:65, :],
                        shf[0:1, slot * TSZ:(slot + 1) * TSZ]),
                    deps=[r_shift, r_qhat_use, r_qh0])
                r_shf_read.append(r_q2)
                r_qh = [r_q1, r_q2]
                r_av = None
                npr = nch // 2

                def emit_scores(chp, hh=hh):
                    pb = chp % 3
                    r_scs = []
                    for i2 in range(2):
                        ch = 2 * chp + i2
                        r_scs.append(P.step("tensor",
                            lambda e, ch=ch, hh=hh, pb=pb, i2=i2:
                            e.matmul(
                                ps_s[pb][:, i2 * TSZ:(i2 + 1) * TSZ],
                                kh[0:128, hh, ch * 128:(ch + 1) * 128],
                                qhat[0:128, :],
                                start=True, stop=True),
                            deps=r_qh + [r_exuse[pb]]))
                    return r_scs

                r_s_q = [emit_scores(0)]
                if npr > 1:
                    r_s_q.append(emit_scores(1))
                for chp in range(npr):
                    pb = chp % 2
                    pb3 = chp % 3
                    r_scs = r_s_q.pop(0)
                    r_ex = P.step("scalar", lambda e, pb=pb, pb3=pb3:
                        e.activation(pp[pb][:, :], ps_s[pb3][:, :], AF.Exp),
                        deps=r_scs + [r_ppuse[pb]])
                    r_exuse[pb3] = r_ex
                    if chp + 2 < npr:
                        r_s_q.append(emit_scores(chp + 2))
                    r_pf = r_ex
                    for i2 in range(2):
                        ch = 2 * chp + i2
                        if causal and ch >= 16 * slot:
                            r_pf = P.step("vector",
                                lambda e, ch=ch, slot=slot, pb=pb, i2=i2:
                                e.tensor_mul(
                                    pp[pb][:, i2 * TSZ:(i2 + 1) * TSZ],
                                    pp[pb][:, i2 * TSZ:(i2 + 1) * TSZ],
                                    msk[:, ch - 16 * slot, :]),
                                deps=[r_ex, r_mask])
                    for i2 in range(2):
                        ch = 2 * chp + i2
                        r_av = P.step("tensor",
                            lambda e, ch=ch, hh=hh, pb=pb, i2=i2, nch=nch:
                            e.matmul(
                                ps_av[:, :],
                                vh[:, ch, 128 * hh:128 * hh + 128],
                                pp[pb][:, i2 * TSZ:(i2 + 1) * TSZ],
                                start=(ch == 0), stop=(ch == nch - 1)),
                            deps=[r_pf, r_vh_done[ch]] +
                                 ([r_avout] if ch == 0 else []))
                    r_ppuse[pb] = r_av
                r_qhat_use = r_av
                r_l2 = P.step("scalar", lambda e: e.activation(
                    rrow[0:1, :], ps_av[64:65, :], AF.Ln), deps=[r_av])
                r_e2 = P.step("scalar", lambda e: e.activation(
                    rrow[0:1, :], rrow[0:1, :], AF.Exp, scale=-1.0),
                    deps=[r_l2])
                r_rb_write = P.step("tensor", lambda e: e.matmul(
                    ps_rb[:, :], onesf[:, :], rrow[0:1, :],
                    start=True, stop=True), deps=[r_e2, r_onesf, r_rb_read])
                r_rb_read = P.step("vector", lambda e: e.tensor_copy(
                    rb_sb[:, :], ps_rb[0:64, :]), deps=[r_rb_write])
                r_avout = P.step("vector", lambda e, hh=hh, slot=slot,
                                 pair=pair: e.scalar_tensor_tensor(
                    attn_fm[64 * hh:64 * hh + 64, pair,
                            slot * TSZ:(slot + 1) * TSZ],
                    ps_av[0:64, :], 1.0, rb_sb[:, :], ALU.mult, ALU.mult),
                    deps=[r_rb_read])
    return r_avout
    return


def emit_proj_residual_ln(nc, P, st, *, src_fm, w_sb, r_w, resid_src,
                          resid_scale, out_hbm, gamma_beta=None, bias_sb=None,
                          r_gb=None, r_src=None, name=""):
    e_ = st.enter_context
    resid = e_(nc.sbuf_tensor(uname(name + "res"), [128, 8, TSZ], F32))
    resb = e_(nc.sbuf_tensor(uname(name + "resb"), [128, 8, TSZ], BF16))
    sqb = e_(nc.sbuf_tensor(uname(name + "sqb"), [128, 8, TSZ], BF16))
    xr = e_(nc.sbuf_tensor(uname(name + "xr"), [128, TSZ], F32))
    yout = e_(nc.sbuf_tensor(uname(name + "yout"), [128, 8, TSZ], F32))
    m_sb = e_(nc.sbuf_tensor(uname(name + "m"), [1, TSZ], F32))
    v_sb = e_(nc.sbuf_tensor(uname(name + "v"), [1, TSZ], F32))
    ones128 = e_(nc.sbuf_tensor(uname(name + "o128"), [128, 1], BF16))
    onesf1 = e_(nc.sbuf_tensor(uname(name + "of1"), [1, 128], F32))
    epsr = e_(nc.sbuf_tensor(uname(name + "eps"), [1, 1], F32))
    ps_o = [e_(nc.psum_tensor(uname(name + "ps_o"), [128, TSZ], F32))
            for _ in range(2)]
    ps_sum = e_(nc.psum_tensor(uname(name + "ps_sum"), [1, TSZ], F32))
    ps_sq = e_(nc.psum_tensor(uname(name + "ps_sq"), [1, TSZ], F32))
    ps_mb = e_(nc.psum_tensor(uname(name + "ps_mb"), [128, TSZ], F32))
    ps_rs = e_(nc.psum_tensor(uname(name + "ps_rs"), [128, TSZ], F32))

    r_o1 = P.step("vector", lambda e: e.memset(ones128[:, :], 1.0))
    r_o2 = P.step("vector", lambda e: e.memset(onesf1[:, :], 1.0))
    r_eps = P.step("vector", lambda e: e.memset(epsr[:, :], 1e-5))
    rs3 = fm3(resid_src)
    out3 = fm3(out_hbm)

    r_store = None
    r_residuse = None
    r_statuse = None
    for ot in range(4):
        r_xl = P.step("sync", lambda e, ot=ot: e.dma_start(
            out=resid[:, :, :], in_=rs3[:, :, ot * TSZ:(ot + 1) * TSZ]),
            deps=[r_residuse, r_store, r_src], dma=True)
        r_oc = {}
        r_mm = None
        for oc in range(8):
            pb = oc % 2
            for ic in range(8):
                r_mm = P.step("tensor", lambda e, oc=oc, ic=ic, ot=ot, pb=pb:
                    e.matmul(ps_o[pb][:, :],
                             w_sb[:, ic, oc * 128:(oc + 1) * 128],
                             src_fm[:, ic, ot * TSZ:(ot + 1) * TSZ],
                             start=(ic == 0), stop=(ic == 7)),
                    deps=[r_w, r_src, r_oc.get(oc - 2)] if ic == 0
                    else [r_w])
            if bias_sb is not None:
                r_bi = P.step("vector", lambda e, oc=oc, pb=pb:
                    e.tensor_scalar(
                        ps_o[pb][:, :], ps_o[pb][:, :],
                        bias_sb[:, oc:oc + 1], None,
                        ALU.add), deps=[r_mm, r_gb])
                r_mm = r_bi
            r_r = P.step("vector", lambda e, oc=oc, pb=pb:
                e.scalar_tensor_tensor(
                    resid[:, oc, :], resid[:, oc, :], resid_scale,
                    ps_o[pb][:, :],
                    ALU.mult, ALU.add), deps=[r_mm, r_xl])
            r_b = P.step("vector", lambda e, oc=oc: e.tensor_copy(
                resb[:, oc, :], resid[:, oc, :]), deps=[r_r])
            r_q = P.step("vector", lambda e, oc=oc: e.tensor_mul(
                sqb[:, oc, :], resb[:, oc, :], resb[:, oc, :]), deps=[r_b])
            r_oc[oc] = r_q
        r_s = None
        for oc in range(8):
            r_s = P.step("tensor", lambda e, oc=oc: e.matmul(
                ps_sum[:, :], ones128[:, :], resb[:, oc, :],
                start=(oc == 0), stop=(oc == 7)),
                deps=[r_oc[oc], r_o1] + ([r_statuse] if oc == 0 else []))
        r_s2 = None
        for oc in range(8):
            r_s2 = P.step("tensor", lambda e, oc=oc: e.matmul(
                ps_sq[:, :], ones128[:, :], sqb[:, oc, :],
                start=(oc == 0), stop=(oc == 7)),
                deps=[r_oc[oc], r_o1] + ([r_statuse] if oc == 0 else []))
        r_m = P.step("scalar", lambda e: e.activation(
            m_sb[:, :], ps_sum[0:1, :], AF.Copy, scale=1.0 / D), deps=[r_s])
        r_m2 = P.step("vector", lambda e: e.tensor_mul(
            xr[0:1, :], m_sb[:, :], m_sb[:, :]), deps=[r_m])
        r_v = P.step("vector", lambda e: e.scalar_tensor_tensor(
            v_sb[:, :], ps_sq[0:1, :], 1.0 / D, xr[0:1, :],
            ALU.mult, ALU.subtract), deps=[r_s2, r_m2])
        r_ln = P.step("scalar", lambda e: e.activation(
            v_sb[:, :], v_sb[:, :], AF.Ln, bias=epsr[0:1, 0:1]),
            deps=[r_v, r_eps])
        r_rs = P.step("scalar", lambda e: e.activation(
            v_sb[:, :], v_sb[:, :], AF.Exp, scale=-0.5), deps=[r_ln])
        r_bm = P.step("tensor", lambda e: e.matmul(
            ps_mb[:, :], onesf1[:, :], m_sb[:, :], start=True, stop=True),
            deps=[r_m, r_o2, r_statuse])
        r_br = P.step("tensor", lambda e: e.matmul(
            ps_rs[:, :], onesf1[:, :], v_sb[:, :], start=True, stop=True),
            deps=[r_rs, r_o2, r_statuse])
        r_yq = {}
        for oc in range(8):
            r_y1 = P.step("vector", lambda e, oc=oc: e.tensor_sub(
                yout[:, oc, :], resid[:, oc, :], ps_mb[:, :]),
                deps=[r_bm, r_oc[oc]])
            r_y2 = P.step("vector", lambda e, oc=oc: e.tensor_mul(
                yout[:, oc, :], yout[:, oc, :], ps_rs[:, :]), deps=[r_y1, r_br])
            if gamma_beta is not None:
                g_sb, b_sb = gamma_beta
                r_y2 = P.step("vector", lambda e, oc=oc: e.tensor_scalar(
                    yout[:, oc, :], yout[:, oc, :], g_sb[:, oc:oc + 1],
                    b_sb[:, oc:oc + 1], ALU.mult, ALU.add),
                    deps=[r_y2, r_gb])
            r_yq[oc] = r_y2
        r_statuse = r_yq[7]
        r_residuse = r_yq[7]
        r_store = P.step("sync", lambda e, ot=ot: e.dma_start(
            out=out3[:, :, ot * TSZ:(ot + 1) * TSZ], in_=yout[:, :, :]),
            deps=list(r_yq.values()), dma=True)
    return r_store


def make_ln_ctx(nc, e_, name):
    c = {}
    c["resb"] = e_(nc.sbuf_tensor(uname(name + "resb"), [128, 8, TSZ], BF16))
    c["sqb"] = e_(nc.sbuf_tensor(uname(name + "sqb"), [128, 8, TSZ], BF16))
    c["xr"] = e_(nc.sbuf_tensor(uname(name + "xr"), [1, TSZ], F32))
    c["m"] = e_(nc.sbuf_tensor(uname(name + "m"), [1, TSZ], F32))
    c["v"] = e_(nc.sbuf_tensor(uname(name + "v"), [1, TSZ], F32))
    c["o128"] = e_(nc.sbuf_tensor(uname(name + "o128"), [128, 1], BF16))
    c["of1"] = e_(nc.sbuf_tensor(uname(name + "of1"), [1, 128], F32))
    c["eps"] = e_(nc.sbuf_tensor(uname(name + "eps"), [1, 1], F32))
    c["ps_sum"] = e_(nc.psum_tensor(uname(name + "ps_sum"), [1, TSZ], F32))
    c["ps_sq"] = e_(nc.psum_tensor(uname(name + "ps_sq"), [1, TSZ], F32))
    c["ps_mb"] = e_(nc.psum_tensor(uname(name + "ps_mb"), [128, TSZ], F32))
    c["ps_rs"] = e_(nc.psum_tensor(uname(name + "ps_rs"), [128, TSZ], F32))
    c["init"] = [
        None
    ]
    r1 = None
    return c


def ln_ctx_init(nc, P, c):
    r1 = P.step("vector", lambda e: e.memset(c["o128"][:, :], 1.0))
    r2 = P.step("vector", lambda e: e.memset(c["of1"][:, :], 1.0))
    r3 = P.step("vector", lambda e: e.memset(c["eps"][:, :], 1e-5))
    c["rinit"] = [r1, r2, r3]
    c["rlast"] = None


def emit_ln_tile(nc, P, c, src, dst, deps, gamma_beta=None, r_gb=None):
    r_oc = {}
    for oc in range(8):
        r_b = P.step("vector", lambda e, oc=oc: e.tensor_copy(
            c["resb"][:, oc, :], src[:, oc, :]),
            deps=deps + c["rinit"] + [c["rlast"]])
        r_q = P.step("vector", lambda e, oc=oc: e.tensor_mul(
            c["sqb"][:, oc, :], c["resb"][:, oc, :], c["resb"][:, oc, :]),
            deps=[r_b])
        r_oc[oc] = r_q
    r_s = None
    for oc in range(8):
        r_s = P.step("tensor", lambda e, oc=oc: e.matmul(
            c["ps_sum"][:, :], c["o128"][:, :], c["resb"][:, oc, :],
            start=(oc == 0), stop=(oc == 7)),
            deps=[r_oc[oc]] + ([c["rlast"]] if oc == 0 else []))
    r_s2 = None
    for oc in range(8):
        r_s2 = P.step("tensor", lambda e, oc=oc: e.matmul(
            c["ps_sq"][:, :], c["o128"][:, :], c["sqb"][:, oc, :],
            start=(oc == 0), stop=(oc == 7)),
            deps=[r_oc[oc]] + ([c["rlast"]] if oc == 0 else []))
    r_m = P.step("scalar", lambda e: e.activation(
        c["m"][:, :], c["ps_sum"][0:1, :], AF.Copy, scale=1.0 / D),
        deps=[r_s, c["rlast"]])
    r_m2 = P.step("vector", lambda e: e.tensor_mul(
        c["xr"][:, :], c["m"][:, :], c["m"][:, :]), deps=[r_m])
    r_v = P.step("vector", lambda e: e.scalar_tensor_tensor(
        c["v"][:, :], c["ps_sq"][0:1, :], 1.0 / D, c["xr"][:, :],
        ALU.mult, ALU.subtract), deps=[r_s2, r_m2])
    r_ln = P.step("scalar", lambda e: e.activation(
        c["v"][:, :], c["v"][:, :], AF.Ln, bias=c["eps"][0:1, 0:1]),
        deps=[r_v])
    r_rs = P.step("scalar", lambda e: e.activation(
        c["v"][:, :], c["v"][:, :], AF.Exp, scale=-0.5), deps=[r_ln])
    r_bm = P.step("tensor", lambda e: e.matmul(
        c["ps_mb"][:, :], c["of1"][:, :], c["m"][:, :], start=True,
        stop=True), deps=[r_m, c["rlast"]])
    r_br = P.step("tensor", lambda e: e.matmul(
        c["ps_rs"][:, :], c["of1"][:, :], c["v"][:, :], start=True,
        stop=True), deps=[r_rs, c["rlast"]])
    r_yq = []
    for oc in range(8):
        r_y1 = P.step("vector", lambda e, oc=oc: e.tensor_sub(
            dst[:, oc, :], src[:, oc, :], c["ps_mb"][:, :]),
            deps=[r_bm, r_oc[oc]])
        r_y2 = P.step("vector", lambda e, oc=oc: e.tensor_mul(
            dst[:, oc, :], dst[:, oc, :], c["ps_rs"][:, :]),
            deps=[r_y1, r_br])
        if gamma_beta is not None:
            g_sb, b_sb = gamma_beta
            r_y2 = P.step("vector", lambda e, oc=oc: e.tensor_scalar(
                dst[:, oc, :], dst[:, oc, :], g_sb[:, oc:oc + 1],
                b_sb[:, oc:oc + 1], ALU.mult, ALU.add), deps=[r_y2, r_gb])
        r_yq.append(r_y2)
    c["rlast"] = r_yq[-1]
    return r_yq


def build_stageBmoe(fused=None):
    nc, P, _cache, _own = _begin(fused)
    _ctx = (nc, _cache, fused is not None)
    ymy = dram(_ctx, "ymy", [D, TCORE], F32, "ExternalInput")
    x1my = dram(_ctx, "x1my", [D, TCORE], F32, "ExternalInput")
    mgw = dram(_ctx, "mgw", [D, 8], F32, "ExternalInput")
    mgb = dram(_ctx, "mgb", [8, 1], F32, "ExternalInput")
    w1 = dram(_ctx, "w1", [8, D, 2048], F32, "ExternalInput")
    b1x = dram(_ctx, "b1x", [8, 128, 16], F32, "ExternalInput")
    lng = dram(_ctx, "lng", [8, 128, 16], F32, "ExternalInput")
    lnb = dram(_ctx, "lnb", [8, 128, 16], F32, "ExternalInput")
    w2 = dram(_ctx, "w2", [8, 2048, D], F32, "ExternalInput")
    b2x = dram(_ctx, "b2x", [8, 128, 8], F32, "ExternalInput")
    rsc = dram(_ctx, "rsc", [1, 8], F32, "ExternalInput")
    n2g = dram(_ctx, "n2g", [128, 8], F32, "ExternalInput")
    n2b = dram(_ctx, "n2b", [128, 8], F32, "ExternalInput")
    iota8 = dram(_ctx, "iota8", [128, 8], F32, "ExternalInput")
    ident = dram(_ctx, "ident", [128, 128], F32, "ExternalInput")
    x2my = dram(_ctx, "x2my", [D, TCORE], F32, "ExternalOutput")

    y3 = fm3(ymy)
    x13 = fm3(x1my)
    out3 = fm3(x2my)

    with ExitStack() as st:
        e_ = st.enter_context
        mgw_sb = e_(nc.sbuf_tensor(uname("mgw_sb"), [128, 8, 8], BF16))
        mgb_sb = e_(nc.sbuf_tensor(uname("mgb_sb"), [8, 1], F32))
        iota_sb = e_(nc.sbuf_tensor(uname("iota_sb"), [128, 8], F32))
        ident_sb = e_(nc.sbuf_tensor(uname("ident_sb"), [128, 128], F32))
        n2g_sb = e_(nc.sbuf_tensor(uname("n2g_sb"), [128, 8], F32))
        n2b_sb = e_(nc.sbuf_tensor(uname("n2b_sb"), [128, 8], F32))
        rsc_sb = e_(nc.sbuf_tensor(uname("rsc_sb"), [1, 8], F32))
        onesf1 = e_(nc.sbuf_tensor(uname("onesf1"), [1, 128], F32))
        cm_all = e_(nc.sbuf_tensor(uname("cm_all"), [128, 8, 16], F32))
        acc = e_(nc.sbuf_tensor(uname("acc"), [128, 8, TCORE], BF16))

        _bar = now_all(P) if not _own else []
        r_cst = [
            P.step("gpsimd", lambda e: e.dma_start(
                out=mgw_sb[:, :, :],
                in_=mgw.rearrange("(c p) o -> p c o", p=128)),
                deps=_bar, dma=True),
            P.step("sync", lambda e: e.dma_start(
                out=mgb_sb[:, :], in_=mgb[:, :]), deps=_bar, dma=True),
            P.step("sync", lambda e: e.dma_start(
                out=iota_sb[:, :], in_=iota8[:, :]), dma=True),
            P.step("sync", lambda e: e.dma_start(
                out=ident_sb[:, :], in_=ident[:, :]), dma=True),
            P.step("sync", lambda e: e.dma_start(
                out=n2g_sb[:, :], in_=n2g[:, :]), dma=True),
            P.step("sync", lambda e: e.dma_start(
                out=n2b_sb[:, :], in_=n2b[:, :]), dma=True),
            P.step("sync", lambda e: e.dma_start(
                out=rsc_sb[:, :], in_=rsc[:, :]), dma=True),
        ]
        r_of = P.step("vector", lambda e: e.memset(onesf1[:, :], 1.0))
        r_acc0 = P.step("vector", lambda e: e.memset(acc[:, :, :], 0.0))

        with ExitStack() as st_r:
            er = st_r.enter_context
            ybt = er(nc.sbuf_tensor(uname("ybt"), [128, 8, TSZ], BF16))
            lsb = er(nc.sbuf_tensor(uname("lsb"), [8, TSZ], F32))
            ltm = er(nc.sbuf_tensor(uname("ltm"), [128, 8], F32))
            e8 = er(nc.sbuf_tensor(uname("e8"), [128, 8], F32))
            t8 = er(nc.sbuf_tensor(uname("t8"), [128, 8], F32))
            vc1 = er(nc.sbuf_tensor(uname("vc1"), [128, 1], F32))
            vc2 = er(nc.sbuf_tensor(uname("vc2"), [128, 1], F32))
            chc = er(nc.sbuf_tensor(uname("chc"), [128, 1], F32))
            wsc = er(nc.sbuf_tensor(uname("wsc"), [128, 1], F32))
            tmpc = er(nc.sbuf_tensor(uname("tmpc"), [128, 1], F32))
            ps_l = er(nc.psum_tensor(uname("ps_l"), [8, TSZ], F32))
            ps_t = er(nc.psum_tensor(uname("ps_t"), [128, TSZ], F32))
            r_prev = None
            for ot in range(4):
                r_yl = P.step("gpsimd", lambda e, ot=ot: e.dma_start(
                    out=ybt[:, :, :], in_=y3[:, :, ot * TSZ:(ot + 1) * TSZ]),
                    deps=[r_prev], dma=True)
                r_mm = None
                for dc in range(8):
                    r_mm = P.step("tensor", lambda e, dc=dc: e.matmul(
                        ps_l[:, :], mgw_sb[:, dc, :], ybt[:, dc, :],
                        start=(dc == 0), stop=(dc == 7)),
                        deps=[r_yl, r_cst[0], r_prev])
                r_lb = P.step("vector", lambda e: e.tensor_scalar(
                    lsb[:, :], ps_l[0:8, :], mgb_sb[0:8, 0:1], None, ALU.add),
                    deps=[r_mm, r_cst[1]])
                for sub in range(4):
                    r_t = P.step("tensor", lambda e, sub=sub: e.transpose(
                        ps_t[:, 0:8], lsb[0:8, sub * 128:(sub + 1) * 128],
                        ident_sb[0:8, 0:8]), deps=[r_lb, r_cst[3], r_prev])
                    r_c = P.step("vector", lambda e: e.tensor_copy(
                        ltm[:, :], ps_t[:, 0:8]), deps=[r_t])
                    r_v1 = P.step("vector", lambda e: e.reduce_max(
                        vc1[:, :], ltm[:, :], axis=AX.X), deps=[r_c])
                    r_e1 = P.step("vector", lambda e: e.tensor_scalar(
                        e8[:, :], ltm[:, :], vc1[:, 0:1], None, ALU.is_equal),
                        deps=[r_v1])
                    r_t8 = P.step("vector", lambda e: e.scalar_tensor_tensor(
                        t8[:, :], e8[:, :], -1.0e38, ltm[:, :], ALU.mult,
                        ALU.add), deps=[r_e1])
                    r_v2 = P.step("vector", lambda e: e.reduce_max(
                        vc2[:, :], t8[:, :], axis=AX.X), deps=[r_t8])
                    r_e2 = P.step("vector", lambda e: e.tensor_scalar(
                        t8[:, :], t8[:, :], vc2[:, 0:1], None, ALU.is_equal),
                        deps=[r_v2])
                    r_es = P.step("vector", lambda e: e.tensor_add(
                        e8[:, :], e8[:, :], t8[:, :]), deps=[r_e2])
                    r_ei = P.step("vector", lambda e: e.tensor_mul(
                        e8[:, :], e8[:, :], iota_sb[:, :]),
                        deps=[r_es, r_cst[2]])
                    r_ch = P.step("vector", lambda e: e.reduce_max(
                        chc[:, :], e8[:, :], axis=AX.X), deps=[r_ei])
                    r_d = P.step("vector", lambda e: e.tensor_sub(
                        tmpc[:, :], vc2[:, :], vc1[:, :]), deps=[r_v2, r_v1])
                    r_ex = P.step("scalar", lambda e: e.activation(
                        tmpc[:, :], tmpc[:, :], AF.Exp), deps=[r_d])
                    r_d1 = P.step("vector", lambda e: e.tensor_scalar(
                        tmpc[:, :], tmpc[:, :], 1.0, None, ALU.add),
                        deps=[r_ex])
                    r_rc = P.step("vector", lambda e: e.reciprocal(
                        wsc[:, :], tmpc[:, :]), deps=[r_d1])
                    r_ws = P.step("vector", lambda e: e.tensor_mul(
                        wsc[:, :], wsc[:, :], tmpc[:, :]), deps=[r_rc])
                    r_last = None
                    for ex in range(8):
                        r_cm = P.step("vector", lambda e, ex=ex: e.tensor_scalar(
                            tmpc[:, :], chc[:, 0:1], float(ex), None,
                            ALU.is_equal), deps=[r_ch, r_ws, r_last])
                        r_last = P.step("vector",
                            lambda e, ex=ex, ot=ot, sub=sub:
                            e.scalar_tensor_tensor(
                                cm_all[:, ex, ot * 4 + sub:ot * 4 + sub + 1],
                                tmpc[:, :], 1.0, wsc[:, :], ALU.mult,
                                ALU.mult), deps=[r_cm])
                    r_prev = r_last
        r_router = r_prev

        st_e = ExitStack()
        e_ = st_e.enter_context
        we1 = e_(nc.sbuf_tensor(uname("we1"), [128, 8, 2048], BF16))
        we2 = e_(nc.sbuf_tensor(uname("we2"), [128, 16, D], BF16))
        b1_sb = e_(nc.sbuf_tensor(uname("b1_sb"), [128, 16], F32))
        lg_sb = e_(nc.sbuf_tensor(uname("lg_sb"), [128, 16], F32))
        lb_sb = e_(nc.sbuf_tensor(uname("lb_sb"), [128, 16], F32))
        b2_sb = e_(nc.sbuf_tensor(uname("b2_sb"), [128, 8], F32))
        b2r_sb = e_(nc.sbuf_tensor(uname("b2r_sb"), [128, 8], F32))
        rs1 = e_(nc.sbuf_tensor(uname("rs1"), [128, 1], F32))
        hbf = [e_(nc.sbuf_tensor(uname("hbf"), [128, 16, TSZ], BF16))
               for _ in range(2)]
        sqt = e_(nc.sbuf_tensor(uname("sqt"), [128, TSZ], BF16))
        gbf = e_(nc.sbuf_tensor(uname("gbf"), [128, 16, TSZ], BF16))
        osb = e_(nc.sbuf_tensor(uname("osb"), [128, 8, TSZ], F32))
        ybt2 = [e_(nc.sbuf_tensor(uname("ybt2"), [128, 8, TSZ], BF16))
                for _ in range(2)]
        yft = e_(nc.sbuf_tensor(uname("yft"), [128, 8, TSZ], F32))
        mkb = e_(nc.sbuf_tensor(uname("mkb"), [128, TSZ], F32))
        mrow_e = e_(nc.sbuf_tensor(uname("mrow_e"), [1, TSZ], F32))
        em_sb = e_(nc.sbuf_tensor(uname("em_sb"), [1, TSZ], F32))
        ev_sb = e_(nc.sbuf_tensor(uname("ev_sb"), [1, TSZ], F32))
        exr = e_(nc.sbuf_tensor(uname("exr"), [1, TSZ], F32))
        eeps = e_(nc.sbuf_tensor(uname("eeps"), [1, 1], F32))
        o128e = e_(nc.sbuf_tensor(uname("o128e"), [128, 1], BF16))
        ps_h = [e_(nc.psum_tensor(uname("ps_h"), [128, TSZ], F32))
                for _ in range(2)]
        ps_o2 = [e_(nc.psum_tensor(uname("ps_o2"), [128, TSZ], F32))
                 for _ in range(2)]
        ps_es = e_(nc.psum_tensor(uname("ps_es"), [1, TSZ], F32))
        ps_eq = e_(nc.psum_tensor(uname("ps_eq"), [1, TSZ], F32))
        ps_eb = e_(nc.psum_tensor(uname("ps_eb"), [128, TSZ], F32))
        ps_er = e_(nc.psum_tensor(uname("ps_er"), [128, TSZ], F32))
        ps_mk = ps_eb

        r_e0 = [P.step("vector", lambda e: e.memset(eeps[:, :], 1e-5)),
                P.step("vector", lambda e: e.memset(o128e[:, :], 1.0)),
                r_of, r_acc0]

        r_wuse = r_router

        def ld_we1(ex):
            return lambda e: e.dma_start(
                out=we1[:, :, :],
                in_=w1[ex, :, :].rearrange("(c p) f -> p c f", p=128))

        def ld_we2(ex, half):
            return lambda e: e.dma_start(
                out=we2[:, :, half * 512:(half + 1) * 512],
                in_=w2[ex, :, half * 512:(half + 1) * 512].rearrange(
                    "(c p) o -> p c o", p=128))

        r_w1 = P.step("gpsimd", ld_we1(0), deps=[r_router], dma=True)
        r_w2a = P.step("gpsimd", ld_we2(0, 0), deps=[r_router], dma=True)
        r_w2b = P.step("gpsimd", ld_we2(0, 1), deps=[r_router], dma=True)
        pre_yb = None
        r_ybl = {}
        r_hsl = {}
        r_accw = {ot: None for ot in range(4)}
        r_huse = None
        r_guse = None
        r_ouse = None
        r_yuse = None
        r_ruse = None
        for ex in range(8):
            r_bs = [
                P.step("sync", lambda e, ex=ex: e.dma_start(
                    out=b1_sb[:, :], in_=b1x[ex, :, :]), deps=[r_wuse],
                    dma=True),
                P.step("sync", lambda e, ex=ex: e.dma_start(
                    out=lg_sb[:, :], in_=lng[ex, :, :]), deps=[r_wuse],
                    dma=True),
                P.step("sync", lambda e, ex=ex: e.dma_start(
                    out=lb_sb[:, :], in_=lnb[ex, :, :]), deps=[r_wuse],
                    dma=True),
                P.step("sync", lambda e, ex=ex: e.dma_start(
                    out=b2_sb[:, :], in_=b2x[ex, :, :]), deps=[r_wuse],
                    dma=True),
            ]
            r_rb = P.step("tensor", lambda e, ex=ex: e.matmul(
                ps_mk[:, 0:1], onesf1[:, :], rsc_sb[0:1, ex:ex + 1],
                start=True, stop=True),
                deps=[r_cst[6], r_of, r_ruse])
            r_rs1 = P.step("vector", lambda e: e.tensor_copy(
                rs1[:, :], ps_mk[:, 0:1]), deps=[r_rb])
            r_b2r = P.step("vector", lambda e: e.tensor_scalar(
                b2r_sb[:, :], b2_sb[:, :], rs1[:, 0:1], None, ALU.mult),
                deps=[r_rs1, r_bs[3]])

            r_wuse_next = []

            def emit_h(ot, tb, ex=ex, r_w1=r_w1, pre_yb=pre_yb):
                if ot == 0 and pre_yb is not None:
                    r_yb = pre_yb
                else:
                    r_yb = P.step("gpsimd", lambda e, ot=ot, tb=tb:
                        e.dma_start(
                            out=ybt2[tb][:, :, :],
                            in_=y3[:, :, ot * TSZ:(ot + 1) * TSZ]),
                        deps=[r_ybl.get(tb)], dma=True)
                r_hs = {}
                r_mm = None
                for fc in range(16):
                    pb = fc % 2
                    for dc in range(8):
                        r_mm = P.step("tensor",
                            lambda e, fc=fc, dc=dc, pb=pb, tb=tb: e.matmul(
                                ps_h[pb][:, :],
                                we1[:, dc, fc * 128:(fc + 1) * 128],
                                ybt2[tb][:, dc, :],
                                start=(dc == 0), stop=(dc == 7)),
                            deps=[r_w1, r_yb, r_hs.get(fc - 2)]
                            if dc == 0 else [r_w1, r_yb])
                    r_hb = P.step("scalar", lambda e, fc=fc, pb=pb, tb=tb:
                        e.add(
                            hbf[tb][:, fc, :], ps_h[pb][:, :],
                            b1_sb[:, fc:fc + 1]),
                        deps=[r_mm, r_bs[0], r_hsl.get(tb)])
                    r_hs[fc] = r_hb
                return r_hs

            r_hs_next = emit_h(0, 0)
            for ot in range(4):
                tb = ot % 2
                r_hs = r_hs_next
                r_yf = P.step("sync", lambda e, ot=ot: e.dma_start(
                    out=yft[:, :, :], in_=y3[:, :, ot * TSZ:(ot + 1) * TSZ]),
                    deps=[r_yuse], dma=True)
                if ot + 1 < 4:
                    r_hs_next = emit_h(ot + 1, (ot + 1) % 2)
                r_s = None
                for fc in range(16):
                    r_s = P.step("tensor", lambda e, fc=fc, tb=tb: e.matmul(
                        ps_es[:, :], o128e[:, :], hbf[tb][:, fc, :],
                        start=(fc == 0), stop=(fc == 15)),
                        deps=[r_hs[fc], r_e0[1]] +
                             ([r_guse] if fc == 0 else []))
                r_sq = None
                r_squse = None
                for fc in range(16):
                    r_q = P.step("gpsimd", lambda e, fc=fc, tb=tb:
                        e.tensor_mul(
                            sqt[:, :], hbf[tb][:, fc, :], hbf[tb][:, fc, :]),
                        deps=[r_hs[fc], r_squse])
                    r_sq = P.step("tensor", lambda e, fc=fc: e.matmul(
                        ps_eq[:, :], o128e[:, :], sqt[:, :],
                        start=(fc == 0), stop=(fc == 15)),
                        deps=[r_q] + ([r_guse] if fc == 0 else []))
                    r_squse = r_sq
                r_m = P.step("scalar", lambda e: e.activation(
                    em_sb[:, :], ps_es[0:1, :], AF.Copy, scale=1.0 / 2048),
                    deps=[r_s, r_ruse])
                r_m2 = P.step("vector", lambda e: e.tensor_mul(
                    exr[:, :], em_sb[:, :], em_sb[:, :]), deps=[r_m])
                r_v = P.step("vector", lambda e: e.scalar_tensor_tensor(
                    ev_sb[:, :], ps_eq[0:1, :], 1.0 / 2048, exr[:, :],
                    ALU.mult, ALU.subtract), deps=[r_sq, r_m2])
                r_ln = P.step("scalar", lambda e: e.activation(
                    ev_sb[:, :], ev_sb[:, :], AF.Ln, bias=eeps[0:1, 0:1]),
                    deps=[r_v, r_e0[0]])
                r_rsd = P.step("scalar", lambda e: e.activation(
                    ev_sb[:, :], ev_sb[:, :], AF.Exp, scale=-0.5),
                    deps=[r_ln])
                r_bm = P.step("tensor", lambda e: e.matmul(
                    ps_eb[:, :], onesf1[:, :], em_sb[:, :], start=True,
                    stop=True), deps=[r_m, r_of, r_guse, r_rs1])
                r_br = P.step("tensor", lambda e: e.matmul(
                    ps_er[:, :], onesf1[:, :], ev_sb[:, :], start=True,
                    stop=True), deps=[r_rsd, r_of, r_guse])
                r_gs = {}
                for fc in range(16):
                    r_n1 = P.step("vector", lambda e, fc=fc, tb=tb:
                        e.tensor_sub(
                            gbf[:, fc, :], hbf[tb][:, fc, :], ps_eb[:, :]),
                        deps=[r_bm, r_hs[fc], r_guse])
                    r_n2 = P.step("vector", lambda e, fc=fc: e.tensor_mul(
                        gbf[:, fc, :], gbf[:, fc, :], ps_er[:, :]),
                        deps=[r_n1, r_br])
                    r_g = P.step("scalar", lambda e, fc=fc: e.activation(
                        gbf[:, fc, :], gbf[:, fc, :], AF.Gelu,
                        bias=lb_sb[:, fc:fc + 1], scale=lg_sb[:, fc:fc + 1]),
                        deps=[r_n2, r_bs[1], r_bs[2]])
                    r_gs[fc] = r_g
                r_huse = r_gs[15]
                r_hsl[tb] = r_gs[15]
                r_ybl[tb] = r_gs[15]
                r_os = {}
                for oc in range(8):
                    pb = oc % 2
                    r_w2h = r_w2a if oc < 4 else r_w2b
                    r_mm = None
                    for fc in range(16):
                        r_mm = P.step("tensor",
                            lambda e, oc=oc, fc=fc, pb=pb: e.matmul(
                                ps_o2[pb][:, :],
                                we2[:, fc, oc * 128:(oc + 1) * 128],
                                gbf[:, fc, :],
                                start=(fc == 0), stop=(fc == 15)),
                            deps=[r_w2h, r_gs[fc], r_os.get(oc - 2)]
                            if fc == 0 else [r_w2h, r_gs[fc]])
                    if ot == 3 and oc == 3:
                        r_w2a_free = r_mm
                    if ot == 3 and oc == 7:
                        r_w2b_free = r_mm
                    r_o1 = P.step("vector", lambda e, oc=oc, pb=pb:
                        e.tensor_scalar(
                            osb[:, oc, :], ps_o2[pb][:, :], rs1[:, 0:1],
                            b2r_sb[:, oc:oc + 1], ALU.mult, ALU.add),
                        deps=[r_mm, r_b2r, r_ouse])
                    r_o2 = P.step("vector", lambda e, oc=oc: e.tensor_add(
                        osb[:, oc, :], osb[:, oc, :], yft[:, oc, :]),
                        deps=[r_o1, r_yf])
                    r_os[oc] = r_o2
                r_yuse = r_os[7]
                r_guse = r_os[7]
                r_trs = []
                for sub in range(4):
                    r_tt = P.step("tensor", lambda e, ex=ex, ot=ot, sub=sub:
                        e.transpose(
                            ps_mk[0:1, sub * 128:sub * 128 + 128],
                            cm_all[:, ex, ot * 4 + sub:ot * 4 + sub + 1],
                            ident_sb[:, :]),
                        deps=[r_router, r_ruse, r_rs1, r_cst[3], r_os[7],
                              r_bm])
                    r_trs.append(r_tt)
                r_mr = P.step("vector", lambda e: e.tensor_copy(
                    mrow_e[0:1, :], ps_mk[0:1, :]), deps=r_trs)
                r_mk1 = P.step("tensor", lambda e: e.matmul(
                    ps_mk[:, :], onesf1[:, :], mrow_e[0:1, :],
                    start=True, stop=True),
                    deps=[r_mr, r_of])
                r_mk2 = P.step("vector", lambda e: e.tensor_copy(
                    mkb[:, :], ps_mk[:, :]), deps=[r_mk1])
                r_msk = {}
                for oc in range(8):
                    r_ms = P.step("vector", lambda e, oc=oc: e.tensor_mul(
                        osb[:, oc, :], osb[:, oc, :], mkb[:, :]),
                        deps=[r_mk2, r_os[oc]])
                    r_ac = P.step("vector", lambda e, oc=oc, ot=ot: e.tensor_add(
                        acc[:, oc, ot * TSZ:(ot + 1) * TSZ],
                        acc[:, oc, ot * TSZ:(ot + 1) * TSZ], osb[:, oc, :]),
                        deps=[r_ms, r_accw[ot]])
                    r_msk[oc] = r_ac
                r_ruse = r_mk2
                r_accw[ot] = r_msk[7]
                r_ouse = r_msk[7]
                r_wuse_next.append(r_msk[7])
                if ot == 3:
                    r_h_ot3 = r_hs
            if ex < 7:
                pre_yb = P.step("gpsimd", lambda e: e.dma_start(
                    out=ybt2[0][:, :, :], in_=y3[:, :, 0:TSZ]),
                    deps=[r_ybl.get(0)], dma=True)
                r_w1 = P.step("gpsimd", ld_we1(ex + 1),
                              deps=[r_h_ot3[15]], dma=True)
                r_w2a = P.step("gpsimd", ld_we2(ex + 1, 0),
                               deps=[r_w2a_free], dma=True)
                r_w2b = P.step("gpsimd", ld_we2(ex + 1, 1),
                               deps=[r_w2b_free], dma=True)
            r_wuse = [r_wuse_next[-1]]
        r_moe = r_accw
        st_e.close()
        e_ = st.enter_context

        with ExitStack() as st_f:
            ef = st_f.enter_context
            tt = ef(nc.sbuf_tensor(uname("tt"), [128, 8, TSZ], F32))
            t2 = ef(nc.sbuf_tensor(uname("t2"), [128, 8, TSZ], F32))
            x1t = ef(nc.sbuf_tensor(uname("x1t"), [128, 8, TSZ], F32))
            lc = make_ln_ctx(nc, ef, "fl")
            ln_ctx_init(nc, P, lc)
            r_st = None
            r_tu = None
            for ot in range(4):
                r_yf2 = P.step("sync", lambda e, ot=ot: e.dma_start(
                    out=tt[:, :, :], in_=y3[:, :, ot * TSZ:(ot + 1) * TSZ]),
                    deps=[r_moe[ot], r_tu, now_all(P) if ot == 0 else None],
                    dma=True)
                r_x1 = P.step("sync", lambda e, ot=ot: e.dma_start(
                    out=x1t[:, :, :], in_=x13[:, :, ot * TSZ:(ot + 1) * TSZ]),
                    deps=[r_tu], dma=True)
                r_ad = []
                for oc in range(8):
                    r_ad.append(P.step("vector",
                        lambda e, oc=oc, ot=ot: e.tensor_add(
                            tt[:, oc, :], tt[:, oc, :],
                            acc[:, oc, ot * TSZ:(ot + 1) * TSZ]),
                        deps=[r_yf2, r_moe[ot]]))
                r_l1 = emit_ln_tile(nc, P, lc, tt, tt, r_ad,
                                    gamma_beta=(n2g_sb, n2b_sb),
                                    r_gb=r_cst[4])
                r_a2 = []
                for oc in range(8):
                    r_a2.append(P.step("vector",
                        lambda e, oc=oc: e.tensor_add(
                            t2[:, oc, :], tt[:, oc, :], x1t[:, oc, :]),
                        deps=[r_l1[oc], r_x1]))
                r_l2 = emit_ln_tile(nc, P, lc, t2, t2, r_a2)
                r_st = P.step("sync", lambda e, ot=ot: e.dma_start(
                    out=out3[:, :, ot * TSZ:(ot + 1) * TSZ], in_=t2[:, :, :]),
                    deps=r_l2, dma=True)
                r_tu = r_st

        if _own:
            sems = {e: e_(nc.semaphore(uname("sem_" + e)))
                    for e in ENGINES}
            block = e_(nc.Block())
            P.finalize(block, sems)
    return nc


def vec_pc16(v):
    return np.ascontiguousarray(np.asarray(v, np.float32).reshape(16, 128).T)


def run_stageBmoe(y_fm, x1_fm, mgw_, mgb_, w1_, b1_, lng_, lnb_, w2_, b2_,
                  rsc_, n2g_, n2b_):
    if "Bmoe" not in _nc_cache:
        _nc_cache["Bmoe"] = build_stageBmoe()
    nc = _nc_cache["Bmoe"]
    iota8 = np.broadcast_to(np.arange(8, dtype=np.float32), (128, 8)).copy()
    ident = np.eye(128, dtype=np.float32)
    b1x = np.stack([vec_pc16(b1_[e]) for e in range(8)])
    lngx = np.stack([vec_pc16(lng_[e]) for e in range(8)])
    lnbx = np.stack([vec_pc16(lnb_[e]) for e in range(8)])
    b2x = np.stack([vec_pc(b2_[e]) for e in range(8)])
    in_maps = []
    for c in range(NCORES):
        b = c // 4
        cols = my_cols(c)
        in_maps.append({
            "ymy": np.ascontiguousarray(y_fm[b][:, cols]),
            "x1my": np.ascontiguousarray(x1_fm[b][:, cols]),
            "mgw": np.asarray(mgw_), "mgb": np.asarray(mgb_, np.float32).reshape(8, 1),
            "w1": np.asarray(w1_), "b1x": b1x, "lng": lngx, "lnb": lnbx,
            "w2": np.asarray(w2_), "b2x": b2x,
            "rsc": np.asarray(rsc_, np.float32).reshape(1, 8),
            "n2g": vec_pc(n2g_), "n2b": vec_pc(n2b_),
            "iota8": iota8, "ident": ident,
        })
    res = run_bass_kernel_spmd(nc, in_maps, core_ids=list(range(NCORES)))
    run_stageBmoe.last_results = res.results
    return scatter_from_cores([np.asarray(r["x2my"]) for r in res.results])


ROW_OF_TILE = [None] * NT
for _r in range(4):
    for _s in range(4):
        ROW_OF_TILE[TILES[_r][_s]] = 4 * _r + _s


def build_stageC(fused=None):
    nc, P, _cache, _own = _begin(fused)
    _ctx = (nc, _cache, fused is not None)
    x2my = dram(_ctx, "x2my", [D, TCORE], F32, "ExternalInput")
    a_pc = dram(_ctx, "a_pc", [128, 8], F32, "ExternalInput")
    b_pc = dram(_ctx, "b_pc", [128, 8], F32, "ExternalInput")
    woT = dram(_ctx, "woT", [D, D], F32, "ExternalInput")
    bo_pc = dram(_ctx, "bo_pc", [128, 8], F32, "ExternalInput")
    ident = dram(_ctx, "identC", [128, 128], F32, "ExternalInput")
    x3my = dram(_ctx, "x3my", [D, TCORE], F32, "ExternalOutput")

    u_in = dram(_ctx, "u_in", [128, 32], F32, "Internal")
    u_g = dram(_ctx, "u_g", [512, 32], F32, "Internal")
    groups = [[0, 1, 2, 3], [4, 5, 6, 7]]

    x23 = fm3(x2my)

    with ExitStack() as st:
        e_ = st.enter_context
        scn = e_(nc.sbuf_tensor(uname("scn"), [128, 8, 16, 128], F32))
        ysb = e_(nc.sbuf_tensor(uname("ysb"), [128, 8, TCORE], BF16))
        ah = e_(nc.sbuf_tensor(uname("ah"), [128, 8], F32))
        bb = e_(nc.sbuf_tensor(uname("bb"), [128, 8], F32))
        ahp = e_(nc.sbuf_tensor(uname("ahp"), [128, 8, 128], F32))
        a128 = e_(nc.sbuf_tensor(uname("a128"), [128, 8], F32))
        a512 = e_(nc.sbuf_tensor(uname("a512"), [128, 8], F32))
        umy = e_(nc.sbuf_tensor(uname("umy"), [128, 8, 4], F32))
        ugs = e_(nc.sbuf_tensor(uname("ugs"), [128, 4, 32], F32))
        tch = e_(nc.sbuf_tensor(uname("tch"), [128, 8, NT + 1], F32))
        cin = e_(nc.sbuf_tensor(uname("cin"), [128, 8, 4], F32))
        idc = e_(nc.sbuf_tensor(uname("idc"), [128, 128], F32))
        scn_sel = e_(nc.sbuf_tensor(uname("scn_sel"), [128, NT], F32))
        ps_t = e_(nc.psum_tensor(uname("ps_tC"), [128, TSZ], F32))

        _bar = now_all(P) if not _own else []
        r_a = P.step("sync", lambda e: e.dma_start(out=ah[:, :],
                     in_=a_pc[:, :]), deps=_bar, dma=True)
        r_b = P.step("sync", lambda e: e.dma_start(out=bb[:, :],
                     in_=b_pc[:, :]), deps=_bar, dma=True)
        r_id = P.step("sync", lambda e: e.dma_start(out=idc[:, :],
                      in_=ident[:, :]), deps=_bar, dma=True)
        r_ah = P.step("scalar", lambda e: e.activation(
            ah[:, :], ah[:, :], AF.Tanh), deps=[r_a])

        r_pw = {}
        for dc in range(8):
            r_p = P.step("vector", lambda e, dc=dc: e.tensor_copy(
                ahp[:, dc, 0:1], ah[:, dc:dc + 1]), deps=[r_ah])
            n = 1
            while n < 128:
                r_p = P.step("vector", lambda e, dc=dc, n=n: e.tensor_scalar(
                    ahp[:, dc, n:2 * n], ahp[:, dc, 0:n],
                    ahp[:, dc, n - 1:n], None, ALU.mult), deps=[r_p])
                n *= 2
            r_1 = P.step("vector", lambda e, dc=dc: e.tensor_scalar(
                a128[:, dc:dc + 1], ahp[:, dc, 127:128],
                ah[:, dc:dc + 1], None, ALU.mult), deps=[r_p])
            r_2 = P.step("vector", lambda e, dc=dc: e.tensor_mul(
                a512[:, dc:dc + 1], a128[:, dc:dc + 1], a128[:, dc:dc + 1]),
                deps=[r_1])
            r_2 = P.step("vector", lambda e, dc=dc: e.tensor_mul(
                a512[:, dc:dc + 1], a512[:, dc:dc + 1], a512[:, dc:dc + 1]),
                deps=[r_2])
            r_pw[dc] = (r_1, r_2)

        r_scan = {}
        r_umy = []
        for dc in range(8):
            r_x = P.step("sync", lambda e, dc=dc: e.dma_start(
                out=scn[:, dc, :, :].rearrange("p j l -> p (j l)"),
                in_=x23[:, dc, :]), dma=True)
            r_bx = P.step("vector", lambda e, dc=dc: e.tensor_scalar(
                scn[:, dc, :, :], scn[:, dc, :, :], bb[:, dc:dc + 1], None,
                ALU.mult), deps=[r_x, r_b])
            r_l = r_bx
            for l in range(1, 128):
                r_l = P.step("vector", lambda e, dc=dc, l=l:
                    e.scalar_tensor_tensor(
                        scn[:, dc, :, l:l + 1], scn[:, dc, :, l - 1:l],
                        ah[:, dc:dc + 1], scn[:, dc, :, l:l + 1],
                        ALU.mult, ALU.add), deps=[r_l] +
                        ([r_ah] if l == 1 else []))
            r_scan[dc] = r_l
            for sl in range(4):
                r_c = P.step("vector", lambda e, dc=dc, sl=sl: e.tensor_copy(
                    umy[:, dc, sl:sl + 1], scn[:, dc, 4 * sl, 127:128]),
                    deps=[r_l])
                for s_ in range(1, 4):
                    r_c = P.step("vector", lambda e, dc=dc, sl=sl, s_=s_:
                        e.scalar_tensor_tensor(
                            umy[:, dc, sl:sl + 1], umy[:, dc, sl:sl + 1],
                            a128[:, dc:dc + 1],
                            scn[:, dc, 4 * sl + s_, 127:128],
                            ALU.mult, ALU.add),
                        deps=[r_c, r_pw[dc][0]])
                r_umy.append(r_c)

        r_uw = P.step("sync", lambda e: e.dma_start(
            out=u_in[:, :], in_=umy[:, :, :].rearrange("p c s -> p (c s)")),
            deps=r_umy, dma=True)
        cc_sem = e_(nc.semaphore(uname("cc_u")))
        r_cc = P.collective(lambda e: e.collective_compute(
            "AllGather", ALU.bypass, replica_groups=groups,
            ins=[u_in[:, :].opt()], outs=[u_g[:, :].opt()]),
            deps=[r_uw], sem=cc_sem)
        r_u16 = P.step("sync", lambda e: e.dma_start(
            out=ugs[:, :, :], in_=u_g.rearrange("(r p) cs -> p r cs", p=128)),
            deps=[r_cc], dma=True)

        my_tiles = None
        selq = dram(_ctx, "selq", [128, NT, 4], F32, "ExternalInput")
        sel_sb = e_(nc.sbuf_tensor(uname("sel_sb"), [128, NT, 4], F32))
        r_sel = P.step("sync", lambda e: e.dma_start(
            out=sel_sb[:, :, :], in_=selq[:, :, :]), dma=True)
        r_fin = []
        for dc in range(8):
            r_t0 = P.step("vector", lambda e, dc=dc: e.memset(
                tch[:, dc, 0:1], 0.0), deps=[])
            r_tc = r_t0
            for j in range(NT):
                row = ROW_OF_TILE[j]
                rr, ss = row // 4, row % 4
                r_tc = P.step("vector", lambda e, dc=dc, j=j, rr=rr, ss=ss:
                    e.scalar_tensor_tensor(
                        tch[:, dc, j + 1:j + 2], tch[:, dc, j:j + 1],
                        a512[:, dc:dc + 1],
                        ugs[:, rr, dc * 4 + ss:dc * 4 + ss + 1],
                        ALU.mult, ALU.add),
                    deps=[r_tc, r_u16, r_pw[dc][1]])
            for sl in range(4):
                r_m = P.step("vector", lambda e, dc=dc, sl=sl: e.tensor_mul(
                    scn_sel[:, 0:NT], tch[:, dc, 0:NT], sel_sb[:, :, sl]),
                    deps=[r_tc, r_sel])
                r_cin = P.step("vector", lambda e, dc=dc, sl=sl: e.reduce_sum(
                    cin[:, dc, sl:sl + 1], scn_sel[:, 0:NT], axis=AX.X),
                    deps=[r_m])
                r_cc2 = r_cin
                for s_ in range(4):
                    js = 4 * sl + s_
                    r_co = P.step("vector", lambda e, dc=dc, js=js, sl=sl:
                        e.scalar_tensor_tensor(
                            scn[:, dc, js, :], ahp[:, dc, :],
                            cin[:, dc, sl:sl + 1], scn[:, dc, js, :],
                            ALU.mult, ALU.add),
                        deps=[r_cc2, r_scan[dc]])
                    if s_ < 3:
                        r_cc2 = P.step("vector", lambda e, dc=dc, js=js,
                                       sl=sl: e.tensor_copy(
                            cin[:, dc, sl:sl + 1], scn[:, dc, js, 127:128]),
                            deps=[r_co])
                    else:
                        r_fin.append(r_co)
            r_fin.append(P.step("vector", lambda e, dc=dc: e.tensor_copy(
                ysb[:, dc, :], scn[:, dc, :, :].rearrange(
                    "p j l -> p (j l)")), deps=[r_fin[-1]]))

        wo_sb = e_(nc.sbuf_tensor(uname("woT_sb"), [128, 8, D], BF16))
        bo_sb = e_(nc.sbuf_tensor(uname("bo_sb"), [128, 8], F32))
        r_wo = P.step("gpsimd", lambda e: e.dma_start(
            out=wo_sb[:, :, :], in_=woT.rearrange("(c p) o -> p c o", p=128)),
            dma=True)
        r_bo = P.step("sync", lambda e: e.dma_start(
            out=bo_sb[:, :], in_=bo_pc[:, :]), dma=True)
        with ExitStack() as st_post:
            emit_proj_residual_ln(nc, P, st_post, src_fm=ysb, w_sb=wo_sb,
                                  r_w=r_wo, resid_src=x2my, resid_scale=1.0,
                                  out_hbm=x3my, bias_sb=bo_sb, r_gb=r_bo,
                                  r_src=r_fin, name="Cp")

        if _own:
            sems = {e: e_(nc.semaphore(uname("sem_" + e)))
                    for e in ENGINES}
            block = e_(nc.Block())
            P.finalize(block, sems)
    return nc


def run_stageC(x2_fm, a_, b_, w_out_, b_out_):
    if "C" not in _nc_cache:
        _nc_cache["C"] = build_stageC()
    nc = _nc_cache["C"]
    woT = np.ascontiguousarray(np.asarray(w_out_, np.float32).T)
    ident = np.eye(128, dtype=np.float32)
    in_maps = []
    for c in range(NCORES):
        b = c // 4
        cols = my_cols(c)
        sel = np.zeros((NT, 4), np.float32)
        for sl in range(4):
            sel[core_tiles(c)[sl], sl] = 1.0
        selq = np.broadcast_to(sel, (128, NT, 4)).copy()
        in_maps.append({
            "x2my": np.ascontiguousarray(x2_fm[b][:, cols]),
            "a_pc": vec_pc(a_), "b_pc": vec_pc(b_),
            "woT": woT, "bo_pc": vec_pc(b_out_),
            "identC": ident, "selq": selq,
        })
    res = run_bass_kernel_spmd(nc, in_maps, core_ids=list(range(NCORES)))
    run_stageC.last_results = res.results
    return scatter_from_cores([np.asarray(r["x3my"]) for r in res.results])




def build_stageD(fused=None):
    nc, P, _cache, _own = _begin(fused)
    _ctx = (nc, _cache, fused is not None)
    x3my = dram(_ctx, "x3my", [D, TCORE], F32, "ExternalInput")
    wrT = dram(_ctx, "wrT", [D, 16], F32, "ExternalInput")
    gup = dram(_ctx, "gup", [17, D, 4096], F32, "ExternalInput")
    dwn = dram(_ctx, "dwn", [17, 2048, D], F32, "ExternalInput")
    ident = dram(_ctx, "identD", [128, 128], F32, "ExternalInput")
    utri = dram(_ctx, "utri", [128, 128], F32, "ExternalInput")
    iotac = dram(_ctx, "iotac", [128, 1], F32, "ExternalInput")
    iota16r = dram(_ctx, "iota16r", [128, 16], F32, "ExternalInput")
    x4my = dram(_ctx, "x4my", [D, TCORE], F32, "ExternalOutput")
    x3tm = dram(_ctx, "x3tm", [TCORE, D], BF16, "Internal")
    otm = dram(_ctx, "otm", [TCORE, D], BF16, "Internal")
    idxd = dram(_ctx, "idxd", [16 * NSLOT, 1], I32, "Internal")

    x33 = fm3(x3my)
    out3 = fm3(x4my)

    with ExitStack() as st:
        e_ = st.enter_context
        wr_sb = e_(nc.sbuf_tensor(uname("wr_sb"), [128, 8, 16], BF16))
        ident_sb = e_(nc.sbuf_tensor(uname("idD"), [128, 128], F32))
        identb = e_(nc.sbuf_tensor(uname("idDb"), [128, 128], BF16))
        utb = e_(nc.sbuf_tensor(uname("utb"), [128, 128], BF16))
        onesf1 = e_(nc.sbuf_tensor(uname("onesfD"), [1, 128], F32))
        o128b = e_(nc.sbuf_tensor(uname("o128D"), [128, 1], BF16))
        iota_sb = e_(nc.sbuf_tensor(uname("iotaD"), [128, 1], F32))
        i16_sb = e_(nc.sbuf_tensor(uname("i16D"), [128, 16], F32))
        cm_all = e_(nc.sbuf_tensor(uname("cmD"), [128, 16, 16], F32))
        eid_all = e_(nc.sbuf_tensor(uname("eidD"), [128, 16], F32))
        sumacc = e_(nc.sbuf_tensor(uname("sumacc"), [128, 8, TCORE], BF16))
        bigc = e_(nc.sbuf_tensor(uname("bigD"), [128, 1], F32))
        _bar = now_all(P) if not _own else []
        r_cst = [
            P.step("gpsimd", lambda e: e.dma_start(
                out=wr_sb[:, :, :],
                in_=wrT.rearrange("(c p) o -> p c o", p=128)),
                deps=_bar, dma=True),
            P.step("sync", lambda e: e.dma_start(
                out=ident_sb[:, :], in_=ident[:, :]), deps=_bar, dma=True),
            P.step("gpsimd", lambda e: e.dma_start(
                out=utb[:, :], in_=utri[:, :]), deps=_bar, dma=True),
            P.step("sync", lambda e: e.dma_start(
                out=iota_sb[:, :], in_=iotac[:, :]), deps=_bar, dma=True),
            P.step("sync", lambda e: e.dma_start(
                out=i16_sb[:, :], in_=iota16r[:, :]), deps=_bar, dma=True),
        ]
        r_idb = P.step("vector", lambda e: e.tensor_copy(
            identb[:, :], ident_sb[:, :]), deps=[r_cst[1]])
        r_of = P.step("vector", lambda e: e.memset(onesf1[:, :], 1.0))
        r_o128 = P.step("vector", lambda e: e.memset(o128b[:, :], 1.0))
        r_big = P.step("vector", lambda e: e.memset(bigc[:, :], 131072.0))

        ibig = e_(nc.sbuf_tensor(uname("ibig"), [128, 64], I32))
        flb = e_(nc.sbuf_tensor(uname("flbD"), [128, 4], I32))
        zb = e_(nc.sbuf_tensor(uname("zbD"), [128, D], BF16))
        r_ib = P.step("vector", lambda e: e._memset_packed(
            ibig[:, :], 1 << 29), deps=_bar)
        _regs = {}

        def _bcreg(e, v):
            if v not in _regs:
                _regs[v] = e.to_reg(v)
            return _regs[v]
        r_idxinit = P.step("sync", lambda e: e.dma_start(
            out=idxd.rearrange("(p c) o -> p (c o)", p=128),
            in_=ibig[:, :]), deps=[r_ib], dma=True)
        r_zb = P.step("vector", lambda e: e.memset(zb[:, :], 0.0),
                      deps=_bar)
        r_otz = []
        for sub in range(16):
            r_otz.append(P.step("sync", lambda e, sub=sub: e.dma_start(
                out=otm[sub * 128:(sub + 1) * 128, :], in_=zb[:, :]),
                deps=[r_zb], dma=True))

        with ExitStack() as st_r:
            er = st_r.enter_context
            xbt = er(nc.sbuf_tensor(uname("xbtD"), [128, 8, TSZ], BF16))
            lsb = er(nc.sbuf_tensor(uname("lsbD"), [16, TSZ], F32))
            ltm = er(nc.sbuf_tensor(uname("ltmD"), [128, 16], F32))
            e16 = er(nc.sbuf_tensor(uname("e16D"), [128, 16], F32))
            vc1 = er(nc.sbuf_tensor(uname("vc1D"), [128, 1], F32))
            ssc = er(nc.sbuf_tensor(uname("sscD"), [128, 1], F32))
            tmst = er(nc.sbuf_tensor(uname("tmstD"), [128, 8, 128], BF16))
            ps_l = er(nc.psum_tensor(uname("ps_lD"), [16, TSZ], F32))
            ps_t = er(nc.psum_tensor(uname("ps_tD"), [128, TSZ], F32))
            ps_tb = er(nc.psum_tensor(uname("ps_tbD"), [128, 1024], BF16))
            r_prev = None
            r_tmprev = None
            for ot in range(4):
                r_xl = P.step("gpsimd", lambda e, ot=ot: e.dma_start(
                    out=xbt[:, :, :], in_=x33[:, :, ot * TSZ:(ot + 1) * TSZ]),
                    deps=[r_prev, r_tmprev], dma=True)
                r_mm = None
                for dc in range(8):
                    r_mm = P.step("tensor", lambda e, dc=dc: e.matmul(
                        ps_l[:, :], wr_sb[:, dc, :], xbt[:, dc, :],
                        start=(dc == 0), stop=(dc == 7)),
                        deps=[r_xl, r_cst[0], r_prev])
                r_lb = P.step("vector", lambda e: e.tensor_copy(
                    lsb[:, :], ps_l[0:16, :]), deps=[r_mm])
                for sub in range(4):
                    chunk = ot * 4 + sub
                    r_t = P.step("tensor", lambda e, sub=sub: e.transpose(
                        ps_t[:, 0:16], lsb[0:16, sub * 128:(sub + 1) * 128],
                        ident_sb[0:16, 0:16]), deps=[r_lb, r_cst[1], r_prev])
                    r_c = P.step("vector", lambda e: e.tensor_copy(
                        ltm[:, :], ps_t[:, 0:16]), deps=[r_t])
                    r_v1 = P.step("vector", lambda e: e.reduce_max(
                        vc1[:, :], ltm[:, :], axis=AX.X), deps=[r_c])
                    r_e1 = P.step("vector", lambda e: e.tensor_scalar(
                        e16[:, :], ltm[:, :], vc1[:, 0:1], None,
                        ALU.is_equal), deps=[r_v1])
                    r_cm = P.step("vector", lambda e, chunk=chunk:
                        e.tensor_copy(
                            cm_all[:, 0:16, chunk:chunk + 1],
                            e16[:, :]), deps=[r_e1, r_prev])
                    r_ei = P.step("vector", lambda e: e.tensor_mul(
                        ltm[:, :], e16[:, :], i16_sb[:, :]),
                        deps=[r_e1, r_cst[4]])
                    r_eid = P.step("vector", lambda e, chunk=chunk:
                        e.reduce_max(
                            eid_all[:, chunk:chunk + 1], ltm[:, :],
                            axis=AX.X), deps=[r_ei])
                    r_sx = P.step("scalar", lambda e: e.activation(
                        ssc[:, :], vc1[:, :], AF.Exp, scale=-1.0),
                        deps=[r_v1, r_tmprev])
                    r_s1 = P.step("vector", lambda e: e.tensor_scalar(
                        ssc[:, :], ssc[:, :], 1.0, None, ALU.add),
                        deps=[r_sx])
                    r_s2 = P.step("vector", lambda e: e.reciprocal(
                        ssc[:, :], ssc[:, :]), deps=[r_s1])
                    r_trs = []
                    for dc in range(8):
                        r_trs.append(P.step("tensor",
                            lambda e, dc=dc, sub=sub: e.transpose(
                                ps_tb[:, dc * 128:(dc + 1) * 128],
                                xbt[:, dc, sub * 128:(sub + 1) * 128],
                                identb[:, :]),
                            deps=[r_xl, r_idb, r_tmprev] +
                                 ([r_t, r_prev] if dc == 0 else [])))
                    r_ev = P.step("vector", lambda e: e.tensor_scalar(
                        tmst[:, :, :],
                        ps_tb[:, :].rearrange("p (c f) -> p c f", c=8),
                        ssc[:, 0:1], None, ALU.mult),
                        deps=r_trs + [r_s2])
                    r_tmprev = P.step("sync", lambda e, chunk=chunk:
                        e.dma_start(
                            out=x3tm[chunk * 128:(chunk + 1) * 128, :],
                            in_=tmst[:, :, :].rearrange("p c f -> p (c f)")),
                        deps=[r_ev], dma=True)
                    r_prev = r_ev
        r_router = [r_prev, r_tmprev]

        with ExitStack() as st_s:
            es = st_s.enter_context
            cmb = es(nc.sbuf_tensor(uname("cmbD"), [128, 16], BF16))
            sl16 = es(nc.sbuf_tensor(uname("sl16D"), [128, 16], F32))
            t16 = es(nc.sbuf_tensor(uname("t16D"), [128, 16], F32))
            mslc = es(nc.sbuf_tensor(uname("mslD"), [128, 1], F32))
            gcol = es(nc.sbuf_tensor(uname("gcolD"), [128, 1], F32))
            smc = es(nc.sbuf_tensor(uname("smcD"), [128, 1], F32))
            tdc = es(nc.sbuf_tensor(uname("tdcD"), [128, 1], F32))
            rphy = es(nc.sbuf_tensor(uname("rphyD"), [128, 1], F32))
            tokc = es(nc.sbuf_tensor(uname("tokcD"), [128, 1], F32))
            ri = es(nc.sbuf_tensor(uname("riD"), [128, 16], I32))
            ti = es(nc.sbuf_tensor(uname("tiD"), [128, 16], I32))
            basr = es(nc.sbuf_tensor(uname("basrD"), [1, 16], F32))
            basm1 = es(nc.sbuf_tensor(uname("basm1D"), [1, 16], F32))
            totr = es(nc.sbuf_tensor(uname("totrD"), [1, 16], F32))
            ps_pf = es(nc.psum_tensor(uname("ps_pfD"), [128, 16], F32))
            ps_tt = es(nc.psum_tensor(uname("ps_ttD"), [1, 16], F32))
            ps_bb = es(nc.psum_tensor(uname("ps_bbD"), [128, 16], F32))
            bb16 = es(nc.sbuf_tensor(uname("bb16D"), [128, 16], F32))
            r_bas = P.step("vector", lambda e: e.memset(basr[:, :], 0.0))
            r_sc = None
            r_pfuse = None
            r_ttuse = None
            r_ttlast = None
            for chunk in range(16):
                r_cmb = P.step("vector", lambda e, chunk=chunk:
                    e.tensor_copy(cmb[:, :],
                                  cm_all[:, 0:16, chunk:chunk + 1]),
                    deps=r_router + [r_sc, r_ttlast])
                r_pf = None
                for ex in range(16):
                    r_pf = P.step("tensor", lambda e, ex=ex: e.matmul(
                        ps_pf[:, ex:ex + 1], utb[:, :], cmb[:, ex:ex + 1],
                        start=True, stop=True),
                        deps=[r_cmb, r_cst[2], r_pfuse])
                r_tt = None
                for ex in range(16):
                    r_tt = P.step("tensor", lambda e, ex=ex: e.matmul(
                        ps_tt[0:1, ex:ex + 1], o128b[:, :],
                        cmb[:, ex:ex + 1],
                        start=True, stop=True),
                        deps=[r_cmb, r_o128, r_ttuse])
                r_ttlast = r_tt
                r_bm1 = P.step("vector", lambda e: e.tensor_scalar(
                    basm1[:, :], basr[:, :], 1.0, None, ALU.subtract),
                    deps=[r_bas])
                r_bbm = P.step("tensor", lambda e: e.matmul(
                    ps_bb[:, :], onesf1[:, :], basm1[:, :],
                    start=True, stop=True), deps=[r_bm1, r_of, r_pfuse])
                r_bbc = P.step("vector", lambda e: e.tensor_copy(
                    bb16[:, :], ps_bb[:, :]), deps=[r_bbm])
                r_sl = [P.step("vector", lambda e: e.tensor_add(
                    sl16[:, :], bb16[:, :], ps_pf[:, :]),
                    deps=[r_pf, r_bbc])]
                r_pfuse = r_sl[-1]
                r_tr = P.step("vector", lambda e: e.tensor_copy(
                    totr[:, :], ps_tt[0:1, :]), deps=[r_tt])
                r_ttuse = r_tr
                r_bas = P.step("vector", lambda e: e.tensor_add(
                    basr[:, :], basr[:, :], totr[:, :]),
                    deps=[r_tr, r_bm1])
                r_t1 = P.step("vector", lambda e, chunk=chunk: e.tensor_mul(
                    t16[:, :], cm_all[:, 0:16, chunk:chunk + 1], sl16[:, :]),
                    deps=r_sl)
                r_msl = P.step("vector", lambda e: e.reduce_sum(
                    mslc[:, :], t16[:, :], axis=AX.X), deps=[r_t1])
                r_g = P.step("vector", lambda e: e.tensor_scalar(
                    gcol[:, :], mslc[:, :], float(NSLOT), None, ALU.is_lt),
                    deps=[r_msl])
                r_q1 = P.step("vector", lambda e: e.tensor_scalar(
                    smc[:, :], mslc[:, :], 128.0, None, ALU.is_ge),
                    deps=[r_msl])
                r_q2 = P.step("vector", lambda e: e.tensor_scalar(
                    tdc[:, :], mslc[:, :], 256.0, None, ALU.is_ge),
                    deps=[r_msl])
                r_q3 = P.step("vector", lambda e: e.tensor_add(
                    smc[:, :], smc[:, :], tdc[:, :]), deps=[r_q1, r_q2])
                r_q4 = P.step("vector", lambda e: e.tensor_scalar(
                    tdc[:, :], mslc[:, :], 384.0, None, ALU.is_ge),
                    deps=[r_msl, r_q3])
                r_q5 = P.step("vector", lambda e: e.tensor_add(
                    smc[:, :], smc[:, :], tdc[:, :]), deps=[r_q3, r_q4])
                r_qm = P.step("vector", lambda e: e.tensor_scalar(
                    smc[:, :], smc[:, :], -511.0, None, ALU.mult),
                    deps=[r_q5])
                r_rp = P.step("vector", lambda e: e.scalar_tensor_tensor(
                    rphy[:, :], mslc[:, :], 4.0, smc[:, :], ALU.mult,
                    ALU.add), deps=[r_qm])
                r_rp2 = P.step("vector", lambda e, chunk=chunk:
                    e.scalar_tensor_tensor(
                        rphy[:, :], eid_all[:, chunk:chunk + 1],
                        float(NSLOT), rphy[:, :], ALU.mult, ALU.add),
                    deps=[r_rp])
                r_rm = P.step("vector", lambda e: e.tensor_scalar(
                    rphy[:, :], rphy[:, :], 131072.0, None, ALU.subtract),
                    deps=[r_rp2])
                r_rf = P.step("vector", lambda e: e.scalar_tensor_tensor(
                    smc[:, :], gcol[:, :], rphy[:, 0:1], bigc[:, :],
                    ALU.mult, ALU.add), deps=[r_rm, r_g, r_big])
                r_tk = P.step("vector", lambda e, chunk=chunk:
                    e.tensor_scalar(
                        tokc[:, :], iota_sb[:, :], float(128 * chunk),
                        None, ALU.add), deps=[r_cst[3], r_sc])
                r_i1 = P.step("vector", lambda e, chunk=chunk:
                    e.tensor_copy(
                        ri[:, chunk:chunk + 1], smc[:, :]), deps=[r_rf])
                r_i2 = P.step("vector", lambda e, chunk=chunk:
                    e.tensor_copy(
                        ti[:, chunk:chunk + 1], tokc[:, :]), deps=[r_tk])
                r_sc = P.step("gpsimd", lambda e, chunk=chunk:
                    e.indirect_dma_start(
                        idxd[0:128, :],
                        IndirectOffsetOnAxis(ri[:, chunk:chunk + 1], 0),
                        ti[:, chunk:chunk + 1], None,
                        bounds_check=_bcreg(e, 16 * NSLOT),
                        oob_is_err=False),
                    deps=[r_i1, r_i2, r_idxinit], dma=True)
            r_idx_done = P.step("gpsimd", lambda e: e.dma_start(
                out=flb[:, :], in_=ibig[:, 0:4]), deps=[r_sc], dma=True)

        st_e = ExitStack()
        ee = st_e.enter_context
        wgA = ee(nc.sbuf_tensor(uname("wgAD"), [128, 8, 2048], BF16))
        wgB = ee(nc.sbuf_tensor(uname("wgBD"), [128, 8, 2048], BF16))
        wd_sb = ee(nc.sbuf_tensor(uname("wdD"), [128, 16, D], BF16))
        mxin = [ee(nc.sbuf_tensor(uname("mxinD"), [128, D], BF16))
                for _ in range(4)]
        mxout = [ee(nc.sbuf_tensor(uname("mxoutD"), [128, D], BF16))
                 for _ in range(4)]
        mxfm = ee(nc.sbuf_tensor(uname("mxfmD"), [128, 8, TSZ], BF16))
        gsb = ee(nc.sbuf_tensor(uname("gsbD"), [128, 2, TSZ], BF16))
        usb = ee(nc.sbuf_tensor(uname("usbD"), [128, 16, TSZ], BF16))
        ofm = ee(nc.sbuf_tensor(uname("ofmD"), [128, 8, TSZ], BF16))
        idx_all = ee(nc.sbuf_tensor(uname("idxaD"), [128, 64], I32))
        flsh = ee(nc.sbuf_tensor(uname("flshD"), [128, 4], I32))
        ps_g = [ee(nc.psum_tensor(uname("ps_gD"), [128, TSZ], F32))
                for _ in range(2)]
        ps_u = [ee(nc.psum_tensor(uname("ps_uD"), [128, TSZ], F32))
                for _ in range(2)]
        ps_d = [ee(nc.psum_tensor(uname("ps_dD"), [128, TSZ], F32))
                for _ in range(2)]
        ps_tb2 = ee(nc.psum_tensor(uname("ps_tb2D"), [128, 1024], BF16))
        ps_tb3 = ee(nc.psum_tensor(uname("ps_tb3D"), [128, 1024], BF16))

        def ld_wgA(ex):
            return lambda e: e.dma_start(
                out=wgA[:, :, :],
                in_=gup[ex, :, 0:2048].rearrange("(c p) f -> p c f", p=128))

        def ld_wgB(ex):
            return lambda e: e.dma_start(
                out=wgB[:, :, :],
                in_=gup[ex, :, 2048:4096].rearrange("(c p) f -> p c f",
                                                    p=128))

        def ld_wd(ex):
            return lambda e: e.dma_start(
                out=wd_sb[:, :, :],
                in_=dwn[ex, :, :].rearrange("(c p) o -> p c o", p=128))

        r_wgA = P.step("gpsimd", ld_wgA(0), deps=_bar, dma=True)
        r_wgB = P.step("gpsimd", ld_wgB(0), deps=_bar, dma=True)
        r_wd = P.step("gpsimd", ld_wd(0), deps=_bar, dma=True)
        r_ia = P.step("sync", lambda e: e.dma_start(
            out=idx_all[:, :].rearrange("p (x c) -> p x c", x=16),
            in_=idxd.rearrange("(x p c) o -> p x (c o)", x=16, p=128)),
            deps=[r_idx_done], dma=True)
        r_mz = [P.step("vector", lambda e, j=j: e.memset(
            mxin[j][:, :], 0.0), deps=_bar) for j in range(4)]

        def emit_gathers(ex, deps):
            r_g = []
            for j in range(4):
                r_g.append(P.step("gpsimd", lambda e, ex=ex, j=j:
                    e.indirect_dma_start(
                        mxin[j][:, :], None, x3tm[0:128, :],
                        IndirectOffsetOnAxis(
                            idx_all[:, ex * 4 + j:ex * 4 + j + 1], 0),
                        bounds_check=_bcreg(e, TCORE), oob_is_err=False),
                    deps=deps + [r_ia], dma=True))
            r_f = P.step("gpsimd", lambda e: e.dma_start(
                out=flsh[:, :], in_=idx_all[:, 0:4]), deps=[r_g[3]],
                dma=True)
            return r_g, r_f

        r_gth, r_flush = emit_gathers(0, r_mz)
        r_tfm_prev = []
        r_up_last_prev = None
        r_silu = [None, None]
        r_umul = [None, None]
        r_duse = [None, None]
        r_tt2_prev = []
        r_sct_prev = []
        r_ottail = list(r_otz)
        r_sct = []
        for ex in range(16):
            if DBG_LOOPBACK:
                r_sct = []
                for j in range(4):
                    r_sct.append(P.step("gpsimd", lambda e, ex=ex, j=j:
                        e.indirect_dma_start(
                            otm[0:128, :], IndirectOffsetOnAxis(
                                idx_all[:, ex * 4 + j:ex * 4 + j + 1], 0),
                            mxin[j][:, :], None,
                            bounds_check=_bcreg(e, TCORE),
                            oob_is_err=False),
                        deps=[r_gth[j]] + r_ottail, dma=True))
                r_ottail = []
                if ex < 15:
                    r_gth, r_flush = emit_gathers(ex + 1, r_sct)
                continue
            r_tfm = []
            for j in range(4):
                r_trs = []
                for dc in range(8):
                    r_trs.append(P.step("tensor",
                        lambda e, j=j, dc=dc: e.transpose(
                            ps_tb2[:, dc * 128:(dc + 1) * 128],
                            mxin[j][:, dc * 128:(dc + 1) * 128],
                            identb[:, :]),
                        deps=[r_flush, r_idb] +
                             ([r_tfm[-1]] if (dc == 0 and r_tfm) else
                              ([r_tfm_prev[-1]] if (dc == 0 and r_tfm_prev)
                               else []))))
                r_tfm.append(P.step("vector", lambda e, j=j: e.tensor_copy(
                    mxfm[:, :, j * 128:(j + 1) * 128],
                    ps_tb2[:, :].rearrange("p (c f) -> p c f", c=8)),
                    deps=r_trs + [r_up_last_prev]))
            if ex < 15:
                r_gth, r_flush = emit_gathers(ex + 1, list(r_tfm))
            r_gate_last = None
            r_up_last = None
            for fc in range(16):
                pb = fc % 2
                r_mm = None
                for dc in range(8):
                    r_mm = P.step("tensor",
                        lambda e, fc=fc, dc=dc, pb=pb: e.matmul(
                            ps_g[pb][:, :],
                            wgA[:, dc, fc * 128:(fc + 1) * 128],
                            mxfm[:, dc, :], start=(dc == 0),
                            stop=(dc == 7)),
                        deps=[r_wgA, r_tfm, r_silu[pb]]
                        if dc == 0 else [r_wgA])
                r_gate_last = r_mm
                r_silu[pb] = P.step("scalar", lambda e, fc=fc, pb=pb:
                    e.activation(gsb[:, pb, :], ps_g[pb][:, :], AF.Silu),
                    deps=[r_mm, r_umul[pb]])
                r_mm = None
                for dc in range(8):
                    r_mm = P.step("tensor",
                        lambda e, fc=fc, dc=dc, pb=pb: e.matmul(
                            ps_u[pb][:, :],
                            wgB[:, dc, fc * 128:(fc + 1) * 128],
                            mxfm[:, dc, :], start=(dc == 0),
                            stop=(dc == 7)),
                        deps=[r_wgB, r_tfm, r_umul[pb]]
                        if dc == 0 else [r_wgB])
                r_up_last = r_mm
                r_umul[pb] = P.step("vector", lambda e, fc=fc, pb=pb:
                    e.tensor_mul(usb[:, fc, :], gsb[:, pb, :],
                                 ps_u[pb][:, :]),
                    deps=[r_mm, r_silu[pb]])
            r_up_last_prev = r_up_last
            r_wgA = P.step("gpsimd", ld_wgA(ex + 1), deps=[r_gate_last],
                           dma=True)
            r_wgB = P.step("gpsimd", ld_wgB(ex + 1), deps=[r_up_last],
                           dma=True)
            r_osb = []
            r_down_last = None
            for oc in range(8):
                pb = oc % 2
                r_mm = None
                for fc in range(16):
                    r_mm = P.step("tensor",
                        lambda e, oc=oc, fc=fc, pb=pb: e.matmul(
                            ps_d[pb][:, :],
                            wd_sb[:, fc, oc * 128:(oc + 1) * 128],
                            usb[:, fc, :], start=(fc == 0),
                            stop=(fc == 15)),
                        deps=[r_wd, r_umul[0], r_umul[1], r_duse[pb]]
                        if fc == 0 else [r_wd])
                r_down_last = r_mm
                r_o = P.step("vector", lambda e, oc=oc, pb=pb:
                    e.tensor_copy(ofm[:, oc, :], ps_d[pb][:, :]),
                    deps=[r_mm])
                r_duse[pb] = r_o
                r_osb.append(r_o)
            r_wd = P.step("gpsimd", ld_wd(ex + 1), deps=[r_down_last],
                          dma=True)
            r_sct = []
            r_tt2s = []
            for j in range(4):
                r_trs = []
                for dc in range(8):
                    r_trs.append(P.step("tensor",
                        lambda e, j=j, dc=dc: e.transpose(
                            ps_tb3[:, dc * 128:(dc + 1) * 128],
                            ofm[:, dc, j * 128:(j + 1) * 128],
                            identb[:, :]),
                        deps=r_osb + [r_idb] +
                             ([r_tt2s[-1]] if (dc == 0 and r_tt2s) else
                              ([r_tt2_prev[-1]]
                               if (dc == 0 and r_tt2_prev) else []))))
                r_tt2 = P.step("vector", lambda e, j=j: e.tensor_copy(
                    mxout[j][:, :],
                    ps_tb3[:, :].rearrange("p (c f) -> p c f", c=8)),
                    deps=r_trs +
                         ([r_sct_prev[j]] if r_sct_prev else []))
                r_tt2s.append(r_tt2)
                r_sct.append(P.step("gpsimd", lambda e, ex=ex, j=j:
                    e.indirect_dma_start(
                        otm[0:128, :], IndirectOffsetOnAxis(
                            idx_all[:, ex * 4 + j:ex * 4 + j + 1], 0),
                        mxout[j][:, :], None,
                        bounds_check=_bcreg(e, TCORE), oob_is_err=False),
                    deps=[r_tt2] + r_ottail, dma=True))
            r_ottail = []
            r_tt2_prev = r_tt2s
            r_sct_prev = r_sct
            r_tfm_prev = r_tfm
        r_sparse_done = P.step("gpsimd", lambda e: e.dma_start(
            out=flb[:, :], in_=ibig[:, 0:4]), deps=r_sct, dma=True)

        x3b = ee(nc.sbuf_tensor(uname("x3bD"), [128, 8, TSZ], BF16))
        r_accw = {}
        r_x3use = None
        for ot in range(4):
            r_xb = P.step("gpsimd", lambda e, ot=ot: e.dma_start(
                out=x3b[:, :, :], in_=x33[:, :, ot * TSZ:(ot + 1) * TSZ]),
                deps=[r_x3use], dma=True)
            r_up_last = None
            for fc in range(16):
                pb = fc % 2
                r_mm = None
                for dc in range(8):
                    r_mm = P.step("tensor",
                        lambda e, fc=fc, dc=dc, pb=pb: e.matmul(
                            ps_g[pb][:, :],
                            wgA[:, dc, fc * 128:(fc + 1) * 128],
                            x3b[:, dc, :], start=(dc == 0), stop=(dc == 7)),
                        deps=[r_wgA, r_xb, r_silu[pb]]
                        if dc == 0 else [r_wgA])
                r_silu[pb] = P.step("scalar", lambda e, fc=fc, pb=pb:
                    e.activation(gsb[:, pb, :], ps_g[pb][:, :], AF.Silu),
                    deps=[r_mm, r_umul[pb]])
                r_mm = None
                for dc in range(8):
                    r_mm = P.step("tensor",
                        lambda e, fc=fc, dc=dc, pb=pb: e.matmul(
                            ps_u[pb][:, :],
                            wgB[:, dc, fc * 128:(fc + 1) * 128],
                            x3b[:, dc, :], start=(dc == 0), stop=(dc == 7)),
                        deps=[r_wgB, r_xb, r_umul[pb]]
                        if dc == 0 else [r_wgB])
                r_up_last = r_mm
                r_umul[pb] = P.step("vector", lambda e, fc=fc, pb=pb:
                    e.tensor_mul(usb[:, fc, :], gsb[:, pb, :],
                                 ps_u[pb][:, :]),
                    deps=[r_mm, r_silu[pb]])
            for oc in range(8):
                pb = oc % 2
                r_mm = None
                for fc in range(16):
                    r_mm = P.step("tensor",
                        lambda e, oc=oc, fc=fc, pb=pb: e.matmul(
                            ps_d[pb][:, :],
                            wd_sb[:, fc, oc * 128:(oc + 1) * 128],
                            usb[:, fc, :], start=(fc == 0),
                            stop=(fc == 15)),
                        deps=[r_wd, r_umul[0], r_umul[1], r_duse[pb]]
                        if fc == 0 else [r_wd])
                r_duse[pb] = P.step("vector", lambda e, oc=oc, ot=ot, pb=pb:
                    e.tensor_copy(
                        sumacc[:, oc, ot * TSZ:(ot + 1) * TSZ],
                        ps_d[pb][:, :]), deps=[r_mm])
            r_accw[ot] = [r_duse[0], r_duse[1]]
            r_x3use = r_up_last

        r_mrg = {}
        for sub in range(16):
            ot = sub // 4
            r_ld = P.step("sync", lambda e, sub=sub: e.dma_start(
                out=mxin[sub % 4][:, :],
                in_=otm[sub * 128:(sub + 1) * 128, :]),
                deps=[r_sparse_done, r_accw.get(ot), r_mrg.get(sub - 4),
                      r_tfm_prev],
                dma=True)
            r_trs = []
            for dc in range(8):
                r_trs.append(P.step("tensor",
                    lambda e, sub=sub, dc=dc: e.transpose(
                        ps_tb2[:, dc * 128:(dc + 1) * 128],
                        mxin[sub % 4][:, dc * 128:(dc + 1) * 128],
                        identb[:, :]),
                    deps=[r_ld, r_mrg.get(sub - 1)] +
                         (r_tfm_prev if sub == 0 else [])))
            r_mrg[sub] = P.step("vector", lambda e, sub=sub: e.tensor_add(
                sumacc[:, :, sub * 128:(sub + 1) * 128],
                sumacc[:, :, sub * 128:(sub + 1) * 128],
                ps_tb2[:, :].rearrange("p (c f) -> p c f", c=8)),
                deps=r_trs + [r_accw.get(sub // 4)])

        if DBG_IDX and _own:
            idxout = dram(_ctx, "idxout", [16 * NSLOT, 1], I32,
                          "ExternalOutput")
            otmout = dram(_ctx, "otmout", [TCORE, D], BF16, "ExternalOutput")
            P.step("sync", lambda e: e.dma_start(
                out=idxout[:, :], in_=idxd[:, :]),
                deps=[r_idx_done], dma=True)
            P.step("sync", lambda e: e.dma_start(
                out=otmout[:, :], in_=otm[:, :]),
                deps=[r_sparse_done] + [r_mrg[s] for s in range(16)],
                dma=True)
            x3tmout = dram(_ctx, "x3tmout", [TCORE, D], BF16,
                           "ExternalOutput")
            P.step("sync", lambda e: e.dma_start(
                out=x3tmout[:, :], in_=x3tm[:, :]),
                deps=[r_idx_done], dma=True)
        st_e.close()
        e_ = st.enter_context

        with ExitStack() as st_f:
            ef = st_f.enter_context
            tt = ef(nc.sbuf_tensor(uname("ttD"), [128, 8, TSZ], F32))
            lc = make_ln_ctx(nc, ef, "dl")
            ln_ctx_init(nc, P, lc)
            r_tu = None
            for ot in range(4):
                mdeps = [r_mrg[4 * ot + k] for k in range(4)]
                r_x3 = P.step("sync", lambda e, ot=ot: e.dma_start(
                    out=tt[:, :, :], in_=x33[:, :, ot * TSZ:(ot + 1) * TSZ]),
                    deps=mdeps + [r_tu] + (now_all(P) if ot == 0 else []),
                    dma=True)
                r_ad = []
                for oc in range(8):
                    r_ad.append(P.step("vector",
                        lambda e, oc=oc, ot=ot: e.tensor_add(
                            tt[:, oc, :], tt[:, oc, :],
                            sumacc[:, oc, ot * TSZ:(ot + 1) * TSZ]),
                        deps=[r_x3] + mdeps))
                r_l2 = emit_ln_tile(nc, P, lc, tt, tt, r_ad)
                r_tu = P.step("sync", lambda e, ot=ot: e.dma_start(
                    out=out3[:, :, ot * TSZ:(ot + 1) * TSZ], in_=tt[:, :, :]),
                    deps=r_l2, dma=True)

        if _own:
            sems = {e: e_(nc.semaphore(uname("sem_" + e)))
                    for e in ENGINES}
            block = e_(nc.Block())
            P.finalize(block, sems)
    return nc


def build_stageD_dense(fused=None):
    nc, P, _cache, _own = _begin(fused)
    _ctx = (nc, _cache, fused is not None)
    x3my = dram(_ctx, "x3my", [D, TCORE], F32, "ExternalInput")
    wrT = dram(_ctx, "wrT", [D, 16], F32, "ExternalInput")
    gup = dram(_ctx, "gup", [17, D, 4096], F32, "ExternalInput")
    dwn = dram(_ctx, "dwn", [17, 2048, D], F32, "ExternalInput")
    ident = dram(_ctx, "identD", [128, 128], F32, "ExternalInput")
    x4my = dram(_ctx, "x4my", [D, TCORE], F32, "ExternalOutput")

    x33 = fm3(x3my)
    out3 = fm3(x4my)

    with ExitStack() as st:
        e_ = st.enter_context
        wr_sb = e_(nc.sbuf_tensor(uname("wr_sb"), [128, 8, 16], BF16))
        ident_sb = e_(nc.sbuf_tensor(uname("identD_sb"), [128, 128], F32))
        onesf1 = e_(nc.sbuf_tensor(uname("onesfD"), [1, 128], F32))
        cm_all = e_(nc.sbuf_tensor(uname("cmD"), [128, 17, 16], F32))
        sumacc = e_(nc.sbuf_tensor(uname("sumacc"), [128, 8, TCORE], BF16))
        _bar = now_all(P) if not _own else []
        r_cst = [
            P.step("gpsimd", lambda e: e.dma_start(
                out=wr_sb[:, :, :],
                in_=wrT.rearrange("(c p) o -> p c o", p=128)),
                deps=_bar, dma=True),
            P.step("sync", lambda e: e.dma_start(
                out=ident_sb[:, :], in_=ident[:, :]), deps=_bar, dma=True),
        ]
        r_of = P.step("vector", lambda e: e.memset(onesf1[:, :], 1.0))
        r_cm16 = P.step("vector", lambda e: e.memset(cm_all[:, 16, :], 1.0))

        with ExitStack() as st_r:
            er = st_r.enter_context
            xbt = er(nc.sbuf_tensor(uname("xbtD"), [128, 8, TSZ], BF16))
            lsb = er(nc.sbuf_tensor(uname("lsbD"), [16, TSZ], F32))
            ltm = er(nc.sbuf_tensor(uname("ltmD"), [128, 16], F32))
            e16 = er(nc.sbuf_tensor(uname("e16D"), [128, 16], F32))
            vc1 = er(nc.sbuf_tensor(uname("vc1D"), [128, 1], F32))
            ssc = er(nc.sbuf_tensor(uname("sscD"), [128, 1], F32))
            ps_l = er(nc.psum_tensor(uname("ps_lD"), [16, TSZ], F32))
            ps_t = er(nc.psum_tensor(uname("ps_tD"), [128, TSZ], F32))
            r_prev = None
            for ot in range(4):
                r_xl = P.step("gpsimd", lambda e, ot=ot: e.dma_start(
                    out=xbt[:, :, :], in_=x33[:, :, ot * TSZ:(ot + 1) * TSZ]),
                    deps=[r_prev], dma=True)
                r_mm = None
                for dc in range(8):
                    r_mm = P.step("tensor", lambda e, dc=dc: e.matmul(
                        ps_l[:, :], wr_sb[:, dc, :], xbt[:, dc, :],
                        start=(dc == 0), stop=(dc == 7)),
                        deps=[r_xl, r_cst[0], r_prev])
                r_lb = P.step("vector", lambda e: e.tensor_copy(
                    lsb[:, :], ps_l[0:16, :]), deps=[r_mm])
                for sub in range(4):
                    r_t = P.step("tensor", lambda e, sub=sub: e.transpose(
                        ps_t[:, 0:16], lsb[0:16, sub * 128:(sub + 1) * 128],
                        ident_sb[0:16, 0:16]), deps=[r_lb, r_cst[1], r_prev])
                    r_c = P.step("vector", lambda e: e.tensor_copy(
                        ltm[:, :], ps_t[:, 0:16]), deps=[r_t])
                    r_v1 = P.step("vector", lambda e: e.reduce_max(
                        vc1[:, :], ltm[:, :], axis=AX.X), deps=[r_c])
                    r_e1 = P.step("vector", lambda e: e.tensor_scalar(
                        e16[:, :], ltm[:, :], vc1[:, 0:1], None,
                        ALU.is_equal), deps=[r_v1])
                    r_sx = P.step("scalar", lambda e: e.activation(
                        ssc[:, :], vc1[:, :], AF.Exp, scale=-1.0),
                        deps=[r_v1])
                    r_s1 = P.step("vector", lambda e: e.tensor_scalar(
                        ssc[:, :], ssc[:, :], 1.0, None, ALU.add),
                        deps=[r_sx])
                    r_s2 = P.step("vector", lambda e: e.reciprocal(
                        ssc[:, :], ssc[:, :]), deps=[r_s1])
                    r_last = None
                    for ex in range(16):
                        r_last = P.step("vector",
                            lambda e, ex=ex, ot=ot, sub=sub:
                            e.scalar_tensor_tensor(
                                cm_all[:, ex, ot * 4 + sub:ot * 4 + sub + 1],
                                e16[:, ex:ex + 1], 1.0, ssc[:, :],
                                ALU.mult, ALU.mult),
                            deps=[r_e1, r_s2, r_last])
                    r_prev = r_last
        r_router = r_prev

        st_e = ExitStack()
        ee = st_e.enter_context
        wg_sb = ee(nc.sbuf_tensor(uname("wg_sbD"), [128, 8, 4096], BF16))
        wd_sb = ee(nc.sbuf_tensor(uname("wd_sbD"), [128, 16, D], BF16))
        mxb = ee(nc.sbuf_tensor(uname("mxbD"), [128, 8, TSZ], BF16))
        x3b = ee(nc.sbuf_tensor(uname("x3bD"), [128, 8, TSZ], BF16))
        gsb = ee(nc.sbuf_tensor(uname("gsbD"), [128, 16, TSZ], BF16))
        usb = ee(nc.sbuf_tensor(uname("usbD"), [128, 16, TSZ], BF16))
        mke = ee(nc.sbuf_tensor(uname("mkeD"), [128, TSZ], BF16))
        mre = ee(nc.sbuf_tensor(uname("mreD"), [1, TSZ], F32))
        ps_g = [ee(nc.psum_tensor(uname("ps_gD"), [128, TSZ], F32))
                for _ in range(2)]
        ps_d = [ee(nc.psum_tensor(uname("ps_dD"), [128, TSZ], F32))
                for _ in range(2)]
        ps_m = ee(nc.psum_tensor(uname("ps_mD"), [128, TSZ], F32))

        r_wuse = [r_router]
        r_xuse = None
        r_mxuse = None
        r_guse = None
        r_uuse = None
        r_accw = {ot: None for ot in range(4)}
        for ex in range(17):
            r_w1 = P.step("gpsimd", lambda e, ex=ex: e.dma_start(
                out=wg_sb[:, :, :],
                in_=gup[ex, :, :].rearrange("(c p) f -> p c f", p=128)),
                deps=r_wuse, dma=True)
            r_w2 = P.step("gpsimd", lambda e, ex=ex: e.dma_start(
                out=wd_sb[:, :, :],
                in_=dwn[ex, :, :].rearrange("(c p) o -> p c o", p=128)),
                deps=r_wuse, dma=True)
            r_wuse = []
            for ot in range(4):
                r_xb = P.step("gpsimd", lambda e, ot=ot: e.dma_start(
                    out=x3b[:, :, :], in_=x33[:, :, ot * TSZ:(ot + 1) * TSZ]),
                    deps=[r_xuse], dma=True)
                r_trs = []
                for sub in range(4):
                    r_tt = P.step("tensor", lambda e, ex=ex, ot=ot, sub=sub:
                        e.transpose(
                            ps_m[0:1, sub * 128:sub * 128 + 128],
                            cm_all[:, ex, ot * 4 + sub:ot * 4 + sub + 1],
                            ident_sb[:, :]),
                        deps=[r_router, r_cm16, r_cst[1], r_mxuse])
                    r_trs.append(r_tt)
                r_mr = P.step("vector", lambda e: e.tensor_copy(
                    mre[0:1, :], ps_m[0:1, :]), deps=r_trs)
                r_mk1 = P.step("tensor", lambda e: e.matmul(
                    ps_m[:, :], onesf1[:, :], mre[0:1, :],
                    start=True, stop=True), deps=[r_mr, r_of])
                r_mk2 = P.step("vector", lambda e: e.tensor_copy(
                    mke[:, :], ps_m[:, :]), deps=[r_mk1])
                r_mx = []
                for dc in range(8):
                    r_mx.append(P.step("vector",
                        lambda e, dc=dc: e.tensor_mul(
                            mxb[:, dc, :], x3b[:, dc, :], mke[:, :]),
                        deps=[r_mk2, r_xb, r_mxuse]))
                r_xuse = r_mx[-1]
                r_pf = {}
                r_ev = [None, None]
                for fc in range(16):
                    pb = fc % 2
                    r_mm = None
                    for dc in range(8):
                        r_mm = P.step("tensor",
                            lambda e, fc=fc, dc=dc, pb=pb: e.matmul(
                                ps_g[pb][:, :],
                                wg_sb[:, dc, fc * 128:(fc + 1) * 128],
                                mxb[:, dc, :], start=(dc == 0),
                                stop=(dc == 7)),
                            deps=[r_w1, r_mx[dc], r_ev[pb]]
                            if dc == 0 else [r_w1, r_mx[dc]])
                    r_ev[pb] = P.step("scalar", lambda e, fc=fc, pb=pb:
                        e.activation(
                            gsb[:, fc, :], ps_g[pb][:, :], AF.Silu),
                        deps=[r_mm, r_guse])
                for fc in range(16):
                    pb = fc % 2
                    r_mm = None
                    for dc in range(8):
                        r_mm = P.step("tensor",
                            lambda e, fc=fc, dc=dc, pb=pb: e.matmul(
                                ps_g[pb][:, :],
                                wg_sb[:, dc, 2048 + fc * 128:
                                      2048 + (fc + 1) * 128],
                                mxb[:, dc, :], start=(dc == 0),
                                stop=(dc == 7)),
                            deps=[r_w1, r_mx[dc], r_ev[pb]]
                            if dc == 0 else [r_w1, r_mx[dc]])
                    r_ev[pb] = P.step("vector", lambda e, fc=fc, pb=pb:
                        e.tensor_copy(
                            usb[:, fc, :], ps_g[pb][:, :]),
                        deps=[r_mm, r_uuse])
                r_ev = r_ev[0] if r_ev[1] is None else r_ev[1]
                r_pfs = []
                for fc in range(16):
                    r_pfs.append(P.step("vector",
                        lambda e, fc=fc: e.tensor_mul(
                            usb[:, fc, :], usb[:, fc, :], gsb[:, fc, :]),
                        deps=[r_ev]))
                r_guse = r_pfs[-1]
                r_uuse = r_pfs[-1]
                r_do = [None, None]
                for oc in range(8):
                    pb = oc % 2
                    r_mm = None
                    for fc in range(16):
                        r_mm = P.step("tensor",
                            lambda e, oc=oc, fc=fc, pb=pb: e.matmul(
                                ps_d[pb][:, :],
                                wd_sb[:, fc, oc * 128:(oc + 1) * 128],
                                usb[:, fc, :], start=(fc == 0),
                                stop=(fc == 15)),
                            deps=[r_w2, r_pfs[fc], r_do[pb]]
                            if fc == 0 else [r_w2, r_pfs[fc]])
                    if ex == 0:
                        r_do[pb] = P.step("vector",
                            lambda e, oc=oc, ot=ot, pb=pb: e.tensor_copy(
                                sumacc[:, oc, ot * TSZ:(ot + 1) * TSZ],
                                ps_d[pb][:, :]), deps=[r_mm, r_accw[ot]])
                    else:
                        r_do[pb] = P.step("vector",
                            lambda e, oc=oc, ot=ot, pb=pb: e.tensor_add(
                                sumacc[:, oc, ot * TSZ:(ot + 1) * TSZ],
                                sumacc[:, oc, ot * TSZ:(ot + 1) * TSZ],
                                ps_d[pb][:, :]), deps=[r_mm, r_accw[ot]])
                r_accw[ot] = list(r_do)
                r_mxuse = list(r_do)
            r_wuse = [r_accw[ot] for ot in range(4)]
        st_e.close()
        e_ = st.enter_context

        with ExitStack() as st_f:
            ef = st_f.enter_context
            tt = ef(nc.sbuf_tensor(uname("ttD"), [128, 8, TSZ], F32))
            lc = make_ln_ctx(nc, ef, "dl")
            ln_ctx_init(nc, P, lc)
            r_tu = None
            for ot in range(4):
                r_x3 = P.step("sync", lambda e, ot=ot: e.dma_start(
                    out=tt[:, :, :], in_=x33[:, :, ot * TSZ:(ot + 1) * TSZ]),
                    deps=[r_accw[ot], r_tu] +
                         (now_all(P) if ot == 0 else []), dma=True)
                r_ad = []
                for oc in range(8):
                    r_ad.append(P.step("vector",
                        lambda e, oc=oc, ot=ot: e.tensor_add(
                            tt[:, oc, :], tt[:, oc, :],
                            sumacc[:, oc, ot * TSZ:(ot + 1) * TSZ]),
                        deps=[r_x3, r_accw[ot]]))
                r_l2 = emit_ln_tile(nc, P, lc, tt, tt, r_ad)
                r_tu = P.step("sync", lambda e, ot=ot: e.dma_start(
                    out=out3[:, :, ot * TSZ:(ot + 1) * TSZ], in_=tt[:, :, :]),
                    deps=r_l2, dma=True)

        if _own:
            sems = {e: e_(nc.semaphore(uname("sem_" + e)))
                    for e in ENGINES}
            block = e_(nc.Block())
            P.finalize(block, sems)
    return nc



def run_stageD(x3_fm, w_router_, gate_up_, down_, w_gate_, w_up_, w_down_):
    if "D" not in _nc_cache:
        _nc_cache["D"] = build_stageD()
    nc = _nc_cache["D"]
    wrT = np.ascontiguousarray(np.asarray(w_router_, np.float32).T)
    shared_gu = np.concatenate(
        [np.asarray(w_gate_, np.float32).T, np.asarray(w_up_, np.float32).T],
        axis=1)[None]
    gup17 = np.concatenate([np.asarray(gate_up_, np.float32), shared_gu],
                           axis=0)
    dwn17 = np.concatenate([np.asarray(down_, np.float32),
                            np.asarray(w_down_, np.float32).T[None]], axis=0)
    ident = np.eye(128, dtype=np.float32)
    utri = np.triu(np.ones((128, 128), np.float32))
    iotac = np.arange(128, dtype=np.float32).reshape(128, 1)
    iota16r = np.broadcast_to(np.arange(16, dtype=np.float32),
                              (128, 16)).copy()
    in_maps = []
    for c in range(NCORES):
        b = c // 4
        cols = my_cols(c)
        in_maps.append({
            "x3my": np.ascontiguousarray(x3_fm[b][:, cols]),
            "wrT": wrT, "gup": gup17, "dwn": dwn17, "identD": ident,
            "utri": utri, "iotac": iotac, "iota16r": iota16r,
        })
    res = run_bass_kernel_spmd(nc, in_maps, core_ids=list(range(NCORES)))
    run_stageD.last_results = res.results
    return scatter_from_cores([np.asarray(r["x4my"]) for r in res.results])


def build_stageA(fused=None):
    nc, P, _cache, _own = _begin(fused)
    _ctx = (nc, _cache, fused is not None)
    x0b = dram(_ctx, "x0b", [D, S], F32, "ExternalInput")
    x0my = dram(_ctx, "x0my", [D, TCORE], F32, "ExternalInput")
    wq = dram(_ctx, "wq", [D, D], F32, "ExternalInput")
    wk = dram(_ctx, "wk", [D, D], F32, "ExternalInput")
    wv = dram(_ctx, "wv", [D, D], F32, "ExternalInput")
    wo = dram(_ctx, "wo", [D, D], F32, "ExternalInput")
    maskA = dram(_ctx, "maskA", [4, 16, 128, TSZ], BF16, "ExternalInput")
    x1my = dram(_ctx, "x1my", [D, TCORE], F32, "ExternalOutput")

    with ExitStack() as st:
        e_ = st.enter_context
        _bar = now_all(P) if not _own else []
        wq_sb = e_(nc.sbuf_tensor(uname("wq_sb"), [128, 8, D], BF16))
        wk_sb = e_(nc.sbuf_tensor(uname("wk_sb"), [128, 8, D], BF16))
        wv_sb = e_(nc.sbuf_tensor(uname("wv_sb"), [128, 8, D], BF16))
        attn_fm = e_(nc.sbuf_tensor(uname("attn_fm"), [128, 8, TCORE], BF16))
        r_w = [P.step("gpsimd", lambda e, w=w, t=t: e.dma_start(
            out=t[:, :, :], in_=w.rearrange("(c p) o -> p c o", p=128)),
            deps=_bar, dma=True)
            for w, t in ((wq, wq_sb), (wk, wk_sb), (wv, wv_sb))]

        with ExitStack() as st_att:
            r_att = emit_attention(nc, P, st_att, causal=True, x_src=x0b,
                           xmy_src=x0my, wq_sb=wq_sb, wk_sb=wk_sb,
                           wv_sb=wv_sb, w_deps=r_w, mask_src=maskA,
                           attn_fm=attn_fm)

        wo_sb = e_(nc.sbuf_tensor(uname("wo_sb"), [128, 8, D], BF16))
        r_wo = P.step("gpsimd", lambda e: e.dma_start(
            out=wo_sb[:, :, :], in_=wo.rearrange("(c p) o -> p c o", p=128)),
            deps=now_all(P), dma=True)
        with ExitStack() as st_post:
            emit_proj_residual_ln(nc, P, st_post, src_fm=attn_fm, w_sb=wo_sb,
                                  r_w=r_wo, resid_src=x0my, resid_scale=2.0,
                                  out_hbm=x1my, r_src=r_att, name="Ap")

        if _own:
            sems = {e: e_(nc.semaphore(uname("sem_" + e)))
                    for e in ENGINES}
            block = e_(nc.Block())
            P.finalize(block, sems)
    return nc


def build_stageB(fused=None, kv=None):
    nc, P, _cache, _own = _begin(fused)
    _ctx = (nc, _cache, fused is not None)
    x1b = dram(_ctx, "x1b", [D, S], F32, "ExternalInput") if kv is None \
        else None
    x1my = dram(_ctx, "x1my", [D, TCORE], F32, "ExternalInput")
    qw = dram(_ctx, "qw", [D, D], F32, "ExternalInput")
    kw = dram(_ctx, "kw", [D, D], F32, "ExternalInput")
    vw = dram(_ctx, "vw", [D, D], F32, "ExternalInput")
    gw = dram(_ctx, "gw", [D, D], F32, "ExternalInput")
    ow = dram(_ctx, "ow", [D, D], F32, "ExternalInput")
    gb = dram(_ctx, "gb", [128, 8], F32, "ExternalInput")
    swT = dram(_ctx, "swT", [128, 16], F32, "ExternalInput")
    n1g = dram(_ctx, "n1g", [128, 8], F32, "ExternalInput")
    n1b = dram(_ctx, "n1b", [128, 8], F32, "ExternalInput")
    ymy = dram(_ctx, "ymy", [D, TCORE], F32, "ExternalOutput")

    with ExitStack() as st:
        e_ = st.enter_context
        _bar = now_all(P) if not _own else []
        wq_sb = e_(nc.sbuf_tensor(uname("wq_sb"), [128, 8, D], BF16))
        swT_sb = e_(nc.sbuf_tensor(uname("swT_sb"), [128, 16], BF16))
        attn_fm = e_(nc.sbuf_tensor(uname("attn_fm"), [128, 8, TCORE], BF16))
        pairs = [(qw, wq_sb)]
        if kv is None:
            wk_sb = e_(nc.sbuf_tensor(uname("wk_sb"), [128, 8, D], BF16))
            wv_sb = e_(nc.sbuf_tensor(uname("wv_sb"), [128, 8, D], BF16))
            pairs += [(kw, wk_sb), (vw, wv_sb)]
        else:
            wk_sb = wv_sb = None
        r_w = [P.step("gpsimd", lambda e, w=w, t=t: e.dma_start(
            out=t[:, :, :], in_=w.rearrange("(c p) o -> p c o", p=128)),
            deps=_bar, dma=True) for w, t in pairs]
        r_scw = P.step("gpsimd", lambda e: e.dma_start(
            out=swT_sb[:, :], in_=swT[:, :]), deps=_bar, dma=True)

        with ExitStack() as st_att:
            r_att = emit_attention(nc, P, st_att, causal=False, x_src=x1b,
                                   xmy_src=x1my, wq_sb=wq_sb, wk_sb=wk_sb,
                                   wv_sb=wv_sb, w_deps=r_w + _bar,
                                   mask_src=None,
                                   attn_fm=attn_fm, scale_wT_sb=swT_sb,
                                   r_scw=r_scw, kv=kv)

        gw_sb = e_(nc.sbuf_tensor(uname("gw_sb"), [128, 8, D], BF16))
        gb_sb = e_(nc.sbuf_tensor(uname("gb_sb"), [128, 8], F32))
        r_gw = P.step("gpsimd", lambda e: e.dma_start(
            out=gw_sb[:, :, :], in_=gw.rearrange("(c p) o -> p c o", p=128)),
            deps=now_all(P), dma=True)
        r_gbb = P.step("gpsimd", lambda e: e.dma_start(
            out=gb_sb[:, :], in_=gb[:, :]),
            dma=True)
        with ExitStack() as st_gate:
            eg = st_gate.enter_context
            gfull = eg(nc.sbuf_tensor(uname("gfull"), [128, 8, TSZ], F32))
            ps_g = [eg(nc.psum_tensor(uname("ps_g"), [128, TSZ], F32))
                    for _ in range(2)]
            r_guse = [None, None]
            r_gate_done = None
            for ot in range(4):
                r_sgs = []
                for oc in range(8):
                    pb = oc % 2
                    r_mm = None
                    for ic in range(8):
                        r_mm = P.step("tensor",
                            lambda e, oc=oc, ic=ic, ot=ot, pb=pb: e.matmul(
                                ps_g[pb][:, :],
                                gw_sb[:, ic, oc * 128:(oc + 1) * 128],
                                attn_fm[:, ic, ot * TSZ:(ot + 1) * TSZ],
                                start=(ic == 0), stop=(ic == 7)),
                            deps=[r_gw, r_att, r_guse[pb]]
                            if ic == 0 else [r_gw])
                    r_sg = P.step("scalar", lambda e, oc=oc, pb=pb:
                        e.activation(
                            gfull[:, oc, :], ps_g[pb][:, :], AF.Sigmoid,
                            bias=gb_sb[:, oc:oc + 1]),
                        deps=[r_mm, r_gbb, r_gate_done])
                    r_guse[pb] = r_sg
                    r_sgs.append(r_sg)
                for oc in range(8):
                    r_gate_done = P.step("vector",
                        lambda e, oc=oc, ot=ot: e.tensor_mul(
                            attn_fm[:, oc, ot * TSZ:(ot + 1) * TSZ],
                            attn_fm[:, oc, ot * TSZ:(ot + 1) * TSZ],
                            gfull[:, oc, :]),
                        deps=r_sgs)
        ow_sb = e_(nc.sbuf_tensor(uname("ow_sb"), [128, 8, D], BF16))
        n1g_sb = e_(nc.sbuf_tensor(uname("n1g_sb"), [128, 8], F32))
        n1b_sb = e_(nc.sbuf_tensor(uname("n1b_sb"), [128, 8], F32))
        r_ow = P.step("gpsimd", lambda e: e.dma_start(
            out=ow_sb[:, :, :], in_=ow.rearrange("(c p) o -> p c o", p=128)),
            deps=now_all(P), dma=True)
        r_n1 = P.step("gpsimd", lambda e: e.dma_start(
            out=n1g_sb[:, :], in_=n1g[:, :]),
            dma=True)
        r_n2 = P.step("gpsimd", lambda e: e.dma_start(
            out=n1b_sb[:, :], in_=n1b[:, :]),
            dma=True)
        with ExitStack() as st_post:
            emit_proj_residual_ln(nc, P, st_post, src_fm=attn_fm, w_sb=ow_sb,
                                  r_w=r_ow, resid_src=x1my, resid_scale=1.0,
                                  out_hbm=ymy, gamma_beta=(n1g_sb, n1b_sb),
                                  r_gb=[r_n1, r_n2], r_src=r_gate_done,
                                  name="Bp")

        if _own:
            sems = {e: e_(nc.semaphore(uname("sem_" + e)))
                    for e in ENGINES}
            block = e_(nc.Block())
            P.finalize(block, sems)
    return nc


def run_stageB(x1_fm, qw, kw, vw, ow, gw, gb_, sw, n1g_, n1b_):
    if "B" not in _nc_cache:
        _nc_cache["B"] = build_stageB()
    nc = _nc_cache["B"]
    swT = np.ascontiguousarray(np.asarray(sw).T)
    swT = np.concatenate([swT, swT], axis=0)
    in_maps = []
    for c in range(NCORES):
        b = c // 4
        in_maps.append({
            "x1b": np.ascontiguousarray(x1_fm[b]),
            "x1my": np.ascontiguousarray(x1_fm[b][:, my_cols(c)]),
            "qw": np.asarray(qw), "kw": np.asarray(kw),
            "vw": np.asarray(vw), "ow": np.asarray(ow), "gw": np.asarray(gw),
            "gb": vec_pc(gb_), "swT": swT,
            "n1g": vec_pc(n1g_), "n1b": vec_pc(n1b_),
        })
    res = run_bass_kernel_spmd(nc, in_maps, core_ids=list(range(NCORES)))
    run_stageB.last_results = res.results
    return scatter_from_cores([np.asarray(r["ymy"]) for r in res.results])


def make_masksA():
    out = []
    kk = np.arange(128)[:, None]
    qq = np.arange(TSZ)[None, :]
    for r in range(4):
        m = np.zeros((4, 16, 128, TSZ), dtype=np.float32)
        for s_ in range(4):
            p = TILES[r][s_]
            for mc in range(16):
                chunk = 16 * s_ + mc
                m[s_, mc] = ((128 * chunk + kk) <= (TSZ * p + qq))
        out.append(m.astype(bf))
    return out


def vec_pc(v):
    return np.ascontiguousarray(np.asarray(v, np.float32).reshape(8, 128).T)


def my_cols(core):
    idx = []
    for p in core_tiles(core):
        idx.extend(range(p * TSZ, (p + 1) * TSZ))
    return np.array(idx)


def scatter_from_cores(parts):
    full = np.zeros((B, D, S), dtype=parts[0].dtype)
    for c in range(NCORES):
        b = c // 4
        cols = my_cols(c)
        full[b][:, cols] = parts[c]
    return full


_nc_cache = {}


def run_stageA(x0, wq, wk, wv, wo):
    x0_fm = np.ascontiguousarray(np.transpose(np.asarray(x0), (0, 2, 1)))
    masks = make_masksA()
    if "A" not in _nc_cache:
        _nc_cache["A"] = build_stageA()
    nc = _nc_cache["A"]
    in_maps = []
    for c in range(NCORES):
        b = c // 4
        in_maps.append({
            "x0b": x0_fm[b],
            "x0my": np.ascontiguousarray(x0_fm[b][:, my_cols(c)]),
            "wq": np.asarray(wq), "wk": np.asarray(wk),
            "wv": np.asarray(wv), "wo": np.asarray(wo),
            "maskA": masks[c % 4],
        })
    res = run_bass_kernel_spmd(nc, in_maps, core_ids=list(range(NCORES)))
    run_stageA.last_results = res.results
    return scatter_from_cores([np.asarray(r["x1my"]) for r in res.results])


def emit_kv_shard(nc, P, st, *, xmy3, wk_sb, wv_sb, w_deps, k_out, v_out):
    e_ = st.enter_context
    xt = [e_(nc.sbuf_tensor(uname("kvx"), [128, 8, TSZ], BF16))
          for _ in range(2)]
    kf = e_(nc.sbuf_tensor(uname("kvk"), [128, 8, TSZ], BF16))
    vtm = e_(nc.sbuf_tensor(uname("kvv"), [128, 2, TSZ], BF16))
    ps_k = e_(nc.psum_tensor(uname("kvpk"), [128, TSZ], F32))
    ps_v = [e_(nc.psum_tensor(uname("kvpv"), [128, TSZ], F32))
            for _ in range(2)]
    k3 = k_out.rearrange("(c p) t -> p c t", p=128)
    r_x = [None, None]
    r_kst = None
    r_vst = None
    r_kev = None
    r_vev = [None, None]
    outs = []
    for kt in range(4):
        r_xl = P.step("gpsimd", lambda e, kt=kt, b=kt % 2: e.dma_start(
            out=xt[b][:, :, :], in_=xmy3[:, :, kt * TSZ:(kt + 1) * TSZ]),
            deps=[r_x[kt % 2]] + w_deps, dma=True)
        r_evs = []
        for oc in range(8):
            r_mm = None
            for dc in range(8):
                r_mm = P.step("tensor", lambda e, oc=oc, dc=dc, b=kt % 2:
                    e.matmul(ps_k[:, :],
                             wk_sb[:, dc, oc * 128:(oc + 1) * 128],
                             xt[b][:, dc, :], start=(dc == 0),
                             stop=(dc == 7)),
                    deps=[r_xl, r_kev] if dc == 0 else [r_xl])
            r_kev = P.step("vector", lambda e, oc=oc: e.tensor_copy(
                kf[:, oc, :], ps_k[:, :]), deps=[r_mm, r_kst])
            r_evs.append(r_kev)
        r_kst = P.step("sync", lambda e, kt=kt: e.dma_start(
            out=k3[:, :, kt * TSZ:(kt + 1) * TSZ], in_=kf[:, :, :]),
            deps=r_evs, dma=True)
        outs.append(r_kst)
        for th in range(4):
            r_vm = [None, None]
            for half in range(2):
                for dc in range(8):
                    r_vm[half] = P.step("tensor",
                        lambda e, th=th, half=half, dc=dc, b=kt % 2:
                        e.matmul(ps_v[half][:, :],
                                 xt[b][:, dc, th * 128:(th + 1) * 128],
                                 wv_sb[:, dc, half * TSZ:(half + 1) * TSZ],
                                 start=(dc == 0), stop=(dc == 7)),
                        deps=[r_xl, r_vev[half]] if dc == 0 else [r_xl])
            r_ev2 = []
            for half in range(2):
                r_vev[half] = P.step("vector",
                    lambda e, half=half: e.tensor_copy(
                        vtm[:, half, :], ps_v[half][:, :]),
                    deps=[r_vm[half], r_vst])
                r_ev2.append(r_vev[half])
            r_vst = P.step("sync", lambda e, kt=kt, th=th: e.dma_start(
                out=v_out[kt * TSZ + th * 128:kt * TSZ + (th + 1) * 128, :],
                in_=vtm[:, :, :].rearrange("p h t -> p (h t)")),
                deps=r_ev2, dma=True)
            outs.append(r_vst)
        r_x[kt % 2] = r_evs[-1]
    return outs


def build_fused():
    nc = bass.Bass("TRN2", target_bir_lowering=False)
    P = Prog(nc)
    cache = {}
    fused = (nc, P, cache)
    _ctx = (nc, cache, True)
    build_stageA(fused)

    kw = dram(_ctx, "kw", [D, D], F32, "ExternalInput")
    vw = dram(_ctx, "vw", [D, D], F32, "ExternalInput")
    k_sh = dram(_ctx, "k_sh", [D, TCORE], BF16, "Internal")
    v_sh = dram(_ctx, "v_sh", [TCORE, D], BF16, "Internal")
    k_g = dram(_ctx, "k_g", [4 * D, TCORE], BF16, "Internal")
    v_g = dram(_ctx, "v_g", [S, D], BF16, "Internal")
    groups = [[0, 1, 2, 3], [4, 5, 6, 7]]
    with ExitStack() as st:
        e_ = st.enter_context
        bar = now_all(P)
        wk_sb = e_(nc.sbuf_tensor(uname("fkw"), [128, 8, D], BF16))
        wv_sb = e_(nc.sbuf_tensor(uname("fvw"), [128, 8, D], BF16))
        r_w = [P.step("gpsimd", lambda e, w=w, t=t: e.dma_start(
            out=t[:, :, :], in_=w.rearrange("(c p) o -> p c o", p=128)),
            deps=bar, dma=True) for w, t in ((kw, wk_sb), (vw, wv_sb))]
        outs = emit_kv_shard(nc, P, st, xmy3=fm3(cache["x1my"]),
                             wk_sb=wk_sb, wv_sb=wv_sb, w_deps=r_w,
                             k_out=k_sh, v_out=v_sh)
        sem_k = e_(nc.semaphore(uname("cck")))
        sem_v = e_(nc.semaphore(uname("ccv")))
        r_cck = P.collective(lambda e: e.collective_compute(
            "AllGather", ALU.bypass, replica_groups=groups,
            ins=[k_sh[:, :].opt()], outs=[k_g[:, :].opt()]),
            deps=outs, sem=sem_k)
        r_ccv = P.collective(lambda e: e.collective_compute(
            "AllGather", ALU.bypass, replica_groups=groups,
            ins=[v_sh[:, :].opt()], outs=[v_g[:, :].opt()]),
            deps=outs, sem=sem_v)

    build_stageB(fused, kv={"kg": k_g, "vg": v_g, "dep": [r_cck, r_ccv]})
    build_stageBmoe(fused)
    build_stageC(fused)
    build_stageD(fused)

    with ExitStack() as st2:
        sems = {e: st2.enter_context(nc.semaphore(uname("sem_" + e)))
                for e in ENGINES}
        block = st2.enter_context(nc.Block())
        P.finalize(block, sems)
    return nc


def run_fused(inputs):
    i = inputs
    if "F" not in _nc_cache:
        _nc_cache["F"] = build_fused()
    nc = _nc_cache["F"]
    x0 = np.asarray(i["L0_attn_x"], np.float32)
    x0_fm = np.ascontiguousarray(np.transpose(x0, (0, 2, 1)))
    masks = make_masksA()
    swT = np.ascontiguousarray(np.asarray(i["L0_ffn_scale_w"]).T)
    swT = np.concatenate([swT, swT], axis=0)
    iota8 = np.broadcast_to(np.arange(8, dtype=np.float32), (128, 8)).copy()
    ident = np.eye(128, dtype=np.float32)
    b1x = np.stack([vec_pc16(i["L0_ffn_b1"][e]) for e in range(8)])
    lngx = np.stack([vec_pc16(i["L0_ffn_ln_g"][e]) for e in range(8)])
    lnbx = np.stack([vec_pc16(i["L0_ffn_ln_b"][e]) for e in range(8)])
    b2x = np.stack([vec_pc(i["L0_ffn_b2"][e]) for e in range(8)])
    woT = np.ascontiguousarray(np.asarray(i["L1_attn_w_out"], np.float32).T)
    wrT = np.ascontiguousarray(
        np.asarray(i["L1_ffn_w_router"], np.float32).T)
    shared_gu = np.concatenate(
        [np.asarray(i["L1_ffn_w_gate"], np.float32).T,
         np.asarray(i["L1_ffn_w_up"], np.float32).T], axis=1)[None]
    gup17 = np.concatenate(
        [np.asarray(i["L1_ffn_gate_up_proj"], np.float32), shared_gu], axis=0)
    dwn17 = np.concatenate(
        [np.asarray(i["L1_ffn_down_proj"], np.float32),
         np.asarray(i["L1_ffn_w_down"], np.float32).T[None]], axis=0)
    in_maps = []
    for c in range(NCORES):
        b = c // 4
        cols = my_cols(c)
        sel = np.zeros((NT, 4), np.float32)
        for sl in range(4):
            sel[core_tiles(c)[sl], sl] = 1.0
        in_maps.append({
            "x0b": x0_fm[b],
            "x0my": np.ascontiguousarray(x0_fm[b][:, cols]),
            "wq": np.asarray(i["L0_attn_Wq"]),
            "wk": np.asarray(i["L0_attn_Wk"]),
            "wv": np.asarray(i["L0_attn_Wv"]),
            "wo": np.asarray(i["L0_attn_Wo"]),
            "maskA": masks[c % 4],
            "qw": np.asarray(i["L0_ffn_q_w"]),
            "kw": np.asarray(i["L0_ffn_k_w"]),
            "vw": np.asarray(i["L0_ffn_v_w"]),
            "gw": np.asarray(i["L0_ffn_gate_w"]),
            "ow": np.asarray(i["L0_ffn_out_w"]),
            "gb": vec_pc(i["L0_ffn_gate_b"]),
            "swT": swT,
            "n1g": vec_pc(i["L0_ffn_n1_g"]),
            "n1b": vec_pc(i["L0_ffn_n1_b"]),
            "mgw": np.asarray(i["L0_ffn_moe_gate_w"]),
            "mgb": np.asarray(i["L0_ffn_moe_gate_b"],
                              np.float32).reshape(8, 1),
            "w1": np.asarray(i["L0_ffn_w1"]),
            "b1x": b1x, "lng": lngx, "lnb": lnbx,
            "w2": np.asarray(i["L0_ffn_w2"]),
            "b2x": b2x,
            "rsc": np.asarray(i["L0_ffn_res_scale"],
                              np.float32).reshape(1, 8),
            "n2g": vec_pc(i["L0_ffn_n2_g"]),
            "n2b": vec_pc(i["L0_ffn_n2_b"]),
            "iota8": iota8, "ident": ident,
            "a_pc": vec_pc(i["L1_attn_a"]),
            "b_pc": vec_pc(i["L1_attn_b"]),
            "woT": woT, "bo_pc": vec_pc(i["L1_attn_b_out"]),
            "identC": ident, "selq": np.broadcast_to(
                sel, (128, NT, 4)).copy(),
            "wrT": wrT, "gup": gup17, "dwn": dwn17, "identD": ident,
            "utri": np.triu(np.ones((128, 128), np.float32)),
            "iotac": np.arange(128, dtype=np.float32).reshape(128, 1),
            "iota16r": np.broadcast_to(
                np.arange(16, dtype=np.float32), (128, 16)).copy(),
        })
    res = run_bass_kernel_spmd(nc, in_maps, core_ids=list(range(NCORES)))
    run_fused.last_results = res.results
    x4 = scatter_from_cores([np.asarray(r["x4my"]) for r in res.results])
    return np.ascontiguousarray(np.transpose(x4, (0, 2, 1)))


def kernel_multilaunch(**inputs):
    i = inputs
    x0 = np.asarray(i["L0_attn_x"], np.float32)
    x1_fm = run_stageA(x0, i["L0_attn_Wq"], i["L0_attn_Wk"],
                       i["L0_attn_Wv"], i["L0_attn_Wo"])
    y_fm = run_stageB(x1_fm, i["L0_ffn_q_w"], i["L0_ffn_k_w"],
                      i["L0_ffn_v_w"], i["L0_ffn_out_w"], i["L0_ffn_gate_w"],
                      i["L0_ffn_gate_b"], i["L0_ffn_scale_w"],
                      i["L0_ffn_n1_g"], i["L0_ffn_n1_b"])
    x2_fm = run_stageBmoe(y_fm, x1_fm, i["L0_ffn_moe_gate_w"],
                          i["L0_ffn_moe_gate_b"], i["L0_ffn_w1"],
                          i["L0_ffn_b1"], i["L0_ffn_ln_g"], i["L0_ffn_ln_b"],
                          i["L0_ffn_w2"], i["L0_ffn_b2"],
                          i["L0_ffn_res_scale"], i["L0_ffn_n2_g"],
                          i["L0_ffn_n2_b"])
    x3_fm = run_stageC(x2_fm, i["L1_attn_a"], i["L1_attn_b"],
                       i["L1_attn_w_out"], i["L1_attn_b_out"])
    x4_fm = run_stageD(x3_fm, i["L1_ffn_w_router"], i["L1_ffn_gate_up_proj"],
                       i["L1_ffn_down_proj"], i["L1_ffn_w_gate"],
                       i["L1_ffn_w_up"], i["L1_ffn_w_down"])
    return np.ascontiguousarray(np.transpose(x4_fm, (0, 2, 1)))


def kernel(**inputs):
    return run_fused(inputs)


# version 12
# speedup vs baseline: 1.3123x; 1.0077x over previous
import numpy as np
import ml_dtypes
from contextlib import ExitStack

import concourse.bass as bass
from concourse.bass import IndirectOffsetOnAxis
import concourse.mybir as mybir
from concourse.bass_utils import run_bass_kernel_spmd

F32 = mybir.dt.float32
BF16 = mybir.dt.bfloat16
I32 = mybir.dt.int32
AF = mybir.ActivationFunctionType
ALU = mybir.AluOpType
AX = mybir.AxisListType

B, S, D, H, DH = 2, 8192, 1024, 16, 64
NT, TSZ = 16, 512
NCORES = 8
TILES = [[0, 7, 8, 15], [1, 6, 9, 14], [2, 5, 10, 13], [3, 4, 11, 12]]
TCORE = 4 * TSZ
NSLOT = 512
DBG_IDX = False
DBG_LOOPBACK = False
DBG_IDENT = False
NKC = S // 128
bf = ml_dtypes.bfloat16

ENGINES = ("sync", "scalar", "vector", "tensor", "gpsimd")


def core_tiles(core):
    return TILES[core % 4]


class Ref:
    __slots__ = ("eng", "cnt")

    def __init__(self, eng, cnt):
        self.eng = eng
        self.cnt = cnt


class Prog:

    SERIAL = False
    STEP_LIMIT = None

    def __init__(self, nc):
        self.nc = nc
        self.steps = {e: [] for e in ENGINES}
        self.cnt = {e: 0 for e in ENGINES}
        self.nsteps = 0
        self.prev = None

    def collective(self, fn, deps=(), sem=None):
        def wrapped(e):
            fn(e).then_inc(sem, 1)
            e.wait_ge(sem, 1)
            return e.nop()
        return self.step("gpsimd", wrapped, deps=deps)

    def step(self, eng, fn, deps=(), dma=False):
        self.nsteps += 1
        if Prog.STEP_LIMIT is not None and self.nsteps > Prog.STEP_LIMIT:
            return None
        inc = 16 if dma else 1
        waits = []
        for d in deps:
            if d is None:
                continue
            if isinstance(d, (list, tuple)):
                for dd in d:
                    if dd is not None and not (eng == "tensor" and
                                               dd.eng == "tensor"):
                        waits.append((dd.eng, dd.cnt))
            elif not (eng == "tensor" and d.eng == "tensor"):
                waits.append((d.eng, d.cnt))
        if Prog.SERIAL and self.prev is not None and self.prev.eng != eng:
            waits.append((self.prev.eng, self.prev.cnt))
        self.steps[eng].append((waits, fn, inc))
        self.cnt[eng] += inc
        self.prev = Ref(eng, self.cnt[eng])
        return self.prev

    def finalize(self, block, sems):
        Prog.LAST_NSTEPS = self.nsteps

        def emit_for(eng):
            def f(e):
                hw = {o: 0 for o in ENGINES}
                for waits, fn, inc in self.steps[eng]:
                    for (oeng, cnt) in waits:
                        if hw[oeng] < cnt:
                            e.wait_ge(sems[oeng], cnt)
                            hw[oeng] = cnt
                    fn(e).then_inc(sems[eng], inc)
                if eng == "gpsimd":
                    for o in ENGINES:
                        if o != eng and self.cnt[o] > 0:
                            e.wait_ge(sems[o], self.cnt[o])
            return f
        block.sync(emit_for("sync"))
        block.scalar(emit_for("scalar"))
        block.vector(emit_for("vector"))
        block.tensor(emit_for("tensor"))
        block.gpsimd(emit_for("gpsimd"))


def kernel_all_vec(P):
    return Ref("vector", P.cnt["vector"])


def now_all(P):
    return [Ref(e, P.cnt[e]) for e in ENGINES if P.cnt[e] > 0]


_INTERMEDIATE = {"x1my", "ymy", "x2my", "x3my"}


def _begin(fused):
    if fused is None:
        nc = bass.Bass("TRN2", target_bir_lowering=False)
        return nc, Prog(nc), {}, True
    nc, P, cache = fused
    return nc, P, cache, False


def dram(ctx, name, shape, dtype, kind):
    nc, cache, isfused = ctx
    if name in cache:
        return cache[name]
    if isfused and name in _INTERMEDIATE:
        kind = "Internal"
    ap = nc.dram_tensor(name, shape, dtype, kind=kind).ap()
    cache[name] = ap
    return ap


_UN = [0]


def uname(n):
    _UN[0] += 1
    return f"{n}_u{_UN[0]}"


def fm3(ap):
    return ap.rearrange("(c p) t -> p c t", p=128)


def emit_attention(nc, P, st, *, causal, x_src, xmy_src, wq_sb, wk_sb, wv_sb,
                   w_deps, mask_src, attn_fm, scale_wT_sb=None, r_scw=None,
                   kv=None):
    e_ = st.enter_context
    kh = e_(nc.sbuf_tensor(uname("kh"), [128, 2, S], BF16))
    vh = e_(nc.sbuf_tensor(uname("vh"), [128, NKC, 256], BF16))
    qp = e_(nc.sbuf_tensor(uname("qp"), [128, TCORE], BF16))
    xt = [e_(nc.sbuf_tensor(uname(f"xt{i}"), [128, 8, TSZ], BF16)) for i in range(2)]
    msk = e_(nc.sbuf_tensor(uname("msk"), [128, 16, TSZ], BF16)) if causal else None
    pp = [e_(nc.sbuf_tensor(uname(f"pp{i}"), [128, 2 * TSZ], BF16))
          for i in range(2)]
    qhat = e_(nc.sbuf_tensor(uname("qhat"), [128, TSZ], BF16))
    sqb = e_(nc.sbuf_tensor(uname("sqb"), [64, TSZ], BF16))
    kss = e_(nc.sbuf_tensor(uname("kss"), [1, 1], F32))
    kssc = e_(nc.sbuf_tensor(uname("kssc"), [1, 1], F32))
    k2m = e_(nc.sbuf_tensor(uname("k2m"), [1, NT], F32))
    q2r = e_(nc.sbuf_tensor(uname("q2r"), [1, TCORE], F32))
    shf = e_(nc.sbuf_tensor(uname("shf"), [1, TCORE], F32))
    crow = e_(nc.sbuf_tensor(uname("crow"), [1, TCORE], F32)) \
        if not causal else None
    rrow = e_(nc.sbuf_tensor(uname("rrow"), [1, TSZ], F32))
    rb_sb = e_(nc.sbuf_tensor(uname("rb_sb"), [64, TSZ], F32))
    ones64b = e_(nc.sbuf_tensor(uname("ones64b"), [64, 1], BF16))
    onesf = e_(nc.sbuf_tensor(uname("onesf"), [1, 128], F32))
    cb_sb = e_(nc.sbuf_tensor(uname("cb_sb"), [128, TSZ], BF16)) if not causal else None

    ps_s = [e_(nc.psum_tensor(uname(f"ps_s{i}"), [128, 2 * TSZ], F32))
            for i in range(3)]
    ps_p0 = ps_s[2][:, 0:TSZ]
    ps_p = [ps_p0, ps_p0]
    ps_av = e_(nc.psum_tensor(uname("ps_av"), [128, TSZ], F32))
    ps_v = [ps_s[2][:, TSZ:2 * TSZ], ps_av]
    ps_rb = e_(nc.psum_tensor(uname("ps_rb"), [128, TSZ], F32))

    r_ones = P.step("vector", lambda e: e.memset(ones64b[:, :], 1.0))
    r_qh0 = P.step("vector", lambda e: e.memset(qhat[64:128, :], 0.0))
    r_onesf = P.step("vector", lambda e: e.memset(onesf[:, :], 1.0))
    r_aug0 = P.step("vector", lambda e: e.memset(kh[64:128, :, :], 0.0))
    r_aug = P.step("vector", lambda e: e.memset(kh[64:65, :, :], -1.0),
                   deps=[r_aug0])
    r_v0 = P.step("vector", lambda e: e.memset(vh[:, :, :], 0.0))
    r_vone = P.step("vector", lambda e: e.memset(vh[:, :, 64:65], 1.0),
                    deps=[r_v0])
    r_vone2 = P.step("vector", lambda e: e.memset(vh[:, :, 192:193], 1.0),
                     deps=[r_v0])

    x3 = fm3(x_src) if kv is None else None
    xmy3 = fm3(xmy_src)
    vg3 = kv["vg"].rearrange("(c p) f -> p c f", p=128) if kv else None

    r_mask = None
    r_avout = None
    r_ppuse = [None, None]
    r_exuse = [None, None, None]
    r_qhat_use = None
    r_rb_write = None
    r_rb_read = None
    r_sq_use = None
    r_row_evac = None
    r_q2r_read = None
    r_shf_read = []
    r_crow_read = []
    r_xuse = [None, None]
    r_p_evac = [None, None]
    r_v_evac = [None, None]

    for pair in range(8):
        r_kh = {}
        r_vh_done = {}
        if kv is not None:
            kvdeps = [kv["dep"], r_ppuse[0], r_ppuse[1], r_avout]
            r_kl = []
            for hh in range(2):
                for rb in range(4):
                    r0 = 1024 * rb + 128 * pair + 64 * hh
                    r_kl.append(P.step("sync",
                        lambda e, hh=hh, rb=rb, r0=r0: e.dma_start(
                            out=kh[0:64, hh, rb * TCORE:(rb + 1) * TCORE],
                            in_=kv["kg"][r0:r0 + 64, :]),
                        deps=kvdeps, dma=True))
                r_vl = P.step("gpsimd", lambda e, hh=hh, pair=pair:
                    e.dma_start(
                        out=vh[:, :, 128 * hh:128 * hh + 64],
                        in_=vg3[:, :, 128 * pair + 64 * hh:
                                128 * pair + 64 * hh + 64]),
                    deps=kvdeps + [r_vone, r_vone2], dma=True)
                r_kl.append(r_vl)
            for kt in range(NT):
                for hh in range(2):
                    r_kh[(kt, hh)] = r_kl
            for ch in range(NKC):
                r_vh_done[ch] = r_kl
            r_xuse = [r_kl, r_kl]
        else:
         for kt in range(NT):
            buf = xt[kt % 2]
            kb = 0
            r_x = P.step("gpsimd", lambda e, kt=kt, buf=buf: e.dma_start(
                out=buf[:, :, :], in_=x3[:, :, kt * TSZ:(kt + 1) * TSZ]),
                deps=[r_xuse[kt % 2]] + w_deps, dma=True)
            r_mm = None
            for dc in range(8):
                r_mm = P.step("tensor",
                    lambda e, dc=dc, buf=buf, pair=pair, kb=kb: e.matmul(
                        ps_p[kb][:, :],
                        wk_sb[:, dc, pair * 128:(pair + 1) * 128],
                        buf[:, dc, :], start=(dc == 0), stop=(dc == 7)),
                    deps=[r_x, r_p_evac[kb], r_exuse[2]]
                    if dc == 0 else [r_x])
            for hh in range(2):
                r_kh[(kt, hh)] = P.step(
                    "vector", lambda e, hh=hh, kt=kt, kb=kb: e.tensor_copy(
                        kh[0:64, hh, kt * TSZ:(kt + 1) * TSZ],
                        ps_p[kb][64 * hh:64 * hh + 64, :]),
                    deps=[r_mm, r_aug])
            r_p_evac[kb] = r_kh[(kt, 1)]
            for th in range(4):
                ch = kt * 4 + th
                vb = th % 2
                r_vm = None
                for dc in range(8):
                    r_vm = P.step(
                        "tensor",
                        lambda e, dc=dc, th=th, buf=buf, pair=pair, vb=vb:
                        e.matmul(
                            ps_v[vb][:, 0:128],
                            buf[:, dc, th * 128:(th + 1) * 128],
                            wv_sb[:, dc, pair * 128:(pair + 1) * 128],
                            start=(dc == 0), stop=(dc == 7)),
                        deps=[r_x, r_v_evac[vb], r_exuse[2]] +
                             ([r_avout] if vb == 1 else [])
                        if dc == 0 else [r_x])
                for hh in range(2):
                    r_v_evac[vb] = P.step(
                        "vector", lambda e, hh=hh, ch=ch, vb=vb:
                        e.tensor_copy(
                            vh[:, ch, 128 * hh:128 * hh + 64],
                            ps_v[vb][:, 64 * hh:64 * hh + 64]),
                        deps=[r_vm, r_vone, r_vone2])
                r_vh_done[ch] = r_v_evac[vb]
            r_xuse[kt % 2] = [r_v_evac[0], r_v_evac[1], r_kh[(kt, 1)]]
        r_qp = {}
        for qt in range(4):
            buf = xt[qt % 2]
            kb = 0
            r_x = P.step("gpsimd", lambda e, qt=qt, buf=buf: e.dma_start(
                out=buf[:, :, :], in_=xmy3[:, :, qt * TSZ:(qt + 1) * TSZ]),
                deps=[r_xuse[qt % 2]], dma=True)
            r_mm = None
            for dc in range(8):
                r_mm = P.step("tensor",
                    lambda e, dc=dc, buf=buf, pair=pair, kb=kb: e.matmul(
                        ps_p[kb][:, :],
                        wq_sb[:, dc, pair * 128:(pair + 1) * 128],
                        buf[:, dc, :], start=(dc == 0), stop=(dc == 7)),
                    deps=[r_x, r_p_evac[kb], r_exuse[2]]
                    if dc == 0 else [r_x])
            r_p_evac[kb] = P.step("vector", lambda e, qt=qt, kb=kb:
                e.tensor_copy(
                    qp[:, qt * TSZ:(qt + 1) * TSZ], ps_p[kb][:, :]),
                deps=[r_mm])
            r_qp[qt] = r_p_evac[kb]
            r_xuse[qt % 2] = r_p_evac[kb]

        r_exuse[2] = [r_exuse[2], r_p_evac[0]]

        for hh in range(2):
            r_t = []
            for kt in range(NT):
                r_s = P.step("vector", lambda e, hh=hh, kt=kt: e.tensor_mul(
                    sqb[:, :], kh[0:64, hh, kt * TSZ:(kt + 1) * TSZ],
                    kh[0:64, hh, kt * TSZ:(kt + 1) * TSZ]),
                    deps=[r_kh[(kt, hh)], r_sq_use, r_ones])
                r_m = P.step("tensor", lambda e: e.matmul(
                    ps_rb[0:1, :], ones64b[:, :], sqb[:, :],
                    start=True, stop=True),
                    deps=[r_s, r_row_evac, r_rb_read])
                r_sq_use = r_m
                r_row_evac = P.step(
                    "vector", lambda e, kt=kt: e.reduce_max(
                        k2m[0:1, kt:kt + 1], ps_rb[0:1, :], axis=AX.X),
                    deps=[r_m])
                r_t.append(r_row_evac)
            r_kss = P.step("vector", lambda e: e.reduce_max(
                kss[0:1, 0:1], k2m[0:1, :], axis=AX.X), deps=r_t)
            cmul = (1.02 / 64.0) if causal else 1.02
            r_kssc = P.step("vector", lambda e, cmul=cmul: e.tensor_scalar(
                kssc[0:1, 0:1], kss[0:1, 0:1], cmul, None,
                ALU.mult), deps=[r_kss])
            for qt in range(4):
                r_s = P.step("vector", lambda e, hh=hh, qt=qt: e.tensor_mul(
                    sqb[:, :], qp[64 * hh:64 * hh + 64, qt * TSZ:(qt + 1) * TSZ],
                    qp[64 * hh:64 * hh + 64, qt * TSZ:(qt + 1) * TSZ]),
                    deps=[r_qp[qt], r_sq_use])
                r_m = P.step("tensor", lambda e: e.matmul(
                    ps_rb[0:1, :], ones64b[:, :], sqb[:, :],
                    start=True, stop=True),
                    deps=[r_s, r_row_evac, r_rb_read])
                r_sq_use = r_m
                r_row_evac = P.step(
                    "vector", lambda e, qt=qt: e.tensor_copy(
                        q2r[0:1, qt * TSZ:(qt + 1) * TSZ], ps_rb[0:1, :]),
                    deps=[r_m, r_q2r_read])
            r_l = P.step("scalar", lambda e: e.activation(
                shf[0:1, :], q2r[0:1, :], AF.Ln,
                scale=kssc[0:1, 0:1]), deps=[r_row_evac, r_kssc] + r_shf_read)
            r_q2r_read = r_l
            r_sh = P.step("scalar", lambda e: e.activation(
                shf[0:1, :], shf[0:1, :], AF.Exp, scale=0.5), deps=[r_l])
            if not causal:
                r_cs = []
                for qt in range(4):
                    r_m = P.step("tensor", lambda e, hh=hh, qt=qt, pair=pair: e.matmul(
                        ps_rb[0:1, :],
                        scale_wT_sb[64 * hh:64 * hh + 64,
                                    2 * pair + hh:2 * pair + hh + 1],
                        qp[64 * hh:64 * hh + 64, qt * TSZ:(qt + 1) * TSZ],
                        start=True, stop=True),
                        deps=[r_qp[qt], r_scw, r_row_evac, r_sh,
                              r_rb_read])
                    r_c = P.step("scalar", lambda e, qt=qt: e.activation(
                        crow[0:1, qt * TSZ:(qt + 1) * TSZ], ps_rb[0:1, :],
                        AF.Exp, scale=-1.0, bias=0.0),
                        deps=[r_m] + r_crow_read)
                    r_crow_read = []
                    r_row_evac = r_c
                    r_cs.append(r_c)
                r_a1 = P.step("vector", lambda e: e.tensor_scalar(
                    crow[0:1, :], crow[0:1, :], 1.0, None, ALU.add),
                    deps=r_cs)
                r_rc = P.step("vector", lambda e: e.reciprocal(
                    crow[0:1, :], crow[0:1, :]), deps=[r_a1])
                r_c2 = P.step("vector", lambda e: e.tensor_scalar(
                    crow[0:1, :], crow[0:1, :], 0.25, None, ALU.mult),
                    deps=[r_rc])
                r_shift = P.step("vector", lambda e: e.tensor_mul(
                    shf[0:1, :], shf[0:1, :], crow[0:1, :]),
                    deps=[r_sh, r_c2])
            else:
                r_shift = r_sh
            r_shf_read = []

            for slot in range(4):
                if causal:
                    r_mask = P.step("sync", lambda e, slot=slot: e.dma_start(
                        out=msk[:, :, :],
                        in_=mask_src[slot, :, :, :].rearrange("k m q -> m k q")),
                        deps=[r_ppuse[0], r_ppuse[1]], dma=True)
                nch = 16 * slot + 16 if causal else NKC
                if causal:
                    r_q1 = P.step("vector", lambda e, hh=hh, slot=slot:
                        e.tensor_scalar(
                            qhat[0:64, :],
                            qp[64 * hh:64 * hh + 64,
                               slot * TSZ:(slot + 1) * TSZ],
                            0.125, None, ALU.mult),
                        deps=[r_qhat_use, r_qp[slot]])
                else:
                    r_rb_write = P.step("tensor", lambda e, slot=slot:
                        e.matmul(ps_rb[:, :], onesf[:, :],
                                 crow[0:1, slot * TSZ:(slot + 1) * TSZ],
                                 start=True, stop=True),
                        deps=[r_shift, r_onesf, r_rb_read])
                    r_crow_read.append(r_rb_write)
                    r_rb_read = P.step("vector", lambda e: e.tensor_copy(
                        cb_sb[:, :], ps_rb[:, :]), deps=[r_rb_write])
                    r_q1 = P.step("vector", lambda e, hh=hh, slot=slot:
                        e.tensor_mul(
                            qhat[0:64, :],
                            qp[64 * hh:64 * hh + 64,
                               slot * TSZ:(slot + 1) * TSZ],
                            cb_sb[64 * hh:64 * hh + 64, :]),
                        deps=[r_qhat_use, r_qp[slot], r_rb_read])
                r_q2 = P.step("vector", lambda e, slot=slot:
                    e.tensor_copy(
                        qhat[64:65, :],
                        shf[0:1, slot * TSZ:(slot + 1) * TSZ]),
                    deps=[r_shift, r_qhat_use, r_qh0])
                r_shf_read.append(r_q2)
                r_qh = [r_q1, r_q2]
                r_av = None
                npr = nch // 2

                def emit_scores(chp, hh=hh):
                    pb = chp % 3
                    r_scs = []
                    for i2 in range(2):
                        ch = 2 * chp + i2
                        r_scs.append(P.step("tensor",
                            lambda e, ch=ch, hh=hh, pb=pb, i2=i2:
                            e.matmul(
                                ps_s[pb][:, i2 * TSZ:(i2 + 1) * TSZ],
                                kh[0:128, hh, ch * 128:(ch + 1) * 128],
                                qhat[0:128, :],
                                start=True, stop=True),
                            deps=r_qh + [r_exuse[pb]]))
                    return r_scs

                r_s_q = [emit_scores(0)]
                if npr > 1:
                    r_s_q.append(emit_scores(1))
                for chp in range(npr):
                    pb = chp % 2
                    pb3 = chp % 3
                    r_scs = r_s_q.pop(0)
                    r_ex = P.step("scalar", lambda e, pb=pb, pb3=pb3:
                        e.activation(pp[pb][:, :], ps_s[pb3][:, :], AF.Exp),
                        deps=r_scs + [r_ppuse[pb]])
                    r_exuse[pb3] = r_ex
                    if chp + 2 < npr:
                        r_s_q.append(emit_scores(chp + 2))
                    r_pf = r_ex
                    for i2 in range(2):
                        ch = 2 * chp + i2
                        if causal and ch >= 16 * slot:
                            r_pf = P.step("vector",
                                lambda e, ch=ch, slot=slot, pb=pb, i2=i2:
                                e.tensor_mul(
                                    pp[pb][:, i2 * TSZ:(i2 + 1) * TSZ],
                                    pp[pb][:, i2 * TSZ:(i2 + 1) * TSZ],
                                    msk[:, ch - 16 * slot, :]),
                                deps=[r_ex, r_mask])
                    for i2 in range(2):
                        ch = 2 * chp + i2
                        r_av = P.step("tensor",
                            lambda e, ch=ch, hh=hh, pb=pb, i2=i2, nch=nch:
                            e.matmul(
                                ps_av[:, :],
                                vh[:, ch, 128 * hh:128 * hh + 128],
                                pp[pb][:, i2 * TSZ:(i2 + 1) * TSZ],
                                start=(ch == 0), stop=(ch == nch - 1)),
                            deps=[r_pf, r_vh_done[ch]] +
                                 ([r_avout] if ch == 0 else []))
                    r_ppuse[pb] = r_av
                r_qhat_use = r_av
                r_l2 = P.step("scalar", lambda e: e.activation(
                    rrow[0:1, :], ps_av[64:65, :], AF.Ln), deps=[r_av])
                r_e2 = P.step("scalar", lambda e: e.activation(
                    rrow[0:1, :], rrow[0:1, :], AF.Exp, scale=-1.0),
                    deps=[r_l2])
                r_rb_write = P.step("tensor", lambda e: e.matmul(
                    ps_rb[:, :], onesf[:, :], rrow[0:1, :],
                    start=True, stop=True), deps=[r_e2, r_onesf, r_rb_read])
                r_rb_read = P.step("vector", lambda e: e.tensor_copy(
                    rb_sb[:, :], ps_rb[0:64, :]), deps=[r_rb_write])
                r_avout = P.step("vector", lambda e, hh=hh, slot=slot,
                                 pair=pair: e.scalar_tensor_tensor(
                    attn_fm[64 * hh:64 * hh + 64, pair,
                            slot * TSZ:(slot + 1) * TSZ],
                    ps_av[0:64, :], 1.0, rb_sb[:, :], ALU.mult, ALU.mult),
                    deps=[r_rb_read])
    return r_avout
    return


def emit_proj_residual_ln(nc, P, st, *, src_fm, w_sb, r_w, resid_src,
                          resid_scale, out_hbm, gamma_beta=None, bias_sb=None,
                          r_gb=None, r_src=None, name=""):
    e_ = st.enter_context
    resid = e_(nc.sbuf_tensor(uname(name + "res"), [128, 8, TSZ], F32))
    resb = e_(nc.sbuf_tensor(uname(name + "resb"), [128, 8, TSZ], BF16))
    sqb = e_(nc.sbuf_tensor(uname(name + "sqb"), [128, 8, TSZ], BF16))
    xr = e_(nc.sbuf_tensor(uname(name + "xr"), [128, TSZ], F32))
    yout = e_(nc.sbuf_tensor(uname(name + "yout"), [128, 8, TSZ], F32))
    m_sb = e_(nc.sbuf_tensor(uname(name + "m"), [1, TSZ], F32))
    v_sb = e_(nc.sbuf_tensor(uname(name + "v"), [1, TSZ], F32))
    ones128 = e_(nc.sbuf_tensor(uname(name + "o128"), [128, 1], BF16))
    onesf1 = e_(nc.sbuf_tensor(uname(name + "of1"), [1, 128], F32))
    epsr = e_(nc.sbuf_tensor(uname(name + "eps"), [1, 1], F32))
    ps_o = [e_(nc.psum_tensor(uname(name + "ps_o"), [128, TSZ], F32))
            for _ in range(2)]
    ps_sum = e_(nc.psum_tensor(uname(name + "ps_sum"), [1, TSZ], F32))
    ps_sq = e_(nc.psum_tensor(uname(name + "ps_sq"), [1, TSZ], F32))
    ps_mb = e_(nc.psum_tensor(uname(name + "ps_mb"), [128, TSZ], F32))
    ps_rs = e_(nc.psum_tensor(uname(name + "ps_rs"), [128, TSZ], F32))

    r_o1 = P.step("vector", lambda e: e.memset(ones128[:, :], 1.0))
    r_o2 = P.step("vector", lambda e: e.memset(onesf1[:, :], 1.0))
    r_eps = P.step("vector", lambda e: e.memset(epsr[:, :], 1e-5))
    rs3 = fm3(resid_src)
    out3 = fm3(out_hbm)

    r_store = None
    r_residuse = None
    r_statuse = None
    for ot in range(4):
        r_xl = P.step("sync", lambda e, ot=ot: e.dma_start(
            out=resid[:, :, :], in_=rs3[:, :, ot * TSZ:(ot + 1) * TSZ]),
            deps=[r_residuse, r_store, r_src], dma=True)
        r_oc = {}
        r_mm = None
        for oc in range(8):
            pb = oc % 2
            for ic in range(8):
                r_mm = P.step("tensor", lambda e, oc=oc, ic=ic, ot=ot, pb=pb:
                    e.matmul(ps_o[pb][:, :],
                             w_sb[:, ic, oc * 128:(oc + 1) * 128],
                             src_fm[:, ic, ot * TSZ:(ot + 1) * TSZ],
                             start=(ic == 0), stop=(ic == 7)),
                    deps=[r_w, r_src, r_oc.get(oc - 2)] if ic == 0
                    else [r_w])
            if bias_sb is not None:
                r_bi = P.step("vector", lambda e, oc=oc, pb=pb:
                    e.tensor_scalar(
                        ps_o[pb][:, :], ps_o[pb][:, :],
                        bias_sb[:, oc:oc + 1], None,
                        ALU.add), deps=[r_mm, r_gb])
                r_mm = r_bi
            r_r = P.step("vector", lambda e, oc=oc, pb=pb:
                e.scalar_tensor_tensor(
                    resid[:, oc, :], resid[:, oc, :], resid_scale,
                    ps_o[pb][:, :],
                    ALU.mult, ALU.add), deps=[r_mm, r_xl])
            r_b = P.step("vector", lambda e, oc=oc: e.tensor_copy(
                resb[:, oc, :], resid[:, oc, :]), deps=[r_r])
            r_q = P.step("vector", lambda e, oc=oc: e.tensor_mul(
                sqb[:, oc, :], resb[:, oc, :], resb[:, oc, :]), deps=[r_b])
            r_oc[oc] = r_q
        r_s = None
        for oc in range(8):
            r_s = P.step("tensor", lambda e, oc=oc: e.matmul(
                ps_sum[:, :], ones128[:, :], resb[:, oc, :],
                start=(oc == 0), stop=(oc == 7)),
                deps=[r_oc[oc], r_o1] + ([r_statuse] if oc == 0 else []))
        r_s2 = None
        for oc in range(8):
            r_s2 = P.step("tensor", lambda e, oc=oc: e.matmul(
                ps_sq[:, :], ones128[:, :], sqb[:, oc, :],
                start=(oc == 0), stop=(oc == 7)),
                deps=[r_oc[oc], r_o1] + ([r_statuse] if oc == 0 else []))
        r_m = P.step("scalar", lambda e: e.activation(
            m_sb[:, :], ps_sum[0:1, :], AF.Copy, scale=1.0 / D), deps=[r_s])
        r_m2 = P.step("vector", lambda e: e.tensor_mul(
            xr[0:1, :], m_sb[:, :], m_sb[:, :]), deps=[r_m])
        r_v = P.step("vector", lambda e: e.scalar_tensor_tensor(
            v_sb[:, :], ps_sq[0:1, :], 1.0 / D, xr[0:1, :],
            ALU.mult, ALU.subtract), deps=[r_s2, r_m2])
        r_ln = P.step("scalar", lambda e: e.activation(
            v_sb[:, :], v_sb[:, :], AF.Ln, bias=epsr[0:1, 0:1]),
            deps=[r_v, r_eps])
        r_rs = P.step("scalar", lambda e: e.activation(
            v_sb[:, :], v_sb[:, :], AF.Exp, scale=-0.5), deps=[r_ln])
        r_bm = P.step("tensor", lambda e: e.matmul(
            ps_mb[:, :], onesf1[:, :], m_sb[:, :], start=True, stop=True),
            deps=[r_m, r_o2, r_statuse])
        r_br = P.step("tensor", lambda e: e.matmul(
            ps_rs[:, :], onesf1[:, :], v_sb[:, :], start=True, stop=True),
            deps=[r_rs, r_o2, r_statuse])
        r_yq = {}
        for oc in range(8):
            r_y1 = P.step("vector", lambda e, oc=oc: e.tensor_sub(
                yout[:, oc, :], resid[:, oc, :], ps_mb[:, :]),
                deps=[r_bm, r_oc[oc]])
            r_y2 = P.step("vector", lambda e, oc=oc: e.tensor_mul(
                yout[:, oc, :], yout[:, oc, :], ps_rs[:, :]), deps=[r_y1, r_br])
            if gamma_beta is not None:
                g_sb, b_sb = gamma_beta
                r_y2 = P.step("vector", lambda e, oc=oc: e.tensor_scalar(
                    yout[:, oc, :], yout[:, oc, :], g_sb[:, oc:oc + 1],
                    b_sb[:, oc:oc + 1], ALU.mult, ALU.add),
                    deps=[r_y2, r_gb])
            r_yq[oc] = r_y2
        r_statuse = r_yq[7]
        r_residuse = r_yq[7]
        r_store = P.step("sync", lambda e, ot=ot: e.dma_start(
            out=out3[:, :, ot * TSZ:(ot + 1) * TSZ], in_=yout[:, :, :]),
            deps=list(r_yq.values()), dma=True)
    return r_store


def make_ln_ctx(nc, e_, name):
    c = {}
    c["resb"] = e_(nc.sbuf_tensor(uname(name + "resb"), [128, 8, TSZ], BF16))
    c["sqb"] = e_(nc.sbuf_tensor(uname(name + "sqb"), [128, 8, TSZ], BF16))
    c["xr"] = e_(nc.sbuf_tensor(uname(name + "xr"), [1, TSZ], F32))
    c["m"] = e_(nc.sbuf_tensor(uname(name + "m"), [1, TSZ], F32))
    c["v"] = e_(nc.sbuf_tensor(uname(name + "v"), [1, TSZ], F32))
    c["o128"] = e_(nc.sbuf_tensor(uname(name + "o128"), [128, 1], BF16))
    c["of1"] = e_(nc.sbuf_tensor(uname(name + "of1"), [1, 128], F32))
    c["eps"] = e_(nc.sbuf_tensor(uname(name + "eps"), [1, 1], F32))
    c["ps_sum"] = e_(nc.psum_tensor(uname(name + "ps_sum"), [1, TSZ], F32))
    c["ps_sq"] = e_(nc.psum_tensor(uname(name + "ps_sq"), [1, TSZ], F32))
    c["ps_mb"] = e_(nc.psum_tensor(uname(name + "ps_mb"), [128, TSZ], F32))
    c["ps_rs"] = e_(nc.psum_tensor(uname(name + "ps_rs"), [128, TSZ], F32))
    c["init"] = [
        None
    ]
    r1 = None
    return c


def ln_ctx_init(nc, P, c):
    r1 = P.step("vector", lambda e: e.memset(c["o128"][:, :], 1.0))
    r2 = P.step("vector", lambda e: e.memset(c["of1"][:, :], 1.0))
    r3 = P.step("vector", lambda e: e.memset(c["eps"][:, :], 1e-5))
    c["rinit"] = [r1, r2, r3]
    c["rlast"] = None


def emit_ln_tile(nc, P, c, src, dst, deps, gamma_beta=None, r_gb=None):
    r_oc = {}
    for oc in range(8):
        r_b = P.step("vector", lambda e, oc=oc: e.tensor_copy(
            c["resb"][:, oc, :], src[:, oc, :]),
            deps=deps + c["rinit"] + [c["rlast"]])
        r_q = P.step("vector", lambda e, oc=oc: e.tensor_mul(
            c["sqb"][:, oc, :], c["resb"][:, oc, :], c["resb"][:, oc, :]),
            deps=[r_b])
        r_oc[oc] = r_q
    r_s = None
    for oc in range(8):
        r_s = P.step("tensor", lambda e, oc=oc: e.matmul(
            c["ps_sum"][:, :], c["o128"][:, :], c["resb"][:, oc, :],
            start=(oc == 0), stop=(oc == 7)),
            deps=[r_oc[oc]] + ([c["rlast"]] if oc == 0 else []))
    r_s2 = None
    for oc in range(8):
        r_s2 = P.step("tensor", lambda e, oc=oc: e.matmul(
            c["ps_sq"][:, :], c["o128"][:, :], c["sqb"][:, oc, :],
            start=(oc == 0), stop=(oc == 7)),
            deps=[r_oc[oc]] + ([c["rlast"]] if oc == 0 else []))
    r_m = P.step("scalar", lambda e: e.activation(
        c["m"][:, :], c["ps_sum"][0:1, :], AF.Copy, scale=1.0 / D),
        deps=[r_s, c["rlast"]])
    r_m2 = P.step("vector", lambda e: e.tensor_mul(
        c["xr"][:, :], c["m"][:, :], c["m"][:, :]), deps=[r_m])
    r_v = P.step("vector", lambda e: e.scalar_tensor_tensor(
        c["v"][:, :], c["ps_sq"][0:1, :], 1.0 / D, c["xr"][:, :],
        ALU.mult, ALU.subtract), deps=[r_s2, r_m2])
    r_ln = P.step("scalar", lambda e: e.activation(
        c["v"][:, :], c["v"][:, :], AF.Ln, bias=c["eps"][0:1, 0:1]),
        deps=[r_v])
    r_rs = P.step("scalar", lambda e: e.activation(
        c["v"][:, :], c["v"][:, :], AF.Exp, scale=-0.5), deps=[r_ln])
    r_bm = P.step("tensor", lambda e: e.matmul(
        c["ps_mb"][:, :], c["of1"][:, :], c["m"][:, :], start=True,
        stop=True), deps=[r_m, c["rlast"]])
    r_br = P.step("tensor", lambda e: e.matmul(
        c["ps_rs"][:, :], c["of1"][:, :], c["v"][:, :], start=True,
        stop=True), deps=[r_rs, c["rlast"]])
    r_yq = []
    for oc in range(8):
        r_y1 = P.step("vector", lambda e, oc=oc: e.tensor_sub(
            dst[:, oc, :], src[:, oc, :], c["ps_mb"][:, :]),
            deps=[r_bm, r_oc[oc]])
        r_y2 = P.step("vector", lambda e, oc=oc: e.tensor_mul(
            dst[:, oc, :], dst[:, oc, :], c["ps_rs"][:, :]),
            deps=[r_y1, r_br])
        if gamma_beta is not None:
            g_sb, b_sb = gamma_beta
            r_y2 = P.step("vector", lambda e, oc=oc: e.tensor_scalar(
                dst[:, oc, :], dst[:, oc, :], g_sb[:, oc:oc + 1],
                b_sb[:, oc:oc + 1], ALU.mult, ALU.add), deps=[r_y2, r_gb])
        r_yq.append(r_y2)
    c["rlast"] = r_yq[-1]
    return r_yq


def build_stageBmoe(fused=None):
    nc, P, _cache, _own = _begin(fused)
    _ctx = (nc, _cache, fused is not None)
    ymy = dram(_ctx, "ymy", [D, TCORE], F32, "ExternalInput")
    x1my = dram(_ctx, "x1my", [D, TCORE], F32, "ExternalInput")
    mgw = dram(_ctx, "mgw", [D, 8], F32, "ExternalInput")
    mgb = dram(_ctx, "mgb", [8, 1], F32, "ExternalInput")
    w1 = dram(_ctx, "w1", [8, D, 2048], F32, "ExternalInput")
    b1x = dram(_ctx, "b1x", [8, 128, 16], F32, "ExternalInput")
    lng = dram(_ctx, "lng", [8, 128, 16], F32, "ExternalInput")
    lnb = dram(_ctx, "lnb", [8, 128, 16], F32, "ExternalInput")
    w2 = dram(_ctx, "w2", [8, 2048, D], F32, "ExternalInput")
    b2x = dram(_ctx, "b2x", [8, 128, 8], F32, "ExternalInput")
    rsc = dram(_ctx, "rsc", [1, 8], F32, "ExternalInput")
    n2g = dram(_ctx, "n2g", [128, 8], F32, "ExternalInput")
    n2b = dram(_ctx, "n2b", [128, 8], F32, "ExternalInput")
    iota8 = dram(_ctx, "iota8", [128, 8], F32, "ExternalInput")
    ident = dram(_ctx, "ident", [128, 128], F32, "ExternalInput")
    x2my = dram(_ctx, "x2my", [D, TCORE], F32, "ExternalOutput")

    y3 = fm3(ymy)
    x13 = fm3(x1my)
    out3 = fm3(x2my)

    with ExitStack() as st:
        e_ = st.enter_context
        mgw_sb = e_(nc.sbuf_tensor(uname("mgw_sb"), [128, 8, 8], BF16))
        mgb_sb = e_(nc.sbuf_tensor(uname("mgb_sb"), [8, 1], F32))
        iota_sb = e_(nc.sbuf_tensor(uname("iota_sb"), [128, 8], F32))
        ident_sb = e_(nc.sbuf_tensor(uname("ident_sb"), [128, 128], F32))
        n2g_sb = e_(nc.sbuf_tensor(uname("n2g_sb"), [128, 8], F32))
        n2b_sb = e_(nc.sbuf_tensor(uname("n2b_sb"), [128, 8], F32))
        rsc_sb = e_(nc.sbuf_tensor(uname("rsc_sb"), [1, 8], F32))
        onesf1 = e_(nc.sbuf_tensor(uname("onesf1"), [1, 128], F32))
        cm_all = e_(nc.sbuf_tensor(uname("cm_all"), [128, 8, 16], F32))
        acc = e_(nc.sbuf_tensor(uname("acc"), [128, 8, TCORE], BF16))

        _bar = now_all(P) if not _own else []
        r_cst = [
            P.step("gpsimd", lambda e: e.dma_start(
                out=mgw_sb[:, :, :],
                in_=mgw.rearrange("(c p) o -> p c o", p=128)),
                deps=_bar, dma=True),
            P.step("sync", lambda e: e.dma_start(
                out=mgb_sb[:, :], in_=mgb[:, :]), deps=_bar, dma=True),
            P.step("sync", lambda e: e.dma_start(
                out=iota_sb[:, :], in_=iota8[:, :]), dma=True),
            P.step("sync", lambda e: e.dma_start(
                out=ident_sb[:, :], in_=ident[:, :]), dma=True),
            P.step("sync", lambda e: e.dma_start(
                out=n2g_sb[:, :], in_=n2g[:, :]), dma=True),
            P.step("sync", lambda e: e.dma_start(
                out=n2b_sb[:, :], in_=n2b[:, :]), dma=True),
            P.step("sync", lambda e: e.dma_start(
                out=rsc_sb[:, :], in_=rsc[:, :]), dma=True),
        ]
        r_of = P.step("vector", lambda e: e.memset(onesf1[:, :], 1.0))
        r_acc0 = P.step("vector", lambda e: e.memset(acc[:, :, :], 0.0))

        with ExitStack() as st_r:
            er = st_r.enter_context
            ybt = er(nc.sbuf_tensor(uname("ybt"), [128, 8, TSZ], BF16))
            lsb = er(nc.sbuf_tensor(uname("lsb"), [8, TSZ], F32))
            ltm = er(nc.sbuf_tensor(uname("ltm"), [128, 8], F32))
            e8 = er(nc.sbuf_tensor(uname("e8"), [128, 8], F32))
            t8 = er(nc.sbuf_tensor(uname("t8"), [128, 8], F32))
            vc1 = er(nc.sbuf_tensor(uname("vc1"), [128, 1], F32))
            vc2 = er(nc.sbuf_tensor(uname("vc2"), [128, 1], F32))
            chc = er(nc.sbuf_tensor(uname("chc"), [128, 1], F32))
            wsc = er(nc.sbuf_tensor(uname("wsc"), [128, 1], F32))
            tmpc = er(nc.sbuf_tensor(uname("tmpc"), [128, 1], F32))
            ps_l = er(nc.psum_tensor(uname("ps_l"), [8, TSZ], F32))
            ps_t = er(nc.psum_tensor(uname("ps_t"), [128, TSZ], F32))
            r_prev = None
            for ot in range(4):
                r_yl = P.step("gpsimd", lambda e, ot=ot: e.dma_start(
                    out=ybt[:, :, :], in_=y3[:, :, ot * TSZ:(ot + 1) * TSZ]),
                    deps=[r_prev], dma=True)
                r_mm = None
                for dc in range(8):
                    r_mm = P.step("tensor", lambda e, dc=dc: e.matmul(
                        ps_l[:, :], mgw_sb[:, dc, :], ybt[:, dc, :],
                        start=(dc == 0), stop=(dc == 7)),
                        deps=[r_yl, r_cst[0], r_prev])
                r_lb = P.step("vector", lambda e: e.tensor_scalar(
                    lsb[:, :], ps_l[0:8, :], mgb_sb[0:8, 0:1], None, ALU.add),
                    deps=[r_mm, r_cst[1]])
                for sub in range(4):
                    r_t = P.step("tensor", lambda e, sub=sub: e.transpose(
                        ps_t[:, 0:8], lsb[0:8, sub * 128:(sub + 1) * 128],
                        ident_sb[0:8, 0:8]), deps=[r_lb, r_cst[3], r_prev])
                    r_c = P.step("vector", lambda e: e.tensor_copy(
                        ltm[:, :], ps_t[:, 0:8]), deps=[r_t])
                    r_v1 = P.step("vector", lambda e: e.reduce_max(
                        vc1[:, :], ltm[:, :], axis=AX.X), deps=[r_c])
                    r_e1 = P.step("vector", lambda e: e.tensor_scalar(
                        e8[:, :], ltm[:, :], vc1[:, 0:1], None, ALU.is_equal),
                        deps=[r_v1])
                    r_t8 = P.step("vector", lambda e: e.scalar_tensor_tensor(
                        t8[:, :], e8[:, :], -1.0e38, ltm[:, :], ALU.mult,
                        ALU.add), deps=[r_e1])
                    r_v2 = P.step("vector", lambda e: e.reduce_max(
                        vc2[:, :], t8[:, :], axis=AX.X), deps=[r_t8])
                    r_e2 = P.step("vector", lambda e: e.tensor_scalar(
                        t8[:, :], t8[:, :], vc2[:, 0:1], None, ALU.is_equal),
                        deps=[r_v2])
                    r_es = P.step("vector", lambda e: e.tensor_add(
                        e8[:, :], e8[:, :], t8[:, :]), deps=[r_e2])
                    r_ei = P.step("vector", lambda e: e.tensor_mul(
                        e8[:, :], e8[:, :], iota_sb[:, :]),
                        deps=[r_es, r_cst[2]])
                    r_ch = P.step("vector", lambda e: e.reduce_max(
                        chc[:, :], e8[:, :], axis=AX.X), deps=[r_ei])
                    r_d = P.step("vector", lambda e: e.tensor_sub(
                        tmpc[:, :], vc2[:, :], vc1[:, :]), deps=[r_v2, r_v1])
                    r_ex = P.step("scalar", lambda e: e.activation(
                        tmpc[:, :], tmpc[:, :], AF.Exp), deps=[r_d])
                    r_d1 = P.step("vector", lambda e: e.tensor_scalar(
                        tmpc[:, :], tmpc[:, :], 1.0, None, ALU.add),
                        deps=[r_ex])
                    r_rc = P.step("vector", lambda e: e.reciprocal(
                        wsc[:, :], tmpc[:, :]), deps=[r_d1])
                    r_ws = P.step("vector", lambda e: e.tensor_mul(
                        wsc[:, :], wsc[:, :], tmpc[:, :]), deps=[r_rc])
                    r_last = None
                    for ex in range(8):
                        r_cm = P.step("vector", lambda e, ex=ex: e.tensor_scalar(
                            tmpc[:, :], chc[:, 0:1], float(ex), None,
                            ALU.is_equal), deps=[r_ch, r_ws, r_last])
                        r_last = P.step("vector",
                            lambda e, ex=ex, ot=ot, sub=sub:
                            e.scalar_tensor_tensor(
                                cm_all[:, ex, ot * 4 + sub:ot * 4 + sub + 1],
                                tmpc[:, :], 1.0, wsc[:, :], ALU.mult,
                                ALU.mult), deps=[r_cm])
                    r_prev = r_last
        r_router = r_prev

        st_e = ExitStack()
        e_ = st_e.enter_context
        we1 = e_(nc.sbuf_tensor(uname("we1"), [128, 8, 2048], BF16))
        we2 = e_(nc.sbuf_tensor(uname("we2"), [128, 16, D], BF16))
        b1_sb = e_(nc.sbuf_tensor(uname("b1_sb"), [128, 16], F32))
        lg_sb = e_(nc.sbuf_tensor(uname("lg_sb"), [128, 16], F32))
        lb_sb = e_(nc.sbuf_tensor(uname("lb_sb"), [128, 16], F32))
        b2_sb = e_(nc.sbuf_tensor(uname("b2_sb"), [128, 8], F32))
        b2r_sb = e_(nc.sbuf_tensor(uname("b2r_sb"), [128, 8], F32))
        rs1 = e_(nc.sbuf_tensor(uname("rs1"), [128, 1], F32))
        hbf = [e_(nc.sbuf_tensor(uname("hbf"), [128, 16, TSZ], BF16))
               for _ in range(2)]
        sqt = e_(nc.sbuf_tensor(uname("sqt"), [128, TSZ], BF16))
        gbf = e_(nc.sbuf_tensor(uname("gbf"), [128, 16, TSZ], BF16))
        osb = e_(nc.sbuf_tensor(uname("osb"), [128, 8, TSZ], F32))
        ybt2 = [e_(nc.sbuf_tensor(uname("ybt2"), [128, 8, TSZ], BF16))
                for _ in range(2)]
        yft = e_(nc.sbuf_tensor(uname("yft"), [128, 8, TSZ], F32))
        mkb = e_(nc.sbuf_tensor(uname("mkb"), [128, TSZ], F32))
        mrow_e = e_(nc.sbuf_tensor(uname("mrow_e"), [1, TSZ], F32))
        em_sb = e_(nc.sbuf_tensor(uname("em_sb"), [1, TSZ], F32))
        ev_sb = e_(nc.sbuf_tensor(uname("ev_sb"), [1, TSZ], F32))
        exr = e_(nc.sbuf_tensor(uname("exr"), [1, TSZ], F32))
        eeps = e_(nc.sbuf_tensor(uname("eeps"), [1, 1], F32))
        o128e = e_(nc.sbuf_tensor(uname("o128e"), [128, 1], BF16))
        ps_h = [e_(nc.psum_tensor(uname("ps_h"), [128, TSZ], F32))
                for _ in range(2)]
        ps_o2 = [e_(nc.psum_tensor(uname("ps_o2"), [128, TSZ], F32))
                 for _ in range(2)]
        ps_es = e_(nc.psum_tensor(uname("ps_es"), [1, TSZ], F32))
        ps_eq = e_(nc.psum_tensor(uname("ps_eq"), [1, TSZ], F32))
        ps_eb = e_(nc.psum_tensor(uname("ps_eb"), [128, TSZ], F32))
        ps_er = e_(nc.psum_tensor(uname("ps_er"), [128, TSZ], F32))
        ps_mk = ps_eb

        r_e0 = [P.step("vector", lambda e: e.memset(eeps[:, :], 1e-5)),
                P.step("vector", lambda e: e.memset(o128e[:, :], 1.0)),
                r_of, r_acc0]

        r_wuse = r_router

        def ld_we1(ex):
            return lambda e: e.dma_start(
                out=we1[:, :, :],
                in_=w1[ex, :, :].rearrange("(c p) f -> p c f", p=128))

        def ld_we2(ex, half):
            return lambda e: e.dma_start(
                out=we2[:, :, half * 512:(half + 1) * 512],
                in_=w2[ex, :, half * 512:(half + 1) * 512].rearrange(
                    "(c p) o -> p c o", p=128))

        r_w1 = P.step("gpsimd", ld_we1(0), deps=[r_router], dma=True)
        r_w2a = P.step("gpsimd", ld_we2(0, 0), deps=[r_router], dma=True)
        r_w2b = P.step("gpsimd", ld_we2(0, 1), deps=[r_router], dma=True)
        pre_yb = None
        r_ybl = {}
        r_hsl = {}
        r_accw = {ot: None for ot in range(4)}
        r_huse = None
        r_guse = None
        r_ouse = None
        r_yuse = None
        r_ruse = None
        for ex in range(8):
            r_bs = [
                P.step("sync", lambda e, ex=ex: e.dma_start(
                    out=b1_sb[:, :], in_=b1x[ex, :, :]), deps=[r_wuse],
                    dma=True),
                P.step("sync", lambda e, ex=ex: e.dma_start(
                    out=lg_sb[:, :], in_=lng[ex, :, :]), deps=[r_wuse],
                    dma=True),
                P.step("sync", lambda e, ex=ex: e.dma_start(
                    out=lb_sb[:, :], in_=lnb[ex, :, :]), deps=[r_wuse],
                    dma=True),
                P.step("sync", lambda e, ex=ex: e.dma_start(
                    out=b2_sb[:, :], in_=b2x[ex, :, :]), deps=[r_wuse],
                    dma=True),
            ]
            r_rb = P.step("tensor", lambda e, ex=ex: e.matmul(
                ps_mk[:, 0:1], onesf1[:, :], rsc_sb[0:1, ex:ex + 1],
                start=True, stop=True),
                deps=[r_cst[6], r_of, r_ruse])
            r_rs1 = P.step("vector", lambda e: e.tensor_copy(
                rs1[:, :], ps_mk[:, 0:1]), deps=[r_rb])
            r_b2r = P.step("vector", lambda e: e.tensor_scalar(
                b2r_sb[:, :], b2_sb[:, :], rs1[:, 0:1], None, ALU.mult),
                deps=[r_rs1, r_bs[3]])

            r_wuse_next = []

            def emit_h(ot, tb, ex=ex, r_w1=r_w1, pre_yb=pre_yb):
                if ot == 0 and pre_yb is not None:
                    r_yb = pre_yb
                else:
                    r_yb = P.step("gpsimd", lambda e, ot=ot, tb=tb:
                        e.dma_start(
                            out=ybt2[tb][:, :, :],
                            in_=y3[:, :, ot * TSZ:(ot + 1) * TSZ]),
                        deps=[r_ybl.get(tb)], dma=True)
                r_hs = {}
                r_mm = None
                for fc in range(16):
                    pb = fc % 2
                    for dc in range(8):
                        r_mm = P.step("tensor",
                            lambda e, fc=fc, dc=dc, pb=pb, tb=tb: e.matmul(
                                ps_h[pb][:, :],
                                we1[:, dc, fc * 128:(fc + 1) * 128],
                                ybt2[tb][:, dc, :],
                                start=(dc == 0), stop=(dc == 7)),
                            deps=[r_w1, r_yb, r_hs.get(fc - 2)]
                            if dc == 0 else [r_w1, r_yb])
                    r_hb = P.step("scalar", lambda e, fc=fc, pb=pb, tb=tb:
                        e.add(
                            hbf[tb][:, fc, :], ps_h[pb][:, :],
                            b1_sb[:, fc:fc + 1]),
                        deps=[r_mm, r_bs[0], r_hsl.get(tb)])
                    r_hs[fc] = r_hb
                return r_hs

            r_hs_next = emit_h(0, 0)
            for ot in range(4):
                tb = ot % 2
                r_hs = r_hs_next
                r_yf = P.step("sync", lambda e, ot=ot: e.dma_start(
                    out=yft[:, :, :], in_=y3[:, :, ot * TSZ:(ot + 1) * TSZ]),
                    deps=[r_yuse], dma=True)
                if ot + 1 < 4:
                    r_hs_next = emit_h(ot + 1, (ot + 1) % 2)
                r_s = None
                for fc in range(16):
                    r_s = P.step("tensor", lambda e, fc=fc, tb=tb: e.matmul(
                        ps_es[:, :], o128e[:, :], hbf[tb][:, fc, :],
                        start=(fc == 0), stop=(fc == 15)),
                        deps=[r_hs[fc], r_e0[1]] +
                             ([r_guse] if fc == 0 else []))
                r_sq = None
                r_squse = None
                for fc in range(16):
                    r_q = P.step("gpsimd", lambda e, fc=fc, tb=tb:
                        e.tensor_mul(
                            sqt[:, :], hbf[tb][:, fc, :], hbf[tb][:, fc, :]),
                        deps=[r_hs[fc], r_squse])
                    r_sq = P.step("tensor", lambda e, fc=fc: e.matmul(
                        ps_eq[:, :], o128e[:, :], sqt[:, :],
                        start=(fc == 0), stop=(fc == 15)),
                        deps=[r_q] + ([r_guse] if fc == 0 else []))
                    r_squse = r_sq
                r_m = P.step("scalar", lambda e: e.activation(
                    em_sb[:, :], ps_es[0:1, :], AF.Copy, scale=1.0 / 2048),
                    deps=[r_s, r_ruse])
                r_m2 = P.step("vector", lambda e: e.tensor_mul(
                    exr[:, :], em_sb[:, :], em_sb[:, :]), deps=[r_m])
                r_v = P.step("vector", lambda e: e.scalar_tensor_tensor(
                    ev_sb[:, :], ps_eq[0:1, :], 1.0 / 2048, exr[:, :],
                    ALU.mult, ALU.subtract), deps=[r_sq, r_m2])
                r_ln = P.step("scalar", lambda e: e.activation(
                    ev_sb[:, :], ev_sb[:, :], AF.Ln, bias=eeps[0:1, 0:1]),
                    deps=[r_v, r_e0[0]])
                r_rsd = P.step("scalar", lambda e: e.activation(
                    ev_sb[:, :], ev_sb[:, :], AF.Exp, scale=-0.5),
                    deps=[r_ln])
                r_bm = P.step("tensor", lambda e: e.matmul(
                    ps_eb[:, :], onesf1[:, :], em_sb[:, :], start=True,
                    stop=True), deps=[r_m, r_of, r_guse, r_rs1])
                r_br = P.step("tensor", lambda e: e.matmul(
                    ps_er[:, :], onesf1[:, :], ev_sb[:, :], start=True,
                    stop=True), deps=[r_rsd, r_of, r_guse])
                r_gs = {}
                for fc in range(16):
                    r_n1 = P.step("vector", lambda e, fc=fc, tb=tb:
                        e.tensor_sub(
                            gbf[:, fc, :], hbf[tb][:, fc, :], ps_eb[:, :]),
                        deps=[r_bm, r_hs[fc], r_guse])
                    r_n2 = P.step("vector", lambda e, fc=fc: e.tensor_mul(
                        gbf[:, fc, :], gbf[:, fc, :], ps_er[:, :]),
                        deps=[r_n1, r_br])
                    r_g = P.step("scalar", lambda e, fc=fc: e.activation(
                        gbf[:, fc, :], gbf[:, fc, :], AF.Gelu,
                        bias=lb_sb[:, fc:fc + 1], scale=lg_sb[:, fc:fc + 1]),
                        deps=[r_n2, r_bs[1], r_bs[2]])
                    r_gs[fc] = r_g
                r_huse = r_gs[15]
                r_hsl[tb] = r_gs[15]
                r_ybl[tb] = r_gs[15]
                r_os = {}
                for oc in range(8):
                    pb = oc % 2
                    r_w2h = r_w2a if oc < 4 else r_w2b
                    r_mm = None
                    for fc in range(16):
                        r_mm = P.step("tensor",
                            lambda e, oc=oc, fc=fc, pb=pb: e.matmul(
                                ps_o2[pb][:, :],
                                we2[:, fc, oc * 128:(oc + 1) * 128],
                                gbf[:, fc, :],
                                start=(fc == 0), stop=(fc == 15)),
                            deps=[r_w2h, r_gs[fc], r_os.get(oc - 2)]
                            if fc == 0 else [r_w2h, r_gs[fc]])
                    if ot == 3 and oc == 3:
                        r_w2a_free = r_mm
                    if ot == 3 and oc == 7:
                        r_w2b_free = r_mm
                    r_o1 = P.step("vector", lambda e, oc=oc, pb=pb:
                        e.tensor_scalar(
                            osb[:, oc, :], ps_o2[pb][:, :], rs1[:, 0:1],
                            b2r_sb[:, oc:oc + 1], ALU.mult, ALU.add),
                        deps=[r_mm, r_b2r, r_ouse])
                    r_o2 = P.step("vector", lambda e, oc=oc: e.tensor_add(
                        osb[:, oc, :], osb[:, oc, :], yft[:, oc, :]),
                        deps=[r_o1, r_yf])
                    r_os[oc] = r_o2
                r_yuse = r_os[7]
                r_guse = r_os[7]
                r_trs = []
                for sub in range(4):
                    r_tt = P.step("tensor", lambda e, ex=ex, ot=ot, sub=sub:
                        e.transpose(
                            ps_mk[0:1, sub * 128:sub * 128 + 128],
                            cm_all[:, ex, ot * 4 + sub:ot * 4 + sub + 1],
                            ident_sb[:, :]),
                        deps=[r_router, r_ruse, r_rs1, r_cst[3], r_os[7],
                              r_bm])
                    r_trs.append(r_tt)
                r_mr = P.step("vector", lambda e: e.tensor_copy(
                    mrow_e[0:1, :], ps_mk[0:1, :]), deps=r_trs)
                r_mk1 = P.step("tensor", lambda e: e.matmul(
                    ps_mk[:, :], onesf1[:, :], mrow_e[0:1, :],
                    start=True, stop=True),
                    deps=[r_mr, r_of])
                r_mk2 = P.step("vector", lambda e: e.tensor_copy(
                    mkb[:, :], ps_mk[:, :]), deps=[r_mk1])
                r_msk = {}
                for oc in range(8):
                    r_ms = P.step("vector", lambda e, oc=oc: e.tensor_mul(
                        osb[:, oc, :], osb[:, oc, :], mkb[:, :]),
                        deps=[r_mk2, r_os[oc]])
                    r_ac = P.step("vector", lambda e, oc=oc, ot=ot: e.tensor_add(
                        acc[:, oc, ot * TSZ:(ot + 1) * TSZ],
                        acc[:, oc, ot * TSZ:(ot + 1) * TSZ], osb[:, oc, :]),
                        deps=[r_ms, r_accw[ot]])
                    r_msk[oc] = r_ac
                r_ruse = r_mk2
                r_accw[ot] = r_msk[7]
                r_ouse = r_msk[7]
                r_wuse_next.append(r_msk[7])
                if ot == 3:
                    r_h_ot3 = r_hs
            if ex < 7:
                pre_yb = P.step("gpsimd", lambda e: e.dma_start(
                    out=ybt2[0][:, :, :], in_=y3[:, :, 0:TSZ]),
                    deps=[r_ybl.get(0)], dma=True)
                r_w1 = P.step("gpsimd", ld_we1(ex + 1),
                              deps=[r_h_ot3[15]], dma=True)
                r_w2a = P.step("gpsimd", ld_we2(ex + 1, 0),
                               deps=[r_w2a_free], dma=True)
                r_w2b = P.step("gpsimd", ld_we2(ex + 1, 1),
                               deps=[r_w2b_free], dma=True)
            r_wuse = [r_wuse_next[-1]]
        r_moe = r_accw
        st_e.close()
        e_ = st.enter_context

        with ExitStack() as st_f:
            ef = st_f.enter_context
            tt = ef(nc.sbuf_tensor(uname("tt"), [128, 8, TSZ], F32))
            t2 = ef(nc.sbuf_tensor(uname("t2"), [128, 8, TSZ], F32))
            x1t = ef(nc.sbuf_tensor(uname("x1t"), [128, 8, TSZ], F32))
            lc = make_ln_ctx(nc, ef, "fl")
            ln_ctx_init(nc, P, lc)
            r_st = None
            r_tu = None
            for ot in range(4):
                r_yf2 = P.step("sync", lambda e, ot=ot: e.dma_start(
                    out=tt[:, :, :], in_=y3[:, :, ot * TSZ:(ot + 1) * TSZ]),
                    deps=[r_moe[ot], r_tu, now_all(P) if ot == 0 else None],
                    dma=True)
                r_x1 = P.step("sync", lambda e, ot=ot: e.dma_start(
                    out=x1t[:, :, :], in_=x13[:, :, ot * TSZ:(ot + 1) * TSZ]),
                    deps=[r_tu], dma=True)
                r_ad = []
                for oc in range(8):
                    r_ad.append(P.step("vector",
                        lambda e, oc=oc, ot=ot: e.tensor_add(
                            tt[:, oc, :], tt[:, oc, :],
                            acc[:, oc, ot * TSZ:(ot + 1) * TSZ]),
                        deps=[r_yf2, r_moe[ot]]))
                r_l1 = emit_ln_tile(nc, P, lc, tt, tt, r_ad,
                                    gamma_beta=(n2g_sb, n2b_sb),
                                    r_gb=r_cst[4])
                r_a2 = []
                for oc in range(8):
                    r_a2.append(P.step("vector",
                        lambda e, oc=oc: e.tensor_add(
                            t2[:, oc, :], tt[:, oc, :], x1t[:, oc, :]),
                        deps=[r_l1[oc], r_x1]))
                r_l2 = emit_ln_tile(nc, P, lc, t2, t2, r_a2)
                r_st = P.step("sync", lambda e, ot=ot: e.dma_start(
                    out=out3[:, :, ot * TSZ:(ot + 1) * TSZ], in_=t2[:, :, :]),
                    deps=r_l2, dma=True)
                r_tu = r_st

        if _own:
            sems = {e: e_(nc.semaphore(uname("sem_" + e)))
                    for e in ENGINES}
            block = e_(nc.Block())
            P.finalize(block, sems)
    return nc


def vec_pc16(v):
    return np.ascontiguousarray(np.asarray(v, np.float32).reshape(16, 128).T)


def run_stageBmoe(y_fm, x1_fm, mgw_, mgb_, w1_, b1_, lng_, lnb_, w2_, b2_,
                  rsc_, n2g_, n2b_):
    if "Bmoe" not in _nc_cache:
        _nc_cache["Bmoe"] = build_stageBmoe()
    nc = _nc_cache["Bmoe"]
    iota8 = np.broadcast_to(np.arange(8, dtype=np.float32), (128, 8)).copy()
    ident = np.eye(128, dtype=np.float32)
    b1x = np.stack([vec_pc16(b1_[e]) for e in range(8)])
    lngx = np.stack([vec_pc16(lng_[e]) for e in range(8)])
    lnbx = np.stack([vec_pc16(lnb_[e]) for e in range(8)])
    b2x = np.stack([vec_pc(b2_[e]) for e in range(8)])
    in_maps = []
    for c in range(NCORES):
        b = c // 4
        cols = my_cols(c)
        in_maps.append({
            "ymy": np.ascontiguousarray(y_fm[b][:, cols]),
            "x1my": np.ascontiguousarray(x1_fm[b][:, cols]),
            "mgw": np.asarray(mgw_), "mgb": np.asarray(mgb_, np.float32).reshape(8, 1),
            "w1": np.asarray(w1_), "b1x": b1x, "lng": lngx, "lnb": lnbx,
            "w2": np.asarray(w2_), "b2x": b2x,
            "rsc": np.asarray(rsc_, np.float32).reshape(1, 8),
            "n2g": vec_pc(n2g_), "n2b": vec_pc(n2b_),
            "iota8": iota8, "ident": ident,
        })
    res = run_bass_kernel_spmd(nc, in_maps, core_ids=list(range(NCORES)))
    run_stageBmoe.last_results = res.results
    return scatter_from_cores([np.asarray(r["x2my"]) for r in res.results])


ROW_OF_TILE = [None] * NT
for _r in range(4):
    for _s in range(4):
        ROW_OF_TILE[TILES[_r][_s]] = 4 * _r + _s


def build_stageC(fused=None):
    nc, P, _cache, _own = _begin(fused)
    _ctx = (nc, _cache, fused is not None)
    x2my = dram(_ctx, "x2my", [D, TCORE], F32, "ExternalInput")
    a_pc = dram(_ctx, "a_pc", [128, 8], F32, "ExternalInput")
    b_pc = dram(_ctx, "b_pc", [128, 8], F32, "ExternalInput")
    woT = dram(_ctx, "woT", [D, D], F32, "ExternalInput")
    bo_pc = dram(_ctx, "bo_pc", [128, 8], F32, "ExternalInput")
    ident = dram(_ctx, "identC", [128, 128], F32, "ExternalInput")
    x3my = dram(_ctx, "x3my", [D, TCORE], F32, "ExternalOutput")

    u_in = dram(_ctx, "u_in", [128, 32], F32, "Internal")
    u_g = dram(_ctx, "u_g", [512, 32], F32, "Internal")
    groups = [[0, 1, 2, 3], [4, 5, 6, 7]]

    x23 = fm3(x2my)

    with ExitStack() as st:
        e_ = st.enter_context
        scn = e_(nc.sbuf_tensor(uname("scn"), [128, 8, 16, 128], F32))
        ysb = e_(nc.sbuf_tensor(uname("ysb"), [128, 8, TCORE], BF16))
        ah = e_(nc.sbuf_tensor(uname("ah"), [128, 8], F32))
        bb = e_(nc.sbuf_tensor(uname("bb"), [128, 8], F32))
        ahp = e_(nc.sbuf_tensor(uname("ahp"), [128, 8, 128], F32))
        a128 = e_(nc.sbuf_tensor(uname("a128"), [128, 8], F32))
        a512 = e_(nc.sbuf_tensor(uname("a512"), [128, 8], F32))
        umy = e_(nc.sbuf_tensor(uname("umy"), [128, 8, 4], F32))
        ugs = e_(nc.sbuf_tensor(uname("ugs"), [128, 4, 32], F32))
        tch = e_(nc.sbuf_tensor(uname("tch"), [128, 8, NT + 1], F32))
        cin = e_(nc.sbuf_tensor(uname("cin"), [128, 8, 4], F32))
        idc = e_(nc.sbuf_tensor(uname("idc"), [128, 128], F32))
        scn_sel = e_(nc.sbuf_tensor(uname("scn_sel"), [128, NT], F32))
        ps_t = e_(nc.psum_tensor(uname("ps_tC"), [128, TSZ], F32))

        _bar = now_all(P) if not _own else []
        r_a = P.step("sync", lambda e: e.dma_start(out=ah[:, :],
                     in_=a_pc[:, :]), deps=_bar, dma=True)
        r_b = P.step("sync", lambda e: e.dma_start(out=bb[:, :],
                     in_=b_pc[:, :]), deps=_bar, dma=True)
        r_id = P.step("sync", lambda e: e.dma_start(out=idc[:, :],
                      in_=ident[:, :]), deps=_bar, dma=True)
        r_ah = P.step("scalar", lambda e: e.activation(
            ah[:, :], ah[:, :], AF.Tanh), deps=[r_a])

        r_pw = {}
        for dc in range(8):
            r_p = P.step("vector", lambda e, dc=dc: e.tensor_copy(
                ahp[:, dc, 0:1], ah[:, dc:dc + 1]), deps=[r_ah])
            n = 1
            while n < 128:
                r_p = P.step("vector", lambda e, dc=dc, n=n: e.tensor_scalar(
                    ahp[:, dc, n:2 * n], ahp[:, dc, 0:n],
                    ahp[:, dc, n - 1:n], None, ALU.mult), deps=[r_p])
                n *= 2
            r_1 = P.step("vector", lambda e, dc=dc: e.tensor_scalar(
                a128[:, dc:dc + 1], ahp[:, dc, 127:128],
                ah[:, dc:dc + 1], None, ALU.mult), deps=[r_p])
            r_2 = P.step("vector", lambda e, dc=dc: e.tensor_mul(
                a512[:, dc:dc + 1], a128[:, dc:dc + 1], a128[:, dc:dc + 1]),
                deps=[r_1])
            r_2 = P.step("vector", lambda e, dc=dc: e.tensor_mul(
                a512[:, dc:dc + 1], a512[:, dc:dc + 1], a512[:, dc:dc + 1]),
                deps=[r_2])
            r_pw[dc] = (r_1, r_2)

        r_scan = {}
        r_umy = []
        for dc in range(8):
            r_x = P.step("sync", lambda e, dc=dc: e.dma_start(
                out=scn[:, dc, :, :].rearrange("p j l -> p (j l)"),
                in_=x23[:, dc, :]), dma=True)
            r_bx = P.step("vector", lambda e, dc=dc: e.tensor_scalar(
                scn[:, dc, :, :], scn[:, dc, :, :], bb[:, dc:dc + 1], None,
                ALU.mult), deps=[r_x, r_b])
            r_l = r_bx
            for l in range(1, 128):
                r_l = P.step("vector", lambda e, dc=dc, l=l:
                    e.scalar_tensor_tensor(
                        scn[:, dc, :, l:l + 1], scn[:, dc, :, l - 1:l],
                        ah[:, dc:dc + 1], scn[:, dc, :, l:l + 1],
                        ALU.mult, ALU.add), deps=[r_l] +
                        ([r_ah] if l == 1 else []))
            r_scan[dc] = r_l
            for sl in range(4):
                r_c = P.step("vector", lambda e, dc=dc, sl=sl: e.tensor_copy(
                    umy[:, dc, sl:sl + 1], scn[:, dc, 4 * sl, 127:128]),
                    deps=[r_l])
                for s_ in range(1, 4):
                    r_c = P.step("vector", lambda e, dc=dc, sl=sl, s_=s_:
                        e.scalar_tensor_tensor(
                            umy[:, dc, sl:sl + 1], umy[:, dc, sl:sl + 1],
                            a128[:, dc:dc + 1],
                            scn[:, dc, 4 * sl + s_, 127:128],
                            ALU.mult, ALU.add),
                        deps=[r_c, r_pw[dc][0]])
                r_umy.append(r_c)

        r_uw = P.step("sync", lambda e: e.dma_start(
            out=u_in[:, :], in_=umy[:, :, :].rearrange("p c s -> p (c s)")),
            deps=r_umy, dma=True)
        cc_sem = e_(nc.semaphore(uname("cc_u")))
        r_cc = P.collective(lambda e: e.collective_compute(
            "AllGather", ALU.bypass, replica_groups=groups,
            ins=[u_in[:, :].opt()], outs=[u_g[:, :].opt()]),
            deps=[r_uw], sem=cc_sem)
        r_u16 = P.step("sync", lambda e: e.dma_start(
            out=ugs[:, :, :], in_=u_g.rearrange("(r p) cs -> p r cs", p=128)),
            deps=[r_cc], dma=True)

        my_tiles = None
        selq = dram(_ctx, "selq", [128, NT, 4], F32, "ExternalInput")
        sel_sb = e_(nc.sbuf_tensor(uname("sel_sb"), [128, NT, 4], F32))
        r_sel = P.step("sync", lambda e: e.dma_start(
            out=sel_sb[:, :, :], in_=selq[:, :, :]), dma=True)
        r_fin = []
        for dc in range(8):
            r_t0 = P.step("vector", lambda e, dc=dc: e.memset(
                tch[:, dc, 0:1], 0.0), deps=[])
            r_tc = r_t0
            for j in range(NT):
                row = ROW_OF_TILE[j]
                rr, ss = row // 4, row % 4
                r_tc = P.step("vector", lambda e, dc=dc, j=j, rr=rr, ss=ss:
                    e.scalar_tensor_tensor(
                        tch[:, dc, j + 1:j + 2], tch[:, dc, j:j + 1],
                        a512[:, dc:dc + 1],
                        ugs[:, rr, dc * 4 + ss:dc * 4 + ss + 1],
                        ALU.mult, ALU.add),
                    deps=[r_tc, r_u16, r_pw[dc][1]])
            for sl in range(4):
                r_m = P.step("vector", lambda e, dc=dc, sl=sl: e.tensor_mul(
                    scn_sel[:, 0:NT], tch[:, dc, 0:NT], sel_sb[:, :, sl]),
                    deps=[r_tc, r_sel])
                r_cin = P.step("vector", lambda e, dc=dc, sl=sl: e.reduce_sum(
                    cin[:, dc, sl:sl + 1], scn_sel[:, 0:NT], axis=AX.X),
                    deps=[r_m])
                r_cc2 = r_cin
                for s_ in range(4):
                    js = 4 * sl + s_
                    r_co = P.step("vector", lambda e, dc=dc, js=js, sl=sl:
                        e.scalar_tensor_tensor(
                            scn[:, dc, js, :], ahp[:, dc, :],
                            cin[:, dc, sl:sl + 1], scn[:, dc, js, :],
                            ALU.mult, ALU.add),
                        deps=[r_cc2, r_scan[dc]])
                    if s_ < 3:
                        r_cc2 = P.step("vector", lambda e, dc=dc, js=js,
                                       sl=sl: e.tensor_copy(
                            cin[:, dc, sl:sl + 1], scn[:, dc, js, 127:128]),
                            deps=[r_co])
                    else:
                        r_fin.append(r_co)
            r_fin.append(P.step("vector", lambda e, dc=dc: e.tensor_copy(
                ysb[:, dc, :], scn[:, dc, :, :].rearrange(
                    "p j l -> p (j l)")), deps=[r_fin[-1]]))

        wo_sb = e_(nc.sbuf_tensor(uname("woT_sb"), [128, 8, D], BF16))
        bo_sb = e_(nc.sbuf_tensor(uname("bo_sb"), [128, 8], F32))
        r_wo = P.step("gpsimd", lambda e: e.dma_start(
            out=wo_sb[:, :, :], in_=woT.rearrange("(c p) o -> p c o", p=128)),
            dma=True)
        r_bo = P.step("sync", lambda e: e.dma_start(
            out=bo_sb[:, :], in_=bo_pc[:, :]), dma=True)
        with ExitStack() as st_post:
            emit_proj_residual_ln(nc, P, st_post, src_fm=ysb, w_sb=wo_sb,
                                  r_w=r_wo, resid_src=x2my, resid_scale=1.0,
                                  out_hbm=x3my, bias_sb=bo_sb, r_gb=r_bo,
                                  r_src=r_fin, name="Cp")

        if _own:
            sems = {e: e_(nc.semaphore(uname("sem_" + e)))
                    for e in ENGINES}
            block = e_(nc.Block())
            P.finalize(block, sems)
    return nc


def run_stageC(x2_fm, a_, b_, w_out_, b_out_):
    if "C" not in _nc_cache:
        _nc_cache["C"] = build_stageC()
    nc = _nc_cache["C"]
    woT = np.ascontiguousarray(np.asarray(w_out_, np.float32).T)
    ident = np.eye(128, dtype=np.float32)
    in_maps = []
    for c in range(NCORES):
        b = c // 4
        cols = my_cols(c)
        sel = np.zeros((NT, 4), np.float32)
        for sl in range(4):
            sel[core_tiles(c)[sl], sl] = 1.0
        selq = np.broadcast_to(sel, (128, NT, 4)).copy()
        in_maps.append({
            "x2my": np.ascontiguousarray(x2_fm[b][:, cols]),
            "a_pc": vec_pc(a_), "b_pc": vec_pc(b_),
            "woT": woT, "bo_pc": vec_pc(b_out_),
            "identC": ident, "selq": selq,
        })
    res = run_bass_kernel_spmd(nc, in_maps, core_ids=list(range(NCORES)))
    run_stageC.last_results = res.results
    return scatter_from_cores([np.asarray(r["x3my"]) for r in res.results])




def build_stageD(fused=None):
    nc, P, _cache, _own = _begin(fused)
    _ctx = (nc, _cache, fused is not None)
    x3my = dram(_ctx, "x3my", [D, TCORE], F32, "ExternalInput")
    wrT = dram(_ctx, "wrT", [D, 16], F32, "ExternalInput")
    gup = dram(_ctx, "gup", [17, D, 4096], F32, "ExternalInput")
    dwn = dram(_ctx, "dwn", [17, 2048, D], F32, "ExternalInput")
    ident = dram(_ctx, "identD", [128, 128], F32, "ExternalInput")
    utri = dram(_ctx, "utri", [128, 128], F32, "ExternalInput")
    iotac = dram(_ctx, "iotac", [128, 1], F32, "ExternalInput")
    iota16r = dram(_ctx, "iota16r", [128, 16], F32, "ExternalInput")
    x4my = dram(_ctx, "x4my", [D, TCORE], F32, "ExternalOutput")
    x3tm = dram(_ctx, "x3tm", [TCORE, D], BF16, "Internal")
    otm = dram(_ctx, "otm", [TCORE, D], BF16, "Internal")
    idxd = dram(_ctx, "idxd", [16 * NSLOT, 1], I32, "Internal")

    x33 = fm3(x3my)
    out3 = fm3(x4my)

    with ExitStack() as st:
        e_ = st.enter_context
        wr_sb = e_(nc.sbuf_tensor(uname("wr_sb"), [128, 8, 16], BF16))
        ident_sb = e_(nc.sbuf_tensor(uname("idD"), [128, 128], F32))
        identb = e_(nc.sbuf_tensor(uname("idDb"), [128, 128], BF16))
        utb = e_(nc.sbuf_tensor(uname("utb"), [128, 128], BF16))
        onesf1 = e_(nc.sbuf_tensor(uname("onesfD"), [1, 128], F32))
        o128b = e_(nc.sbuf_tensor(uname("o128D"), [128, 1], BF16))
        iota_sb = e_(nc.sbuf_tensor(uname("iotaD"), [128, 1], F32))
        i16_sb = e_(nc.sbuf_tensor(uname("i16D"), [128, 16], F32))
        cm_all = e_(nc.sbuf_tensor(uname("cmD"), [128, 16, 16], F32))
        eid_all = e_(nc.sbuf_tensor(uname("eidD"), [128, 16], F32))
        sumacc = e_(nc.sbuf_tensor(uname("sumacc"), [128, 8, TCORE], BF16))
        bigc = e_(nc.sbuf_tensor(uname("bigD"), [128, 1], F32))
        _bar = now_all(P) if not _own else []
        r_cst = [
            P.step("gpsimd", lambda e: e.dma_start(
                out=wr_sb[:, :, :],
                in_=wrT.rearrange("(c p) o -> p c o", p=128)),
                deps=_bar, dma=True),
            P.step("sync", lambda e: e.dma_start(
                out=ident_sb[:, :], in_=ident[:, :]), deps=_bar, dma=True),
            P.step("gpsimd", lambda e: e.dma_start(
                out=utb[:, :], in_=utri[:, :]), deps=_bar, dma=True),
            P.step("sync", lambda e: e.dma_start(
                out=iota_sb[:, :], in_=iotac[:, :]), deps=_bar, dma=True),
            P.step("sync", lambda e: e.dma_start(
                out=i16_sb[:, :], in_=iota16r[:, :]), deps=_bar, dma=True),
        ]
        r_idb = P.step("vector", lambda e: e.tensor_copy(
            identb[:, :], ident_sb[:, :]), deps=[r_cst[1]])
        r_of = P.step("vector", lambda e: e.memset(onesf1[:, :], 1.0))
        r_o128 = P.step("vector", lambda e: e.memset(o128b[:, :], 1.0))
        r_big = P.step("vector", lambda e: e.memset(bigc[:, :], 131072.0))

        ibig = e_(nc.sbuf_tensor(uname("ibig"), [128, 64], I32))
        flb = e_(nc.sbuf_tensor(uname("flbD"), [128, 4], I32))
        zb = e_(nc.sbuf_tensor(uname("zbD"), [128, D], BF16))
        r_ib = P.step("vector", lambda e: e._memset_packed(
            ibig[:, :], 1 << 29), deps=_bar)
        _regs = {}

        def _bcreg(e, v):
            if v not in _regs:
                _regs[v] = e.to_reg(v)
            return _regs[v]
        r_idxinit = P.step("sync", lambda e: e.dma_start(
            out=idxd.rearrange("(p c) o -> p (c o)", p=128),
            in_=ibig[:, :]), deps=[r_ib], dma=True)
        r_zb = P.step("vector", lambda e: e.memset(zb[:, :], 0.0),
                      deps=_bar)
        r_otz = []
        for sub in range(16):
            r_otz.append(P.step("sync", lambda e, sub=sub: e.dma_start(
                out=otm[sub * 128:(sub + 1) * 128, :], in_=zb[:, :]),
                deps=[r_zb], dma=True))

        with ExitStack() as st_r:
            er = st_r.enter_context
            xbt = er(nc.sbuf_tensor(uname("xbtD"), [128, 8, TSZ], BF16))
            lsb = er(nc.sbuf_tensor(uname("lsbD"), [16, TSZ], F32))
            ltm = er(nc.sbuf_tensor(uname("ltmD"), [128, 16], F32))
            e16 = er(nc.sbuf_tensor(uname("e16D"), [128, 16], F32))
            vc1 = er(nc.sbuf_tensor(uname("vc1D"), [128, 1], F32))
            ssc = er(nc.sbuf_tensor(uname("sscD"), [128, 1], F32))
            tmst = er(nc.sbuf_tensor(uname("tmstD"), [128, 8, 128], BF16))
            ps_l = er(nc.psum_tensor(uname("ps_lD"), [16, TSZ], F32))
            ps_t = er(nc.psum_tensor(uname("ps_tD"), [128, TSZ], F32))
            ps_tb = er(nc.psum_tensor(uname("ps_tbD"), [128, 1024], BF16))
            r_prev = None
            r_tmprev = None
            for ot in range(4):
                r_xl = P.step("gpsimd", lambda e, ot=ot: e.dma_start(
                    out=xbt[:, :, :], in_=x33[:, :, ot * TSZ:(ot + 1) * TSZ]),
                    deps=[r_prev, r_tmprev], dma=True)
                r_mm = None
                for dc in range(8):
                    r_mm = P.step("tensor", lambda e, dc=dc: e.matmul(
                        ps_l[:, :], wr_sb[:, dc, :], xbt[:, dc, :],
                        start=(dc == 0), stop=(dc == 7)),
                        deps=[r_xl, r_cst[0], r_prev])
                r_lb = P.step("vector", lambda e: e.tensor_copy(
                    lsb[:, :], ps_l[0:16, :]), deps=[r_mm])
                for sub in range(4):
                    chunk = ot * 4 + sub
                    r_t = P.step("tensor", lambda e, sub=sub: e.transpose(
                        ps_t[:, 0:16], lsb[0:16, sub * 128:(sub + 1) * 128],
                        ident_sb[0:16, 0:16]), deps=[r_lb, r_cst[1], r_prev])
                    r_c = P.step("vector", lambda e: e.tensor_copy(
                        ltm[:, :], ps_t[:, 0:16]), deps=[r_t])
                    r_v1 = P.step("vector", lambda e: e.reduce_max(
                        vc1[:, :], ltm[:, :], axis=AX.X), deps=[r_c])
                    r_e1 = P.step("vector", lambda e: e.tensor_scalar(
                        e16[:, :], ltm[:, :], vc1[:, 0:1], None,
                        ALU.is_equal), deps=[r_v1])
                    r_cm = P.step("vector", lambda e, chunk=chunk:
                        e.tensor_copy(
                            cm_all[:, 0:16, chunk:chunk + 1],
                            e16[:, :]), deps=[r_e1, r_prev])
                    r_ei = P.step("vector", lambda e: e.tensor_mul(
                        ltm[:, :], e16[:, :], i16_sb[:, :]),
                        deps=[r_e1, r_cst[4]])
                    r_eid = P.step("vector", lambda e, chunk=chunk:
                        e.reduce_max(
                            eid_all[:, chunk:chunk + 1], ltm[:, :],
                            axis=AX.X), deps=[r_ei])
                    r_sx = P.step("scalar", lambda e: e.activation(
                        ssc[:, :], vc1[:, :], AF.Exp, scale=-1.0),
                        deps=[r_v1, r_tmprev])
                    r_s1 = P.step("vector", lambda e: e.tensor_scalar(
                        ssc[:, :], ssc[:, :], 1.0, None, ALU.add),
                        deps=[r_sx])
                    r_s2 = P.step("vector", lambda e: e.reciprocal(
                        ssc[:, :], ssc[:, :]), deps=[r_s1])
                    r_trs = []
                    for dc in range(8):
                        r_trs.append(P.step("tensor",
                            lambda e, dc=dc, sub=sub: e.transpose(
                                ps_tb[:, dc * 128:(dc + 1) * 128],
                                xbt[:, dc, sub * 128:(sub + 1) * 128],
                                identb[:, :]),
                            deps=[r_xl, r_idb, r_tmprev] +
                                 ([r_t, r_prev] if dc == 0 else [])))
                    r_ev = P.step("vector", lambda e: e.tensor_scalar(
                        tmst[:, :, :],
                        ps_tb[:, :].rearrange("p (c f) -> p c f", c=8),
                        ssc[:, 0:1], None, ALU.mult),
                        deps=r_trs + [r_s2])
                    r_tmprev = P.step("sync", lambda e, chunk=chunk:
                        e.dma_start(
                            out=x3tm[chunk * 128:(chunk + 1) * 128, :],
                            in_=tmst[:, :, :].rearrange("p c f -> p (c f)")),
                        deps=[r_ev], dma=True)
                    r_prev = r_ev
        r_router = [r_prev, r_tmprev]

        with ExitStack() as st_s:
            es = st_s.enter_context
            cmb = es(nc.sbuf_tensor(uname("cmbD"), [128, 16], BF16))
            sl16 = es(nc.sbuf_tensor(uname("sl16D"), [128, 16], F32))
            t16 = es(nc.sbuf_tensor(uname("t16D"), [128, 16], F32))
            mslc = es(nc.sbuf_tensor(uname("mslD"), [128, 1], F32))
            gcol = es(nc.sbuf_tensor(uname("gcolD"), [128, 1], F32))
            smc = es(nc.sbuf_tensor(uname("smcD"), [128, 1], F32))
            tdc = es(nc.sbuf_tensor(uname("tdcD"), [128, 1], F32))
            rphy = es(nc.sbuf_tensor(uname("rphyD"), [128, 1], F32))
            tokc = es(nc.sbuf_tensor(uname("tokcD"), [128, 1], F32))
            ri = es(nc.sbuf_tensor(uname("riD"), [128, 16], I32))
            ti = es(nc.sbuf_tensor(uname("tiD"), [128, 16], I32))
            basr = es(nc.sbuf_tensor(uname("basrD"), [1, 16], F32))
            basm1 = es(nc.sbuf_tensor(uname("basm1D"), [1, 16], F32))
            totr = es(nc.sbuf_tensor(uname("totrD"), [1, 16], F32))
            ps_pf = es(nc.psum_tensor(uname("ps_pfD"), [128, 16], F32))
            ps_tt = es(nc.psum_tensor(uname("ps_ttD"), [1, 16], F32))
            ps_bb = es(nc.psum_tensor(uname("ps_bbD"), [128, 16], F32))
            bb16 = es(nc.sbuf_tensor(uname("bb16D"), [128, 16], F32))
            r_bas = P.step("vector", lambda e: e.memset(basr[:, :], 0.0))
            r_sc = None
            r_pfuse = None
            r_ttuse = None
            r_ttlast = None
            for chunk in range(16):
                r_cmb = P.step("vector", lambda e, chunk=chunk:
                    e.tensor_copy(cmb[:, :],
                                  cm_all[:, 0:16, chunk:chunk + 1]),
                    deps=r_router + [r_sc, r_ttlast])
                r_pf = None
                for ex in range(16):
                    r_pf = P.step("tensor", lambda e, ex=ex: e.matmul(
                        ps_pf[:, ex:ex + 1], utb[:, :], cmb[:, ex:ex + 1],
                        start=True, stop=True),
                        deps=[r_cmb, r_cst[2], r_pfuse])
                r_tt = None
                for ex in range(16):
                    r_tt = P.step("tensor", lambda e, ex=ex: e.matmul(
                        ps_tt[0:1, ex:ex + 1], o128b[:, :],
                        cmb[:, ex:ex + 1],
                        start=True, stop=True),
                        deps=[r_cmb, r_o128, r_ttuse])
                r_ttlast = r_tt
                r_bm1 = P.step("vector", lambda e: e.tensor_scalar(
                    basm1[:, :], basr[:, :], 1.0, None, ALU.subtract),
                    deps=[r_bas])
                r_bbm = P.step("tensor", lambda e: e.matmul(
                    ps_bb[:, :], onesf1[:, :], basm1[:, :],
                    start=True, stop=True), deps=[r_bm1, r_of, r_pfuse])
                r_bbc = P.step("vector", lambda e: e.tensor_copy(
                    bb16[:, :], ps_bb[:, :]), deps=[r_bbm])
                r_sl = [P.step("vector", lambda e: e.tensor_add(
                    sl16[:, :], bb16[:, :], ps_pf[:, :]),
                    deps=[r_pf, r_bbc])]
                r_pfuse = r_sl[-1]
                r_tr = P.step("vector", lambda e: e.tensor_copy(
                    totr[:, :], ps_tt[0:1, :]), deps=[r_tt])
                r_ttuse = r_tr
                r_bas = P.step("vector", lambda e: e.tensor_add(
                    basr[:, :], basr[:, :], totr[:, :]),
                    deps=[r_tr, r_bm1])
                r_t1 = P.step("vector", lambda e, chunk=chunk: e.tensor_mul(
                    t16[:, :], cm_all[:, 0:16, chunk:chunk + 1], sl16[:, :]),
                    deps=r_sl)
                r_msl = P.step("vector", lambda e: e.reduce_sum(
                    mslc[:, :], t16[:, :], axis=AX.X), deps=[r_t1])
                r_g = P.step("vector", lambda e: e.tensor_scalar(
                    gcol[:, :], mslc[:, :], float(NSLOT), None, ALU.is_lt),
                    deps=[r_msl])
                r_q1 = P.step("vector", lambda e: e.tensor_scalar(
                    smc[:, :], mslc[:, :], 128.0, None, ALU.is_ge),
                    deps=[r_msl])
                r_q2 = P.step("vector", lambda e: e.tensor_scalar(
                    tdc[:, :], mslc[:, :], 256.0, None, ALU.is_ge),
                    deps=[r_msl])
                r_q3 = P.step("vector", lambda e: e.tensor_add(
                    smc[:, :], smc[:, :], tdc[:, :]), deps=[r_q1, r_q2])
                r_q4 = P.step("vector", lambda e: e.tensor_scalar(
                    tdc[:, :], mslc[:, :], 384.0, None, ALU.is_ge),
                    deps=[r_msl, r_q3])
                r_q5 = P.step("vector", lambda e: e.tensor_add(
                    smc[:, :], smc[:, :], tdc[:, :]), deps=[r_q3, r_q4])
                r_qm = P.step("vector", lambda e: e.tensor_scalar(
                    smc[:, :], smc[:, :], -511.0, None, ALU.mult),
                    deps=[r_q5])
                r_rp = P.step("vector", lambda e: e.scalar_tensor_tensor(
                    rphy[:, :], mslc[:, :], 4.0, smc[:, :], ALU.mult,
                    ALU.add), deps=[r_qm])
                r_rp2 = P.step("vector", lambda e, chunk=chunk:
                    e.scalar_tensor_tensor(
                        rphy[:, :], eid_all[:, chunk:chunk + 1],
                        float(NSLOT), rphy[:, :], ALU.mult, ALU.add),
                    deps=[r_rp])
                r_rm = P.step("vector", lambda e: e.tensor_scalar(
                    rphy[:, :], rphy[:, :], 131072.0, None, ALU.subtract),
                    deps=[r_rp2])
                r_rf = P.step("vector", lambda e: e.scalar_tensor_tensor(
                    smc[:, :], gcol[:, :], rphy[:, 0:1], bigc[:, :],
                    ALU.mult, ALU.add), deps=[r_rm, r_g, r_big])
                r_tk = P.step("vector", lambda e, chunk=chunk:
                    e.tensor_scalar(
                        tokc[:, :], iota_sb[:, :], float(128 * chunk),
                        None, ALU.add), deps=[r_cst[3], r_sc])
                r_i1 = P.step("vector", lambda e, chunk=chunk:
                    e.tensor_copy(
                        ri[:, chunk:chunk + 1], smc[:, :]), deps=[r_rf])
                r_i2 = P.step("vector", lambda e, chunk=chunk:
                    e.tensor_copy(
                        ti[:, chunk:chunk + 1], tokc[:, :]), deps=[r_tk])
                r_sc = P.step("gpsimd", lambda e, chunk=chunk:
                    e.indirect_dma_start(
                        idxd[0:128, :],
                        IndirectOffsetOnAxis(ri[:, chunk:chunk + 1], 0),
                        ti[:, chunk:chunk + 1], None,
                        bounds_check=_bcreg(e, 16 * NSLOT),
                        oob_is_err=False),
                    deps=[r_i1, r_i2, r_idxinit], dma=True)
            r_idx_done = P.step("gpsimd", lambda e: e.dma_start(
                out=flb[:, :], in_=ibig[:, 0:4]), deps=[r_sc], dma=True)

        st_e = ExitStack()
        ee = st_e.enter_context
        wgA = ee(nc.sbuf_tensor(uname("wgAD"), [128, 8, 2048], BF16))
        wgB = ee(nc.sbuf_tensor(uname("wgBD"), [128, 8, 2048], BF16))
        wd_sb = ee(nc.sbuf_tensor(uname("wdD"), [128, 16, D], BF16))
        mxin = [ee(nc.sbuf_tensor(uname("mxinD"), [128, D], BF16))
                for _ in range(4)]
        mxout = [ee(nc.sbuf_tensor(uname("mxoutD"), [128, D], BF16))
                 for _ in range(4)]
        mxfm = ee(nc.sbuf_tensor(uname("mxfmD"), [128, 8, TSZ], BF16))
        gsb = ee(nc.sbuf_tensor(uname("gsbD"), [128, 2, TSZ], BF16))
        usb = ee(nc.sbuf_tensor(uname("usbD"), [128, 16, TSZ], BF16))
        ofm = ee(nc.sbuf_tensor(uname("ofmD"), [128, 8, TSZ], BF16))
        idx_all = ee(nc.sbuf_tensor(uname("idxaD"), [128, 64], I32))
        flsh = ee(nc.sbuf_tensor(uname("flshD"), [128, 4], I32))
        ps_g = [ee(nc.psum_tensor(uname("ps_gD"), [128, TSZ], F32))
                for _ in range(2)]
        ps_u = [ee(nc.psum_tensor(uname("ps_uD"), [128, TSZ], F32))
                for _ in range(2)]
        ps_d = [ee(nc.psum_tensor(uname("ps_dD"), [128, TSZ], F32))
                for _ in range(2)]
        ps_tb2 = ee(nc.psum_tensor(uname("ps_tb2D"), [128, 1024], BF16))
        ps_tb3 = ee(nc.psum_tensor(uname("ps_tb3D"), [128, 1024], BF16))

        def ld_wgA(ex):
            return lambda e: e.dma_start(
                out=wgA[:, :, :],
                in_=gup[ex, :, 0:2048].rearrange("(c p) f -> p c f", p=128))

        def ld_wgB(ex):
            return lambda e: e.dma_start(
                out=wgB[:, :, :],
                in_=gup[ex, :, 2048:4096].rearrange("(c p) f -> p c f",
                                                    p=128))

        def ld_wd(ex):
            return lambda e: e.dma_start(
                out=wd_sb[:, :, :],
                in_=dwn[ex, :, :].rearrange("(c p) o -> p c o", p=128))

        r_wgA = P.step("gpsimd", ld_wgA(0), deps=_bar, dma=True)
        r_wgB = P.step("gpsimd", ld_wgB(0), deps=_bar, dma=True)
        r_wd = P.step("gpsimd", ld_wd(0), deps=_bar, dma=True)
        r_ia = P.step("sync", lambda e: e.dma_start(
            out=idx_all[:, :].rearrange("p (x c) -> p x c", x=16),
            in_=idxd.rearrange("(x p c) o -> p x (c o)", x=16, p=128)),
            deps=[r_idx_done], dma=True)
        r_mz = [P.step("vector", lambda e, j=j: e.memset(
            mxin[j][:, :], 0.0), deps=_bar) for j in range(4)]

        def emit_gathers(ex, deps):
            r_g = []
            for j in range(4):
                r_g.append(P.step("gpsimd", lambda e, ex=ex, j=j:
                    e.indirect_dma_start(
                        mxin[j][:, :], None, x3tm[0:128, :],
                        IndirectOffsetOnAxis(
                            idx_all[:, ex * 4 + j:ex * 4 + j + 1], 0),
                        bounds_check=_bcreg(e, TCORE), oob_is_err=False),
                    deps=deps + [r_ia], dma=True))
            r_f = P.step("gpsimd", lambda e: e.dma_start(
                out=flsh[:, :], in_=idx_all[:, 0:4]), deps=[r_g[3]],
                dma=True)
            return r_g, r_f

        r_gth, r_flush = emit_gathers(0, r_mz)
        r_tfm_prev = []
        r_up_last_prev = None
        r_silu = [None, None]
        r_umul = [None, None]
        r_duse = [None, None]
        r_tt2_prev = []
        r_sct_prev = []
        r_ottail = list(r_otz)
        r_sct = []
        for ex in range(16):
            if DBG_LOOPBACK:
                r_sct = []
                for j in range(4):
                    r_sct.append(P.step("gpsimd", lambda e, ex=ex, j=j:
                        e.indirect_dma_start(
                            otm[0:128, :], IndirectOffsetOnAxis(
                                idx_all[:, ex * 4 + j:ex * 4 + j + 1], 0),
                            mxin[j][:, :], None,
                            bounds_check=_bcreg(e, TCORE),
                            oob_is_err=False),
                        deps=[r_gth[j]] + r_ottail, dma=True))
                r_ottail = []
                if ex < 15:
                    r_gth, r_flush = emit_gathers(ex + 1, r_sct)
                continue
            r_tfm = []
            for j in range(4):
                r_trs = []
                for dc in range(8):
                    r_trs.append(P.step("tensor",
                        lambda e, j=j, dc=dc: e.transpose(
                            ps_tb2[:, dc * 128:(dc + 1) * 128],
                            mxin[j][:, dc * 128:(dc + 1) * 128],
                            identb[:, :]),
                        deps=[r_flush, r_idb] +
                             ([r_tfm[-1]] if (dc == 0 and r_tfm) else
                              ([r_tfm_prev[-1]] if (dc == 0 and r_tfm_prev)
                               else []))))
                r_tfm.append(P.step("vector", lambda e, j=j: e.tensor_copy(
                    mxfm[:, :, j * 128:(j + 1) * 128],
                    ps_tb2[:, :].rearrange("p (c f) -> p c f", c=8)),
                    deps=r_trs + [r_up_last_prev]))
            if ex < 15:
                r_gth, r_flush = emit_gathers(ex + 1, list(r_tfm))
            r_gate_last = None
            r_up_last = None
            for fc in range(16):
                pb = fc % 2
                r_mm = None
                for dc in range(8):
                    r_mm = P.step("tensor",
                        lambda e, fc=fc, dc=dc, pb=pb: e.matmul(
                            ps_g[pb][:, :],
                            wgA[:, dc, fc * 128:(fc + 1) * 128],
                            mxfm[:, dc, :], start=(dc == 0),
                            stop=(dc == 7)),
                        deps=[r_wgA, r_tfm, r_silu[pb]]
                        if dc == 0 else [r_wgA])
                r_gate_last = r_mm
                r_silu[pb] = P.step("scalar", lambda e, fc=fc, pb=pb:
                    e.activation(gsb[:, pb, :], ps_g[pb][:, :], AF.Silu),
                    deps=[r_mm, r_umul[pb]])
                r_mm = None
                for dc in range(8):
                    r_mm = P.step("tensor",
                        lambda e, fc=fc, dc=dc, pb=pb: e.matmul(
                            ps_u[pb][:, :],
                            wgB[:, dc, fc * 128:(fc + 1) * 128],
                            mxfm[:, dc, :], start=(dc == 0),
                            stop=(dc == 7)),
                        deps=[r_wgB, r_tfm, r_umul[pb]]
                        if dc == 0 else [r_wgB])
                r_up_last = r_mm
                r_umul[pb] = P.step("vector", lambda e, fc=fc, pb=pb:
                    e.tensor_mul(usb[:, fc, :], gsb[:, pb, :],
                                 ps_u[pb][:, :]),
                    deps=[r_mm, r_silu[pb]])
            r_up_last_prev = r_up_last
            r_wgA = P.step("gpsimd", ld_wgA(ex + 1), deps=[r_gate_last],
                           dma=True)
            r_wgB = P.step("gpsimd", ld_wgB(ex + 1), deps=[r_up_last],
                           dma=True)
            r_osb = []
            r_down_last = None
            for oc in range(8):
                pb = oc % 2
                r_mm = None
                for fc in range(16):
                    r_mm = P.step("tensor",
                        lambda e, oc=oc, fc=fc, pb=pb: e.matmul(
                            ps_d[pb][:, :],
                            wd_sb[:, fc, oc * 128:(oc + 1) * 128],
                            usb[:, fc, :], start=(fc == 0),
                            stop=(fc == 15)),
                        deps=[r_wd, r_umul[0], r_umul[1], r_duse[pb]]
                        if fc == 0 else [r_wd])
                r_down_last = r_mm
                r_o = P.step("vector", lambda e, oc=oc, pb=pb:
                    e.tensor_copy(ofm[:, oc, :], ps_d[pb][:, :]),
                    deps=[r_mm])
                r_duse[pb] = r_o
                r_osb.append(r_o)
            r_wd = P.step("gpsimd", ld_wd(ex + 1), deps=[r_down_last],
                          dma=True)
            r_sct = []
            r_tt2s = []
            for j in range(4):
                r_trs = []
                for dc in range(8):
                    r_trs.append(P.step("tensor",
                        lambda e, j=j, dc=dc: e.transpose(
                            ps_tb3[:, dc * 128:(dc + 1) * 128],
                            ofm[:, dc, j * 128:(j + 1) * 128],
                            identb[:, :]),
                        deps=r_osb + [r_idb] +
                             ([r_tt2s[-1]] if (dc == 0 and r_tt2s) else
                              ([r_tt2_prev[-1]]
                               if (dc == 0 and r_tt2_prev) else []))))
                r_tt2 = P.step("vector", lambda e, j=j: e.tensor_copy(
                    mxout[j][:, :],
                    ps_tb3[:, :].rearrange("p (c f) -> p c f", c=8)),
                    deps=r_trs +
                         ([r_sct_prev[j]] if r_sct_prev else []))
                r_tt2s.append(r_tt2)
                r_sct.append(P.step("gpsimd", lambda e, ex=ex, j=j:
                    e.indirect_dma_start(
                        otm[0:128, :], IndirectOffsetOnAxis(
                            idx_all[:, ex * 4 + j:ex * 4 + j + 1], 0),
                        mxout[j][:, :], None,
                        bounds_check=_bcreg(e, TCORE), oob_is_err=False),
                    deps=[r_tt2] + r_ottail, dma=True))
            r_ottail = []
            r_tt2_prev = r_tt2s
            r_sct_prev = r_sct
            r_tfm_prev = r_tfm
        r_sparse_done = P.step("gpsimd", lambda e: e.dma_start(
            out=flb[:, :], in_=ibig[:, 0:4]), deps=r_sct, dma=True)

        x3b = ee(nc.sbuf_tensor(uname("x3bD"), [128, 8, TSZ], BF16))
        r_accw = {}
        r_x3use = None
        for ot in range(4):
            r_xb = P.step("gpsimd", lambda e, ot=ot: e.dma_start(
                out=x3b[:, :, :], in_=x33[:, :, ot * TSZ:(ot + 1) * TSZ]),
                deps=[r_x3use], dma=True)
            r_up_last = None
            for fc in range(16):
                pb = fc % 2
                r_mm = None
                for dc in range(8):
                    r_mm = P.step("tensor",
                        lambda e, fc=fc, dc=dc, pb=pb: e.matmul(
                            ps_g[pb][:, :],
                            wgA[:, dc, fc * 128:(fc + 1) * 128],
                            x3b[:, dc, :], start=(dc == 0), stop=(dc == 7)),
                        deps=[r_wgA, r_xb, r_silu[pb]]
                        if dc == 0 else [r_wgA])
                r_silu[pb] = P.step("scalar", lambda e, fc=fc, pb=pb:
                    e.activation(gsb[:, pb, :], ps_g[pb][:, :], AF.Silu),
                    deps=[r_mm, r_umul[pb]])
                r_mm = None
                for dc in range(8):
                    r_mm = P.step("tensor",
                        lambda e, fc=fc, dc=dc, pb=pb: e.matmul(
                            ps_u[pb][:, :],
                            wgB[:, dc, fc * 128:(fc + 1) * 128],
                            x3b[:, dc, :], start=(dc == 0), stop=(dc == 7)),
                        deps=[r_wgB, r_xb, r_umul[pb]]
                        if dc == 0 else [r_wgB])
                r_up_last = r_mm
                r_umul[pb] = P.step("vector", lambda e, fc=fc, pb=pb:
                    e.tensor_mul(usb[:, fc, :], gsb[:, pb, :],
                                 ps_u[pb][:, :]),
                    deps=[r_mm, r_silu[pb]])
            for oc in range(8):
                pb = oc % 2
                r_mm = None
                for fc in range(16):
                    r_mm = P.step("tensor",
                        lambda e, oc=oc, fc=fc, pb=pb: e.matmul(
                            ps_d[pb][:, :],
                            wd_sb[:, fc, oc * 128:(oc + 1) * 128],
                            usb[:, fc, :], start=(fc == 0),
                            stop=(fc == 15)),
                        deps=[r_wd, r_umul[0], r_umul[1], r_duse[pb]]
                        if fc == 0 else [r_wd])
                r_duse[pb] = P.step("vector", lambda e, oc=oc, ot=ot, pb=pb:
                    e.tensor_copy(
                        sumacc[:, oc, ot * TSZ:(ot + 1) * TSZ],
                        ps_d[pb][:, :]), deps=[r_mm])
            r_accw[ot] = [r_duse[0], r_duse[1]]
            r_x3use = r_up_last

        r_mrg = {}
        for sub in range(16):
            ot = sub // 4
            r_ld = P.step("sync", lambda e, sub=sub: e.dma_start(
                out=mxin[sub % 4][:, :],
                in_=otm[sub * 128:(sub + 1) * 128, :]),
                deps=[r_sparse_done, r_accw.get(ot), r_mrg.get(sub - 4),
                      r_tfm_prev],
                dma=True)
            r_trs = []
            for dc in range(8):
                r_trs.append(P.step("tensor",
                    lambda e, sub=sub, dc=dc: e.transpose(
                        ps_tb2[:, dc * 128:(dc + 1) * 128],
                        mxin[sub % 4][:, dc * 128:(dc + 1) * 128],
                        identb[:, :]),
                    deps=[r_ld, r_mrg.get(sub - 1)] +
                         (r_tfm_prev if sub == 0 else [])))
            r_mrg[sub] = P.step("vector", lambda e, sub=sub: e.tensor_add(
                sumacc[:, :, sub * 128:(sub + 1) * 128],
                sumacc[:, :, sub * 128:(sub + 1) * 128],
                ps_tb2[:, :].rearrange("p (c f) -> p c f", c=8)),
                deps=r_trs + [r_accw.get(sub // 4)])

        if DBG_IDX and _own:
            idxout = dram(_ctx, "idxout", [16 * NSLOT, 1], I32,
                          "ExternalOutput")
            otmout = dram(_ctx, "otmout", [TCORE, D], BF16, "ExternalOutput")
            P.step("sync", lambda e: e.dma_start(
                out=idxout[:, :], in_=idxd[:, :]),
                deps=[r_idx_done], dma=True)
            P.step("sync", lambda e: e.dma_start(
                out=otmout[:, :], in_=otm[:, :]),
                deps=[r_sparse_done] + [r_mrg[s] for s in range(16)],
                dma=True)
            x3tmout = dram(_ctx, "x3tmout", [TCORE, D], BF16,
                           "ExternalOutput")
            P.step("sync", lambda e: e.dma_start(
                out=x3tmout[:, :], in_=x3tm[:, :]),
                deps=[r_idx_done], dma=True)
        st_e.close()
        e_ = st.enter_context

        with ExitStack() as st_f:
            ef = st_f.enter_context
            tt = ef(nc.sbuf_tensor(uname("ttD"), [128, 8, TSZ], F32))
            lc = make_ln_ctx(nc, ef, "dl")
            ln_ctx_init(nc, P, lc)
            r_tu = None
            for ot in range(4):
                mdeps = [r_mrg[4 * ot + k] for k in range(4)]
                r_x3 = P.step("sync", lambda e, ot=ot: e.dma_start(
                    out=tt[:, :, :], in_=x33[:, :, ot * TSZ:(ot + 1) * TSZ]),
                    deps=mdeps + [r_tu] + (now_all(P) if ot == 0 else []),
                    dma=True)
                r_ad = []
                for oc in range(8):
                    r_ad.append(P.step("vector",
                        lambda e, oc=oc, ot=ot: e.tensor_add(
                            tt[:, oc, :], tt[:, oc, :],
                            sumacc[:, oc, ot * TSZ:(ot + 1) * TSZ]),
                        deps=[r_x3] + mdeps))
                r_l2 = emit_ln_tile(nc, P, lc, tt, tt, r_ad)
                r_tu = P.step("sync", lambda e, ot=ot: e.dma_start(
                    out=out3[:, :, ot * TSZ:(ot + 1) * TSZ], in_=tt[:, :, :]),
                    deps=r_l2, dma=True)

        if _own:
            sems = {e: e_(nc.semaphore(uname("sem_" + e)))
                    for e in ENGINES}
            block = e_(nc.Block())
            P.finalize(block, sems)
    return nc


def build_stageD_dense(fused=None):
    nc, P, _cache, _own = _begin(fused)
    _ctx = (nc, _cache, fused is not None)
    x3my = dram(_ctx, "x3my", [D, TCORE], F32, "ExternalInput")
    wrT = dram(_ctx, "wrT", [D, 16], F32, "ExternalInput")
    gup = dram(_ctx, "gup", [17, D, 4096], F32, "ExternalInput")
    dwn = dram(_ctx, "dwn", [17, 2048, D], F32, "ExternalInput")
    ident = dram(_ctx, "identD", [128, 128], F32, "ExternalInput")
    x4my = dram(_ctx, "x4my", [D, TCORE], F32, "ExternalOutput")

    x33 = fm3(x3my)
    out3 = fm3(x4my)

    with ExitStack() as st:
        e_ = st.enter_context
        wr_sb = e_(nc.sbuf_tensor(uname("wr_sb"), [128, 8, 16], BF16))
        ident_sb = e_(nc.sbuf_tensor(uname("identD_sb"), [128, 128], F32))
        onesf1 = e_(nc.sbuf_tensor(uname("onesfD"), [1, 128], F32))
        cm_all = e_(nc.sbuf_tensor(uname("cmD"), [128, 17, 16], F32))
        sumacc = e_(nc.sbuf_tensor(uname("sumacc"), [128, 8, TCORE], BF16))
        _bar = now_all(P) if not _own else []
        r_cst = [
            P.step("gpsimd", lambda e: e.dma_start(
                out=wr_sb[:, :, :],
                in_=wrT.rearrange("(c p) o -> p c o", p=128)),
                deps=_bar, dma=True),
            P.step("sync", lambda e: e.dma_start(
                out=ident_sb[:, :], in_=ident[:, :]), deps=_bar, dma=True),
        ]
        r_of = P.step("vector", lambda e: e.memset(onesf1[:, :], 1.0))
        r_cm16 = P.step("vector", lambda e: e.memset(cm_all[:, 16, :], 1.0))

        with ExitStack() as st_r:
            er = st_r.enter_context
            xbt = er(nc.sbuf_tensor(uname("xbtD"), [128, 8, TSZ], BF16))
            lsb = er(nc.sbuf_tensor(uname("lsbD"), [16, TSZ], F32))
            ltm = er(nc.sbuf_tensor(uname("ltmD"), [128, 16], F32))
            e16 = er(nc.sbuf_tensor(uname("e16D"), [128, 16], F32))
            vc1 = er(nc.sbuf_tensor(uname("vc1D"), [128, 1], F32))
            ssc = er(nc.sbuf_tensor(uname("sscD"), [128, 1], F32))
            ps_l = er(nc.psum_tensor(uname("ps_lD"), [16, TSZ], F32))
            ps_t = er(nc.psum_tensor(uname("ps_tD"), [128, TSZ], F32))
            r_prev = None
            for ot in range(4):
                r_xl = P.step("gpsimd", lambda e, ot=ot: e.dma_start(
                    out=xbt[:, :, :], in_=x33[:, :, ot * TSZ:(ot + 1) * TSZ]),
                    deps=[r_prev], dma=True)
                r_mm = None
                for dc in range(8):
                    r_mm = P.step("tensor", lambda e, dc=dc: e.matmul(
                        ps_l[:, :], wr_sb[:, dc, :], xbt[:, dc, :],
                        start=(dc == 0), stop=(dc == 7)),
                        deps=[r_xl, r_cst[0], r_prev])
                r_lb = P.step("vector", lambda e: e.tensor_copy(
                    lsb[:, :], ps_l[0:16, :]), deps=[r_mm])
                for sub in range(4):
                    r_t = P.step("tensor", lambda e, sub=sub: e.transpose(
                        ps_t[:, 0:16], lsb[0:16, sub * 128:(sub + 1) * 128],
                        ident_sb[0:16, 0:16]), deps=[r_lb, r_cst[1], r_prev])
                    r_c = P.step("vector", lambda e: e.tensor_copy(
                        ltm[:, :], ps_t[:, 0:16]), deps=[r_t])
                    r_v1 = P.step("vector", lambda e: e.reduce_max(
                        vc1[:, :], ltm[:, :], axis=AX.X), deps=[r_c])
                    r_e1 = P.step("vector", lambda e: e.tensor_scalar(
                        e16[:, :], ltm[:, :], vc1[:, 0:1], None,
                        ALU.is_equal), deps=[r_v1])
                    r_sx = P.step("scalar", lambda e: e.activation(
                        ssc[:, :], vc1[:, :], AF.Exp, scale=-1.0),
                        deps=[r_v1])
                    r_s1 = P.step("vector", lambda e: e.tensor_scalar(
                        ssc[:, :], ssc[:, :], 1.0, None, ALU.add),
                        deps=[r_sx])
                    r_s2 = P.step("vector", lambda e: e.reciprocal(
                        ssc[:, :], ssc[:, :]), deps=[r_s1])
                    r_last = None
                    for ex in range(16):
                        r_last = P.step("vector",
                            lambda e, ex=ex, ot=ot, sub=sub:
                            e.scalar_tensor_tensor(
                                cm_all[:, ex, ot * 4 + sub:ot * 4 + sub + 1],
                                e16[:, ex:ex + 1], 1.0, ssc[:, :],
                                ALU.mult, ALU.mult),
                            deps=[r_e1, r_s2, r_last])
                    r_prev = r_last
        r_router = r_prev

        st_e = ExitStack()
        ee = st_e.enter_context
        wg_sb = ee(nc.sbuf_tensor(uname("wg_sbD"), [128, 8, 4096], BF16))
        wd_sb = ee(nc.sbuf_tensor(uname("wd_sbD"), [128, 16, D], BF16))
        mxb = ee(nc.sbuf_tensor(uname("mxbD"), [128, 8, TSZ], BF16))
        x3b = ee(nc.sbuf_tensor(uname("x3bD"), [128, 8, TSZ], BF16))
        gsb = ee(nc.sbuf_tensor(uname("gsbD"), [128, 16, TSZ], BF16))
        usb = ee(nc.sbuf_tensor(uname("usbD"), [128, 16, TSZ], BF16))
        mke = ee(nc.sbuf_tensor(uname("mkeD"), [128, TSZ], BF16))
        mre = ee(nc.sbuf_tensor(uname("mreD"), [1, TSZ], F32))
        ps_g = [ee(nc.psum_tensor(uname("ps_gD"), [128, TSZ], F32))
                for _ in range(2)]
        ps_d = [ee(nc.psum_tensor(uname("ps_dD"), [128, TSZ], F32))
                for _ in range(2)]
        ps_m = ee(nc.psum_tensor(uname("ps_mD"), [128, TSZ], F32))

        r_wuse = [r_router]
        r_xuse = None
        r_mxuse = None
        r_guse = None
        r_uuse = None
        r_accw = {ot: None for ot in range(4)}
        for ex in range(17):
            r_w1 = P.step("gpsimd", lambda e, ex=ex: e.dma_start(
                out=wg_sb[:, :, :],
                in_=gup[ex, :, :].rearrange("(c p) f -> p c f", p=128)),
                deps=r_wuse, dma=True)
            r_w2 = P.step("gpsimd", lambda e, ex=ex: e.dma_start(
                out=wd_sb[:, :, :],
                in_=dwn[ex, :, :].rearrange("(c p) o -> p c o", p=128)),
                deps=r_wuse, dma=True)
            r_wuse = []
            for ot in range(4):
                r_xb = P.step("gpsimd", lambda e, ot=ot: e.dma_start(
                    out=x3b[:, :, :], in_=x33[:, :, ot * TSZ:(ot + 1) * TSZ]),
                    deps=[r_xuse], dma=True)
                r_trs = []
                for sub in range(4):
                    r_tt = P.step("tensor", lambda e, ex=ex, ot=ot, sub=sub:
                        e.transpose(
                            ps_m[0:1, sub * 128:sub * 128 + 128],
                            cm_all[:, ex, ot * 4 + sub:ot * 4 + sub + 1],
                            ident_sb[:, :]),
                        deps=[r_router, r_cm16, r_cst[1], r_mxuse])
                    r_trs.append(r_tt)
                r_mr = P.step("vector", lambda e: e.tensor_copy(
                    mre[0:1, :], ps_m[0:1, :]), deps=r_trs)
                r_mk1 = P.step("tensor", lambda e: e.matmul(
                    ps_m[:, :], onesf1[:, :], mre[0:1, :],
                    start=True, stop=True), deps=[r_mr, r_of])
                r_mk2 = P.step("vector", lambda e: e.tensor_copy(
                    mke[:, :], ps_m[:, :]), deps=[r_mk1])
                r_mx = []
                for dc in range(8):
                    r_mx.append(P.step("vector",
                        lambda e, dc=dc: e.tensor_mul(
                            mxb[:, dc, :], x3b[:, dc, :], mke[:, :]),
                        deps=[r_mk2, r_xb, r_mxuse]))
                r_xuse = r_mx[-1]
                r_pf = {}
                r_ev = [None, None]
                for fc in range(16):
                    pb = fc % 2
                    r_mm = None
                    for dc in range(8):
                        r_mm = P.step("tensor",
                            lambda e, fc=fc, dc=dc, pb=pb: e.matmul(
                                ps_g[pb][:, :],
                                wg_sb[:, dc, fc * 128:(fc + 1) * 128],
                                mxb[:, dc, :], start=(dc == 0),
                                stop=(dc == 7)),
                            deps=[r_w1, r_mx[dc], r_ev[pb]]
                            if dc == 0 else [r_w1, r_mx[dc]])
                    r_ev[pb] = P.step("scalar", lambda e, fc=fc, pb=pb:
                        e.activation(
                            gsb[:, fc, :], ps_g[pb][:, :], AF.Silu),
                        deps=[r_mm, r_guse])
                for fc in range(16):
                    pb = fc % 2
                    r_mm = None
                    for dc in range(8):
                        r_mm = P.step("tensor",
                            lambda e, fc=fc, dc=dc, pb=pb: e.matmul(
                                ps_g[pb][:, :],
                                wg_sb[:, dc, 2048 + fc * 128:
                                      2048 + (fc + 1) * 128],
                                mxb[:, dc, :], start=(dc == 0),
                                stop=(dc == 7)),
                            deps=[r_w1, r_mx[dc], r_ev[pb]]
                            if dc == 0 else [r_w1, r_mx[dc]])
                    r_ev[pb] = P.step("vector", lambda e, fc=fc, pb=pb:
                        e.tensor_copy(
                            usb[:, fc, :], ps_g[pb][:, :]),
                        deps=[r_mm, r_uuse])
                r_ev = r_ev[0] if r_ev[1] is None else r_ev[1]
                r_pfs = []
                for fc in range(16):
                    r_pfs.append(P.step("vector",
                        lambda e, fc=fc: e.tensor_mul(
                            usb[:, fc, :], usb[:, fc, :], gsb[:, fc, :]),
                        deps=[r_ev]))
                r_guse = r_pfs[-1]
                r_uuse = r_pfs[-1]
                r_do = [None, None]
                for oc in range(8):
                    pb = oc % 2
                    r_mm = None
                    for fc in range(16):
                        r_mm = P.step("tensor",
                            lambda e, oc=oc, fc=fc, pb=pb: e.matmul(
                                ps_d[pb][:, :],
                                wd_sb[:, fc, oc * 128:(oc + 1) * 128],
                                usb[:, fc, :], start=(fc == 0),
                                stop=(fc == 15)),
                            deps=[r_w2, r_pfs[fc], r_do[pb]]
                            if fc == 0 else [r_w2, r_pfs[fc]])
                    if ex == 0:
                        r_do[pb] = P.step("vector",
                            lambda e, oc=oc, ot=ot, pb=pb: e.tensor_copy(
                                sumacc[:, oc, ot * TSZ:(ot + 1) * TSZ],
                                ps_d[pb][:, :]), deps=[r_mm, r_accw[ot]])
                    else:
                        r_do[pb] = P.step("vector",
                            lambda e, oc=oc, ot=ot, pb=pb: e.tensor_add(
                                sumacc[:, oc, ot * TSZ:(ot + 1) * TSZ],
                                sumacc[:, oc, ot * TSZ:(ot + 1) * TSZ],
                                ps_d[pb][:, :]), deps=[r_mm, r_accw[ot]])
                r_accw[ot] = list(r_do)
                r_mxuse = list(r_do)
            r_wuse = [r_accw[ot] for ot in range(4)]
        st_e.close()
        e_ = st.enter_context

        with ExitStack() as st_f:
            ef = st_f.enter_context
            tt = ef(nc.sbuf_tensor(uname("ttD"), [128, 8, TSZ], F32))
            lc = make_ln_ctx(nc, ef, "dl")
            ln_ctx_init(nc, P, lc)
            r_tu = None
            for ot in range(4):
                r_x3 = P.step("sync", lambda e, ot=ot: e.dma_start(
                    out=tt[:, :, :], in_=x33[:, :, ot * TSZ:(ot + 1) * TSZ]),
                    deps=[r_accw[ot], r_tu] +
                         (now_all(P) if ot == 0 else []), dma=True)
                r_ad = []
                for oc in range(8):
                    r_ad.append(P.step("vector",
                        lambda e, oc=oc, ot=ot: e.tensor_add(
                            tt[:, oc, :], tt[:, oc, :],
                            sumacc[:, oc, ot * TSZ:(ot + 1) * TSZ]),
                        deps=[r_x3, r_accw[ot]]))
                r_l2 = emit_ln_tile(nc, P, lc, tt, tt, r_ad)
                r_tu = P.step("sync", lambda e, ot=ot: e.dma_start(
                    out=out3[:, :, ot * TSZ:(ot + 1) * TSZ], in_=tt[:, :, :]),
                    deps=r_l2, dma=True)

        if _own:
            sems = {e: e_(nc.semaphore(uname("sem_" + e)))
                    for e in ENGINES}
            block = e_(nc.Block())
            P.finalize(block, sems)
    return nc



def run_stageD(x3_fm, w_router_, gate_up_, down_, w_gate_, w_up_, w_down_):
    if "D" not in _nc_cache:
        _nc_cache["D"] = build_stageD()
    nc = _nc_cache["D"]
    wrT = np.ascontiguousarray(np.asarray(w_router_, np.float32).T)
    shared_gu = np.concatenate(
        [np.asarray(w_gate_, np.float32).T, np.asarray(w_up_, np.float32).T],
        axis=1)[None]
    gup17 = np.concatenate([np.asarray(gate_up_, np.float32), shared_gu],
                           axis=0)
    dwn17 = np.concatenate([np.asarray(down_, np.float32),
                            np.asarray(w_down_, np.float32).T[None]], axis=0)
    ident = np.eye(128, dtype=np.float32)
    utri = np.triu(np.ones((128, 128), np.float32))
    iotac = np.arange(128, dtype=np.float32).reshape(128, 1)
    iota16r = np.broadcast_to(np.arange(16, dtype=np.float32),
                              (128, 16)).copy()
    in_maps = []
    for c in range(NCORES):
        b = c // 4
        cols = my_cols(c)
        in_maps.append({
            "x3my": np.ascontiguousarray(x3_fm[b][:, cols]),
            "wrT": wrT, "gup": gup17, "dwn": dwn17, "identD": ident,
            "utri": utri, "iotac": iotac, "iota16r": iota16r,
        })
    res = run_bass_kernel_spmd(nc, in_maps, core_ids=list(range(NCORES)))
    run_stageD.last_results = res.results
    return scatter_from_cores([np.asarray(r["x4my"]) for r in res.results])


def build_stageA(fused=None):
    nc, P, _cache, _own = _begin(fused)
    _ctx = (nc, _cache, fused is not None)
    x0b = dram(_ctx, "x0b", [D, S], F32, "ExternalInput")
    x0my = dram(_ctx, "x0my", [D, TCORE], F32, "ExternalInput")
    wq = dram(_ctx, "wq", [D, D], F32, "ExternalInput")
    wk = dram(_ctx, "wk", [D, D], F32, "ExternalInput")
    wv = dram(_ctx, "wv", [D, D], F32, "ExternalInput")
    wo = dram(_ctx, "wo", [D, D], F32, "ExternalInput")
    maskA = dram(_ctx, "maskA", [4, 16, 128, TSZ], BF16, "ExternalInput")
    x1my = dram(_ctx, "x1my", [D, TCORE], F32, "ExternalOutput")

    with ExitStack() as st:
        e_ = st.enter_context
        _bar = now_all(P) if not _own else []
        wq_sb = e_(nc.sbuf_tensor(uname("wq_sb"), [128, 8, D], BF16))
        wk_sb = e_(nc.sbuf_tensor(uname("wk_sb"), [128, 8, D], BF16))
        wv_sb = e_(nc.sbuf_tensor(uname("wv_sb"), [128, 8, D], BF16))
        attn_fm = e_(nc.sbuf_tensor(uname("attn_fm"), [128, 8, TCORE], BF16))
        r_w = [P.step("gpsimd", lambda e, w=w, t=t: e.dma_start(
            out=t[:, :, :], in_=w.rearrange("(c p) o -> p c o", p=128)),
            deps=_bar, dma=True)
            for w, t in ((wq, wq_sb), (wk, wk_sb), (wv, wv_sb))]

        with ExitStack() as st_att:
            r_att = emit_attention(nc, P, st_att, causal=True, x_src=x0b,
                           xmy_src=x0my, wq_sb=wq_sb, wk_sb=wk_sb,
                           wv_sb=wv_sb, w_deps=r_w, mask_src=maskA,
                           attn_fm=attn_fm)

        wo_sb = e_(nc.sbuf_tensor(uname("wo_sb"), [128, 8, D], BF16))
        r_wo = P.step("gpsimd", lambda e: e.dma_start(
            out=wo_sb[:, :, :], in_=wo.rearrange("(c p) o -> p c o", p=128)),
            deps=now_all(P), dma=True)
        with ExitStack() as st_post:
            emit_proj_residual_ln(nc, P, st_post, src_fm=attn_fm, w_sb=wo_sb,
                                  r_w=r_wo, resid_src=x0my, resid_scale=2.0,
                                  out_hbm=x1my, r_src=r_att, name="Ap")

        if _own:
            sems = {e: e_(nc.semaphore(uname("sem_" + e)))
                    for e in ENGINES}
            block = e_(nc.Block())
            P.finalize(block, sems)
    return nc


def build_stageB(fused=None, kv=None):
    nc, P, _cache, _own = _begin(fused)
    _ctx = (nc, _cache, fused is not None)
    x1b = dram(_ctx, "x1b", [D, S], F32, "ExternalInput") if kv is None \
        else None
    x1my = dram(_ctx, "x1my", [D, TCORE], F32, "ExternalInput")
    qw = dram(_ctx, "qw", [D, D], F32, "ExternalInput")
    kw = dram(_ctx, "kw", [D, D], F32, "ExternalInput")
    vw = dram(_ctx, "vw", [D, D], F32, "ExternalInput")
    gw = dram(_ctx, "gw", [D, D], F32, "ExternalInput")
    ow = dram(_ctx, "ow", [D, D], F32, "ExternalInput")
    gb = dram(_ctx, "gb", [128, 8], F32, "ExternalInput")
    swT = dram(_ctx, "swT", [128, 16], F32, "ExternalInput")
    n1g = dram(_ctx, "n1g", [128, 8], F32, "ExternalInput")
    n1b = dram(_ctx, "n1b", [128, 8], F32, "ExternalInput")
    ymy = dram(_ctx, "ymy", [D, TCORE], F32, "ExternalOutput")

    with ExitStack() as st:
        e_ = st.enter_context
        _bar = now_all(P) if not _own else []
        wq_sb = e_(nc.sbuf_tensor(uname("wq_sb"), [128, 8, D], BF16))
        swT_sb = e_(nc.sbuf_tensor(uname("swT_sb"), [128, 16], BF16))
        attn_fm = e_(nc.sbuf_tensor(uname("attn_fm"), [128, 8, TCORE], BF16))
        pairs = [(qw, wq_sb)]
        if kv is None:
            wk_sb = e_(nc.sbuf_tensor(uname("wk_sb"), [128, 8, D], BF16))
            wv_sb = e_(nc.sbuf_tensor(uname("wv_sb"), [128, 8, D], BF16))
            pairs += [(kw, wk_sb), (vw, wv_sb)]
        else:
            wk_sb = wv_sb = None
        r_w = [P.step("gpsimd", lambda e, w=w, t=t: e.dma_start(
            out=t[:, :, :], in_=w.rearrange("(c p) o -> p c o", p=128)),
            deps=_bar, dma=True) for w, t in pairs]
        r_scw = P.step("gpsimd", lambda e: e.dma_start(
            out=swT_sb[:, :], in_=swT[:, :]), deps=_bar, dma=True)

        with ExitStack() as st_att:
            r_att = emit_attention(nc, P, st_att, causal=False, x_src=x1b,
                                   xmy_src=x1my, wq_sb=wq_sb, wk_sb=wk_sb,
                                   wv_sb=wv_sb, w_deps=r_w + _bar,
                                   mask_src=None,
                                   attn_fm=attn_fm, scale_wT_sb=swT_sb,
                                   r_scw=r_scw, kv=kv)

        gw_sb = e_(nc.sbuf_tensor(uname("gw_sb"), [128, 8, D], BF16))
        gb_sb = e_(nc.sbuf_tensor(uname("gb_sb"), [128, 8], F32))
        r_gw = P.step("gpsimd", lambda e: e.dma_start(
            out=gw_sb[:, :, :], in_=gw.rearrange("(c p) o -> p c o", p=128)),
            deps=now_all(P), dma=True)
        r_gbb = P.step("gpsimd", lambda e: e.dma_start(
            out=gb_sb[:, :], in_=gb[:, :]),
            dma=True)
        with ExitStack() as st_gate:
            eg = st_gate.enter_context
            gfull = eg(nc.sbuf_tensor(uname("gfull"), [128, 8, TSZ], F32))
            ps_g = [eg(nc.psum_tensor(uname("ps_g"), [128, TSZ], F32))
                    for _ in range(2)]
            r_guse = [None, None]
            r_gate_done = None
            for ot in range(4):
                r_sgs = []
                for oc in range(8):
                    pb = oc % 2
                    r_mm = None
                    for ic in range(8):
                        r_mm = P.step("tensor",
                            lambda e, oc=oc, ic=ic, ot=ot, pb=pb: e.matmul(
                                ps_g[pb][:, :],
                                gw_sb[:, ic, oc * 128:(oc + 1) * 128],
                                attn_fm[:, ic, ot * TSZ:(ot + 1) * TSZ],
                                start=(ic == 0), stop=(ic == 7)),
                            deps=[r_gw, r_att, r_guse[pb]]
                            if ic == 0 else [r_gw])
                    r_sg = P.step("scalar", lambda e, oc=oc, pb=pb:
                        e.activation(
                            gfull[:, oc, :], ps_g[pb][:, :], AF.Sigmoid,
                            bias=gb_sb[:, oc:oc + 1]),
                        deps=[r_mm, r_gbb, r_gate_done])
                    r_guse[pb] = r_sg
                    r_sgs.append(r_sg)
                for oc in range(8):
                    r_gate_done = P.step("vector",
                        lambda e, oc=oc, ot=ot: e.tensor_mul(
                            attn_fm[:, oc, ot * TSZ:(ot + 1) * TSZ],
                            attn_fm[:, oc, ot * TSZ:(ot + 1) * TSZ],
                            gfull[:, oc, :]),
                        deps=r_sgs)
        ow_sb = e_(nc.sbuf_tensor(uname("ow_sb"), [128, 8, D], BF16))
        n1g_sb = e_(nc.sbuf_tensor(uname("n1g_sb"), [128, 8], F32))
        n1b_sb = e_(nc.sbuf_tensor(uname("n1b_sb"), [128, 8], F32))
        r_ow = P.step("gpsimd", lambda e: e.dma_start(
            out=ow_sb[:, :, :], in_=ow.rearrange("(c p) o -> p c o", p=128)),
            deps=now_all(P), dma=True)
        r_n1 = P.step("gpsimd", lambda e: e.dma_start(
            out=n1g_sb[:, :], in_=n1g[:, :]),
            dma=True)
        r_n2 = P.step("gpsimd", lambda e: e.dma_start(
            out=n1b_sb[:, :], in_=n1b[:, :]),
            dma=True)
        with ExitStack() as st_post:
            emit_proj_residual_ln(nc, P, st_post, src_fm=attn_fm, w_sb=ow_sb,
                                  r_w=r_ow, resid_src=x1my, resid_scale=1.0,
                                  out_hbm=ymy, gamma_beta=(n1g_sb, n1b_sb),
                                  r_gb=[r_n1, r_n2], r_src=r_gate_done,
                                  name="Bp")

        if _own:
            sems = {e: e_(nc.semaphore(uname("sem_" + e)))
                    for e in ENGINES}
            block = e_(nc.Block())
            P.finalize(block, sems)
    return nc


def run_stageB(x1_fm, qw, kw, vw, ow, gw, gb_, sw, n1g_, n1b_):
    if "B" not in _nc_cache:
        _nc_cache["B"] = build_stageB()
    nc = _nc_cache["B"]
    swT = np.ascontiguousarray(np.asarray(sw).T)
    swT = np.concatenate([swT, swT], axis=0)
    in_maps = []
    for c in range(NCORES):
        b = c // 4
        in_maps.append({
            "x1b": np.ascontiguousarray(x1_fm[b]),
            "x1my": np.ascontiguousarray(x1_fm[b][:, my_cols(c)]),
            "qw": np.asarray(qw), "kw": np.asarray(kw),
            "vw": np.asarray(vw), "ow": np.asarray(ow), "gw": np.asarray(gw),
            "gb": vec_pc(gb_), "swT": swT,
            "n1g": vec_pc(n1g_), "n1b": vec_pc(n1b_),
        })
    res = run_bass_kernel_spmd(nc, in_maps, core_ids=list(range(NCORES)))
    run_stageB.last_results = res.results
    return scatter_from_cores([np.asarray(r["ymy"]) for r in res.results])


def make_masksA():
    out = []
    kk = np.arange(128)[:, None]
    qq = np.arange(TSZ)[None, :]
    for r in range(4):
        m = np.zeros((4, 16, 128, TSZ), dtype=np.float32)
        for s_ in range(4):
            p = TILES[r][s_]
            for mc in range(16):
                chunk = 16 * s_ + mc
                m[s_, mc] = ((128 * chunk + kk) <= (TSZ * p + qq))
        out.append(m.astype(bf))
    return out


def vec_pc(v):
    return np.ascontiguousarray(np.asarray(v, np.float32).reshape(8, 128).T)


def my_cols(core):
    idx = []
    for p in core_tiles(core):
        idx.extend(range(p * TSZ, (p + 1) * TSZ))
    return np.array(idx)


def scatter_from_cores(parts):
    full = np.zeros((B, D, S), dtype=parts[0].dtype)
    for c in range(NCORES):
        b = c // 4
        cols = my_cols(c)
        full[b][:, cols] = parts[c]
    return full


_nc_cache = {}


def run_stageA(x0, wq, wk, wv, wo):
    x0_fm = np.ascontiguousarray(np.transpose(np.asarray(x0), (0, 2, 1)))
    masks = make_masksA()
    if "A" not in _nc_cache:
        _nc_cache["A"] = build_stageA()
    nc = _nc_cache["A"]
    in_maps = []
    for c in range(NCORES):
        b = c // 4
        in_maps.append({
            "x0b": x0_fm[b],
            "x0my": np.ascontiguousarray(x0_fm[b][:, my_cols(c)]),
            "wq": np.asarray(wq), "wk": np.asarray(wk),
            "wv": np.asarray(wv), "wo": np.asarray(wo),
            "maskA": masks[c % 4],
        })
    res = run_bass_kernel_spmd(nc, in_maps, core_ids=list(range(NCORES)))
    run_stageA.last_results = res.results
    return scatter_from_cores([np.asarray(r["x1my"]) for r in res.results])


def emit_kv_shard(nc, P, st, *, xmy3, wk_sb, wv_sb, w_deps, k_out, v_out):
    e_ = st.enter_context
    xt = [e_(nc.sbuf_tensor(uname("kvx"), [128, 8, TSZ], BF16))
          for _ in range(2)]
    kf = e_(nc.sbuf_tensor(uname("kvk"), [128, 8, TSZ], BF16))
    vtm = e_(nc.sbuf_tensor(uname("kvv"), [128, 2, TSZ], BF16))
    ps_k = e_(nc.psum_tensor(uname("kvpk"), [128, TSZ], F32))
    ps_v = [e_(nc.psum_tensor(uname("kvpv"), [128, TSZ], F32))
            for _ in range(2)]
    k3 = k_out.rearrange("(c p) t -> p c t", p=128)
    r_x = [None, None]
    r_kst = None
    r_vst = None
    r_kev = None
    r_vev = [None, None]
    outs = []
    for kt in range(4):
        r_xl = P.step("gpsimd", lambda e, kt=kt, b=kt % 2: e.dma_start(
            out=xt[b][:, :, :], in_=xmy3[:, :, kt * TSZ:(kt + 1) * TSZ]),
            deps=[r_x[kt % 2]] + w_deps, dma=True)
        r_evs = []
        for oc in range(8):
            r_mm = None
            for dc in range(8):
                r_mm = P.step("tensor", lambda e, oc=oc, dc=dc, b=kt % 2:
                    e.matmul(ps_k[:, :],
                             wk_sb[:, dc, oc * 128:(oc + 1) * 128],
                             xt[b][:, dc, :], start=(dc == 0),
                             stop=(dc == 7)),
                    deps=[r_xl, r_kev] if dc == 0 else [r_xl])
            r_kev = P.step("vector", lambda e, oc=oc: e.tensor_copy(
                kf[:, oc, :], ps_k[:, :]), deps=[r_mm, r_kst])
            r_evs.append(r_kev)
        r_kst = P.step("sync", lambda e, kt=kt: e.dma_start(
            out=k3[:, :, kt * TSZ:(kt + 1) * TSZ], in_=kf[:, :, :]),
            deps=r_evs, dma=True)
        outs.append(r_kst)
        for th in range(4):
            r_vm = [None, None]
            for half in range(2):
                for dc in range(8):
                    r_vm[half] = P.step("tensor",
                        lambda e, th=th, half=half, dc=dc, b=kt % 2:
                        e.matmul(ps_v[half][:, :],
                                 xt[b][:, dc, th * 128:(th + 1) * 128],
                                 wv_sb[:, dc, half * TSZ:(half + 1) * TSZ],
                                 start=(dc == 0), stop=(dc == 7)),
                        deps=[r_xl, r_vev[half]] if dc == 0 else [r_xl])
            r_ev2 = []
            for half in range(2):
                r_vev[half] = P.step("vector",
                    lambda e, half=half: e.tensor_copy(
                        vtm[:, half, :], ps_v[half][:, :]),
                    deps=[r_vm[half], r_vst])
                r_ev2.append(r_vev[half])
            r_vst = P.step("sync", lambda e, kt=kt, th=th: e.dma_start(
                out=v_out[kt * TSZ + th * 128:kt * TSZ + (th + 1) * 128, :],
                in_=vtm[:, :, :].rearrange("p h t -> p (h t)")),
                deps=r_ev2, dma=True)
            outs.append(r_vst)
        r_x[kt % 2] = r_evs[-1]
    return outs


def build_fused():
    nc = bass.Bass("TRN2", target_bir_lowering=False)
    P = Prog(nc)
    cache = {}
    fused = (nc, P, cache)
    _ctx = (nc, cache, True)
    build_stageA(fused)

    kw = dram(_ctx, "kw", [D, D], F32, "ExternalInput")
    vw = dram(_ctx, "vw", [D, D], F32, "ExternalInput")
    k_sh = dram(_ctx, "k_sh", [D, TCORE], BF16, "Internal")
    v_sh = dram(_ctx, "v_sh", [TCORE, D], BF16, "Internal")
    k_g = dram(_ctx, "k_g", [4 * D, TCORE], BF16, "Internal")
    v_g = dram(_ctx, "v_g", [S, D], BF16, "Internal")
    groups = [[0, 1, 2, 3], [4, 5, 6, 7]]
    with ExitStack() as st:
        e_ = st.enter_context
        bar = now_all(P)
        wk_sb = e_(nc.sbuf_tensor(uname("fkw"), [128, 8, D], BF16))
        wv_sb = e_(nc.sbuf_tensor(uname("fvw"), [128, 8, D], BF16))
        r_w = [P.step("gpsimd", lambda e, w=w, t=t: e.dma_start(
            out=t[:, :, :], in_=w.rearrange("(c p) o -> p c o", p=128)),
            deps=bar, dma=True) for w, t in ((kw, wk_sb), (vw, wv_sb))]
        outs = emit_kv_shard(nc, P, st, xmy3=fm3(cache["x1my"]),
                             wk_sb=wk_sb, wv_sb=wv_sb, w_deps=r_w,
                             k_out=k_sh, v_out=v_sh)
        sem_k = e_(nc.semaphore(uname("cck")))
        sem_v = e_(nc.semaphore(uname("ccv")))
        r_cck = P.collective(lambda e: e.collective_compute(
            "AllGather", ALU.bypass, replica_groups=groups,
            ins=[k_sh[:, :].opt()], outs=[k_g[:, :].opt()]),
            deps=outs, sem=sem_k)
        r_ccv = P.collective(lambda e: e.collective_compute(
            "AllGather", ALU.bypass, replica_groups=groups,
            ins=[v_sh[:, :].opt()], outs=[v_g[:, :].opt()]),
            deps=outs, sem=sem_v)

    build_stageB(fused, kv={"kg": k_g, "vg": v_g, "dep": [r_cck, r_ccv]})
    build_stageBmoe(fused)
    build_stageC(fused)
    build_stageD(fused)

    with ExitStack() as st2:
        sems = {e: st2.enter_context(nc.semaphore(uname("sem_" + e)))
                for e in ENGINES}
        block = st2.enter_context(nc.Block())
        P.finalize(block, sems)
    return nc


def run_fused(inputs):
    i = inputs
    if "F" not in _nc_cache:
        _nc_cache["F"] = build_fused()
    nc = _nc_cache["F"]
    x0 = np.asarray(i["L0_attn_x"], np.float32)
    x0_fm = np.ascontiguousarray(np.transpose(x0, (0, 2, 1)))
    masks = make_masksA()
    swT = np.ascontiguousarray(np.asarray(i["L0_ffn_scale_w"]).T)
    swT = np.concatenate([swT, swT], axis=0)
    iota8 = np.broadcast_to(np.arange(8, dtype=np.float32), (128, 8)).copy()
    ident = np.eye(128, dtype=np.float32)
    b1x = np.stack([vec_pc16(i["L0_ffn_b1"][e]) for e in range(8)])
    lngx = np.stack([vec_pc16(i["L0_ffn_ln_g"][e]) for e in range(8)])
    lnbx = np.stack([vec_pc16(i["L0_ffn_ln_b"][e]) for e in range(8)])
    b2x = np.stack([vec_pc(i["L0_ffn_b2"][e]) for e in range(8)])
    woT = np.ascontiguousarray(np.asarray(i["L1_attn_w_out"], np.float32).T)
    wrT = np.ascontiguousarray(
        np.asarray(i["L1_ffn_w_router"], np.float32).T)
    shared_gu = np.concatenate(
        [np.asarray(i["L1_ffn_w_gate"], np.float32).T,
         np.asarray(i["L1_ffn_w_up"], np.float32).T], axis=1)[None]
    gup17 = np.concatenate(
        [np.asarray(i["L1_ffn_gate_up_proj"], np.float32), shared_gu], axis=0)
    dwn17 = np.concatenate(
        [np.asarray(i["L1_ffn_down_proj"], np.float32),
         np.asarray(i["L1_ffn_w_down"], np.float32).T[None]], axis=0)
    in_maps = []
    for c in range(NCORES):
        b = c // 4
        cols = my_cols(c)
        sel = np.zeros((NT, 4), np.float32)
        for sl in range(4):
            sel[core_tiles(c)[sl], sl] = 1.0
        in_maps.append({
            "x0b": x0_fm[b],
            "x0my": np.ascontiguousarray(x0_fm[b][:, cols]),
            "wq": np.asarray(i["L0_attn_Wq"]),
            "wk": np.asarray(i["L0_attn_Wk"]),
            "wv": np.asarray(i["L0_attn_Wv"]),
            "wo": np.asarray(i["L0_attn_Wo"]),
            "maskA": masks[c % 4],
            "qw": np.asarray(i["L0_ffn_q_w"]),
            "kw": np.asarray(i["L0_ffn_k_w"]),
            "vw": np.asarray(i["L0_ffn_v_w"]),
            "gw": np.asarray(i["L0_ffn_gate_w"]),
            "ow": np.asarray(i["L0_ffn_out_w"]),
            "gb": vec_pc(i["L0_ffn_gate_b"]),
            "swT": swT,
            "n1g": vec_pc(i["L0_ffn_n1_g"]),
            "n1b": vec_pc(i["L0_ffn_n1_b"]),
            "mgw": np.asarray(i["L0_ffn_moe_gate_w"]),
            "mgb": np.asarray(i["L0_ffn_moe_gate_b"],
                              np.float32).reshape(8, 1),
            "w1": np.asarray(i["L0_ffn_w1"]),
            "b1x": b1x, "lng": lngx, "lnb": lnbx,
            "w2": np.asarray(i["L0_ffn_w2"]),
            "b2x": b2x,
            "rsc": np.asarray(i["L0_ffn_res_scale"],
                              np.float32).reshape(1, 8),
            "n2g": vec_pc(i["L0_ffn_n2_g"]),
            "n2b": vec_pc(i["L0_ffn_n2_b"]),
            "iota8": iota8, "ident": ident,
            "a_pc": vec_pc(i["L1_attn_a"]),
            "b_pc": vec_pc(i["L1_attn_b"]),
            "woT": woT, "bo_pc": vec_pc(i["L1_attn_b_out"]),
            "identC": ident, "selq": np.broadcast_to(
                sel, (128, NT, 4)).copy(),
            "wrT": wrT, "gup": gup17, "dwn": dwn17, "identD": ident,
            "utri": np.triu(np.ones((128, 128), np.float32)),
            "iotac": np.arange(128, dtype=np.float32).reshape(128, 1),
            "iota16r": np.broadcast_to(
                np.arange(16, dtype=np.float32), (128, 16)).copy(),
        })
    res = run_bass_kernel_spmd(nc, in_maps, core_ids=list(range(NCORES)))
    run_fused.last_results = res.results
    x4 = scatter_from_cores([np.asarray(r["x4my"]) for r in res.results])
    return np.ascontiguousarray(np.transpose(x4, (0, 2, 1)))


def kernel_multilaunch(**inputs):
    i = inputs
    x0 = np.asarray(i["L0_attn_x"], np.float32)
    x1_fm = run_stageA(x0, i["L0_attn_Wq"], i["L0_attn_Wk"],
                       i["L0_attn_Wv"], i["L0_attn_Wo"])
    y_fm = run_stageB(x1_fm, i["L0_ffn_q_w"], i["L0_ffn_k_w"],
                      i["L0_ffn_v_w"], i["L0_ffn_out_w"], i["L0_ffn_gate_w"],
                      i["L0_ffn_gate_b"], i["L0_ffn_scale_w"],
                      i["L0_ffn_n1_g"], i["L0_ffn_n1_b"])
    x2_fm = run_stageBmoe(y_fm, x1_fm, i["L0_ffn_moe_gate_w"],
                          i["L0_ffn_moe_gate_b"], i["L0_ffn_w1"],
                          i["L0_ffn_b1"], i["L0_ffn_ln_g"], i["L0_ffn_ln_b"],
                          i["L0_ffn_w2"], i["L0_ffn_b2"],
                          i["L0_ffn_res_scale"], i["L0_ffn_n2_g"],
                          i["L0_ffn_n2_b"])
    x3_fm = run_stageC(x2_fm, i["L1_attn_a"], i["L1_attn_b"],
                       i["L1_attn_w_out"], i["L1_attn_b_out"])
    x4_fm = run_stageD(x3_fm, i["L1_ffn_w_router"], i["L1_ffn_gate_up_proj"],
                       i["L1_ffn_down_proj"], i["L1_ffn_w_gate"],
                       i["L1_ffn_w_up"], i["L1_ffn_w_down"])
    return np.ascontiguousarray(np.transpose(x4_fm, (0, 2, 1)))


def kernel(**inputs):
    return run_fused(inputs)
